# version 05
# baseline (speedup 1.0000x reference)
import numpy as np
import ml_dtypes
import concourse.bass as bass
import concourse.mybir as mybir
from concourse.bass_utils import run_bass_kernel_spmd
from contextlib import ExitStack
from types import SimpleNamespace

F32 = mybir.dt.float32
F16 = mybir.dt.float16
BF16 = mybir.dt.bfloat16
AF = mybir.ActivationFunctionType
ALU = mybir.AluOpType
AX = mybir.AxisListType

NCORES = 8
D = 1024
I2 = 2048
IF = 2816
T = 4096
TT = 512
NTT = 8
NST = 32
MAGIC = 1536.0
R1024 = 1.0 / 1024.0
QI = 1.0 / 127.0


def build_nc():
    nc = bass.Bass("TRN2", target_bir_lowering=False, debug=False)

    x_h = nc.dram_tensor("x", [T, D], F32, kind="ExternalInput").ap()
    identb_h = nc.dram_tensor("identb", [128, 128], BF16, kind="ExternalInput").ap()
    W = {}
    for nm, shape in [
        ("norm_w", [D]), ("in_w", [I2, D]), ("in_b", [I2]),
        ("gate_w", [I2, D]), ("gate_b", [I2]),
        ("b_w", [I2, I2]), ("b_b", [I2]), ("c_w", [I2, I2]), ("c_b", [I2]),
        ("d_w", [I2, I2]), ("d_b", [I2]),
        ("out_w", [D, I2]), ("out_b", [D]), ("a_log", [I2]),
        ("clip_x", [1]), ("clip_down", [1]),
        ("w_gate", [IF, D]), ("w_up", [IF, D]), ("w_down", [D, IF]),
    ]:
        W[nm] = nc.dram_tensor(nm, shape, F32, kind="ExternalInput").ap()
    out_h = nc.dram_tensor("out", [T, D], F32, kind="ExternalOutput").ap()

    xn_h = nc.dram_tensor("xn_i", [T, D], F32, kind="Internal").ap()
    uhi_h = nc.dram_tensor("uhi_i", [I2, T], BF16, kind="Internal").ap()
    ulo_h = nc.dram_tensor("ulo_i", [I2, T], BF16, kind="Internal").ap()
    c_hb = nc.dram_tensor("c_i", [I2, T], F32, kind="Internal").ap()
    d_hb = nc.dram_tensor("d_i", [I2, T], F32, kind="Internal").ap()
    yph_h = nc.dram_tensor("yph_i", [I2, T], BF16, kind="Internal").ap()
    ypl_h = nc.dram_tensor("ypl_i", [I2, T], BF16, kind="Internal").ap()
    hp_h = nc.dram_tensor("hp_i", [IF, T], F32, kind="Internal").ap()
    acc_h = nc.dram_tensor("acc_i", [T, D], F32, kind="Internal").ap()
    scr_h = nc.dram_tensor("scr_i", [4096], F32, kind="Internal").ap()
    cc_i = [nc.dram_tensor(f"cci{k}", [8], F32, kind="Internal").ap() for k in range(2)]
    cc_o = [nc.dram_tensor(f"cco{k}", [8], F32, kind="Internal").ap() for k in range(2)]

    gctx = ExitStack()
    with gctx:
        def mkbuf(ctx, name, shape, dt=F32):
            return ctx.enter_context(nc.sbuf_tensor(name, shape, dt))

        _semno = [0]

        def mksem(tag):
            _semno[0] += 1
            return SimpleNamespace(
                h=gctx.enter_context(nc.semaphore(f"s{_semno[0]}_{tag}")), v=0)

        _shared = {}

        def shsem(tag):
            if tag not in _shared:
                _shared[tag] = mksem(tag)
            return _shared[tag]

        class DStream:

            def __init__(self, tag):
                self.s = [mksem(tag + "0"), mksem(tag + "1")]
                self.chk = []

            def dma(self, eng, out, in_, **kw):
                sm = self.s[len(self.chk) % 2]
                if sm.v >= 16:
                    eng.wait_ge(sm.h, sm.v)
                eng.dma_start(out=out, in_=in_, **kw).then_inc(sm.h, 16)
                sm.v += 16
                self.chk.append((sm, sm.v))
                return len(self.chk) - 1

            def wait(self, eng, idx):
                sm, v = self.chk[idx]
                eng.wait_ge(sm.h, v)

            def wait_upto(self, eng, idx):
                assert idx < len(self.chk), (idx, len(self.chk))
                best = {}
                for sm, v in self.chk[:idx + 1]:
                    best[id(sm)] = (sm, v)
                for sm, v in best.values():
                    eng.wait_ge(sm.h, v)

            def finals(self):
                best = {}
                for sm, v in self.chk:
                    best[id(sm)] = (sm, v)
                return list(best.values())

        identb = mkbuf(gctx, "identb_s", [128, 128], BF16)
        bias_ig = mkbuf(gctx, "bias_ig", [128, 32])
        bias_bcd = mkbuf(gctx, "bias_bcd", [128, 48])
        a_sb = mkbuf(gctx, "a_sb", [128, 16])
        scal = mkbuf(gctx, "scal", [1, 16])
        invsx = mkbuf(gctx, "invsx", [128, 1])
        invsd = mkbuf(gctx, "invsd", [128, 1])
        sxrep = mkbuf(gctx, "sxrep", [128, 1])
        sdrep = mkbuf(gctx, "sdrep", [128, 1])
        swg = mkbuf(gctx, "swg", [128, 22])
        swu = mkbuf(gctx, "swu", [128, 22])
        swd8 = mkbuf(gctx, "swd8", [128, 8])
        mx_xn = mkbuf(gctx, "mx_xn", [128, 1])
        mx_h = mkbuf(gctx, "mx_h", [128, 1])
        state = mkbuf(gctx, "state", [128, 16])
        row128 = mkbuf(gctx, "row128", [1, 128])
        row8 = mkbuf(gctx, "row8", [1, 8])
        sc4 = mkbuf(gctx, "sc4", [1, 4])

        pb = [gctx.enter_context(nc.psum_tensor(f"pb{i}", [128, 512], F32))
              for i in range(8)]
        block = gctx.enter_context(nc.Block())
        fence = mksem("fence")

        def fence_here(*wait_list):
            fence.v += 1
            fv = fence.v

            @block.gpsimd
            def _(g):
                for s, v in wait_list:
                    g.wait_ge(s.h, v)
                g.sem_inc(fence.h, 1)
            return fv

        def wf(eng, fv):
            eng.wait_ge(fence.h, fv)

        ds_const = DStream("const")
        s_p0a = mksem("p0a")
        s_p0v = mksem("p0v")

        @block.sync
        def _(sync):
            for dst, src in [
                (identb[:], identb_h[:]),
                (bias_ig[:, 0:16], W["in_b"].rearrange("(c p) -> p c", p=128)),
                (bias_ig[:, 16:32], W["gate_b"].rearrange("(c p) -> p c", p=128)),
                (bias_bcd[:, 0:16], W["b_b"].rearrange("(c p) -> p c", p=128)),
                (bias_bcd[:, 16:32], W["c_b"].rearrange("(c p) -> p c", p=128)),
                (bias_bcd[:, 32:48], W["d_b"].rearrange("(c p) -> p c", p=128)),
                (a_sb[:], W["a_log"].rearrange("(c p) -> p c", p=128)),
                (scal[0:1, 0:1], W["clip_x"][None, :]),
                (scal[0:1, 1:2], W["clip_down"][None, :]),
            ]:
                ds_const.dma(sync, dst, src, allow_slow_non_contiguous=True)

        @block.scalar
        def _(sc):
            for sm, v in ds_const.finals():
                sc.wait_ge(sm.h, v)
            sc.activation(a_sb[:], a_sb[:], AF.Sigmoid, scale=-1.0).then_inc(s_p0a.h, 1)
        s_p0a.v = 1

        @block.vector
        def _(v):
            v.memset(mx_xn[:], 0.0)
            v.drain()
            v.memset(mx_h[:], 0.0).then_inc(s_p0v.h, 1)
        s_p0v.v = 1
        fv0 = fence_here(*ds_const.finals(), (s_p0a, 1), (s_p0v, 1))

        def prep_weight(ctx, tag, src_ap, n_oc, Kdim, dsts, fv, mode="hilo",
                        layout="oc", fold_rep=None, row_off=0, sw_dst=None,
                        sw_mul_ap=None, extra_dma=(), scratch=None,
                        load_engine="sync", pe_guard=None):
            nk = Kdim // 128
            grp = "B" if tag in ("pga", "pqu") else "A"
            if "prL" + grp not in _shared:
                _shared["prL" + grp] = DStream("prL" + grp)
            dsL = _shared["prL" + grp]
            sS = shsem("prS" + grp)
            sP = shsem("prP" + grp)
            sE = shsem("prE" + grp)
            bS, bP, bE = sS.v, sP.v, sE.v
            pb_base = 6 if grp == "A" else 4
            ld_idx = {}
            nsrc = 2 if mode == "hilo" else 1
            if scratch is None:
                wsl = [mkbuf(ctx, f"{tag}_w{i}", [128, Kdim]) for i in range(2)]
                bsl = [[mkbuf(ctx, f"{tag}_b{j}{i}", [128, Kdim], BF16)
                        for i in range(2)] for j in range(nsrc)]
                q16 = (mkbuf(ctx, f"{tag}_q16", [128, Kdim], F16)
                       if mode == "quant" else None)
                rst = mkbuf(ctx, f"{tag}_rst", [128, 2]) if mode == "quant" else None
            else:
                wsl, bsl, q16, rst = scratch
            kbs = [(k0, min(4, nk - k0)) for k0 in range(0, nk, 4)]
            nbb = len(kbs) * nsrc
            n_batches = n_oc * nbb

            def emit_loads(eng):
                if fv is not None:
                    wf(eng, fv)
                for dst, src in extra_dma:
                    ld_idx.setdefault("x", []).append(dsL.dma(eng, dst, src))
                for s in range(n_oc):
                    if s >= 2:
                        eng.wait_ge(sS.h, bS + s - 1)
                    ld_idx[s] = dsL.dma(
                        eng, wsl[s % 2][:],
                        src_ap[row_off + s * 128: row_off + (s + 1) * 128, :])

            if load_engine == "sync":
                @block.sync
                def _(sync):
                    emit_loads(sync)
            else:
                @block.scalar
                def _(sc):
                    emit_loads(sc)

            @block.vector
            def _(v):
                if fv is not None:
                    wf(v, fv)
                for s in range(n_oc):
                    dsL.wait(v, ld_idx[s])
                    if s >= 2:
                        v.wait_ge(sP.h, bP + (s - 1) * nbb)
                    elif bP > 0:
                        v.wait_ge(sP.h, bP)
                    if mode == "hilo":
                        if fold_rep is not None:
                            for xi in ld_idx.get("x", []):
                                dsL.wait(v, xi)
                            v.tensor_tensor(wsl[s % 2][:], wsl[s % 2][:], fold_rep,
                                            ALU.mult)
                            v.drain()
                        v.tensor_copy(bsl[0][s % 2][:], wsl[s % 2][:])
                        v.drain()
                        v.tensor_sub(bsl[1][s % 2][:], wsl[s % 2][:],
                                     bsl[0][s % 2][:]).then_inc(sS.h, 1)
                        v.drain()
                    else:
                        v.reduce_max(rst[:, 0:1], wsl[s % 2][:], axis=AX.X,
                                     apply_absolute_value=True)
                        v.drain()
                        v.reciprocal(rst[:, 1:2], rst[:, 0:1])
                        v.drain()
                        v.tensor_scalar(rst[:, 1:2], rst[:, 1:2], 127.0, None, ALU.mult)
                        v.drain()
                        v.tensor_scalar(q16[:], wsl[s % 2][:], rst[:, 1:2], MAGIC,
                                        ALU.mult, ALU.add)
                        v.drain()
                        v.tensor_scalar(bsl[0][s % 2][:], q16[:], -MAGIC, None, ALU.add)
                        v.drain()
                        if sw_mul_ap is not None:
                            v.tensor_scalar(sw_dst[:, s:s + 1], rst[:, 0:1],
                                            sw_mul_ap, QI, ALU.mult,
                                            ALU.mult).then_inc(sS.h, 1)
                        else:
                            v.tensor_scalar(sw_dst[:, s:s + 1], rst[:, 0:1],
                                            QI, None, ALU.mult).then_inc(sS.h, 1)
                        v.drain()
                sS.v = bS + n_oc

            @block.tensor
            def _(te):
                if fv is not None:
                    wf(te, fv)
                if pe_guard is not None:
                    te.wait_ge(pe_guard[0].h, pe_guard[1])
                b = 0
                for s in range(n_oc):
                    te.wait_ge(sS.h, bS + s + 1)
                    for j in range(nsrc):
                        for (k0, kn) in kbs:
                            if b >= 2:
                                te.wait_ge(sE.h, bE + b - 1)
                            bank = pb[pb_base + (b % 2)]
                            for c in range(kn):
                                inst = te.matmul(
                                    bank[:, c * 128:(c + 1) * 128],
                                    bsl[j][s % 2][:, (k0 + c) * 128:(k0 + c + 1) * 128],
                                    identb[:], start=(c == 0), stop=(c == kn - 1))
                            inst.then_inc(sP.h, 1)
                            b += 1
                sP.v = bP + n_batches

            @block.scalar
            def _(scx):
                if fv is not None:
                    wf(scx, fv)
                b = 0
                for s in range(n_oc):
                    for j in range(nsrc):
                        for (k0, kn) in kbs:
                            scx.wait_ge(sP.h, bP + b + 1)
                            bank = pb[pb_base + (b % 2)]
                            dst = dsts[j]
                            if layout == "oc":
                                col0 = (s * nk + k0) * 128
                                scx.copy(dst[:, col0:col0 + kn * 128],
                                         bank[:, 0:kn * 128]).then_inc(sE.h, 1)
                            else:
                                dv = dst.rearrange("p (k s2 c) -> p k s2 c",
                                                   k=nk, s2=n_oc)
                                scx.copy(dv[:, k0:k0 + kn, s, :],
                                         bank[:, 0:kn * 128].rearrange(
                                             "p (a c) -> p a c", a=kn)).then_inc(sE.h, 1)
                            scx.drain()
                            b += 1
                sE.v = bE + n_batches
            return SimpleNamespace(sP=sP, sE=sE, n=bE + n_batches)

        a1ctx = gctx.enter_context(ExitStack())
        nwrep = mkbuf(a1ctx, "nwrep", [128, D])
        inTh = mkbuf(a1ctx, "inTh", [128, 16 * 8 * 128], BF16)
        inTl = mkbuf(a1ctx, "inTl", [128, 16 * 8 * 128], BF16)
        gaTh = mkbuf(a1ctx, "gaTh", [128, 16 * 8 * 128], BF16)
        gaTl = mkbuf(a1ctx, "gaTl", [128, 16 * 8 * 128], BF16)

        prctx = a1ctx.enter_context(ExitStack())
        pr_in = prep_weight(prctx, "pin", W["in_w"], 16, D, [inTh, inTl], fv0,
                            fold_rep=nwrep[:],
                            extra_dma=[(nwrep[:],
                                        W["norm_w"][None, :].broadcast_to([128, D]))])
        pr_ga = prep_weight(prctx, "pga", W["gate_w"], 16, D, [gaTh, gaTl], fv0,
                            fold_rep=nwrep[:])
        fvA1 = fence_here((pr_in.sE, pr_in.n), (pr_ga.sE, pr_ga.n))
        prctx.close()

        xb = [mkbuf(a1ctx, f"a1x{i}", [128, D]) for i in range(2)]
        xcb = [mkbuf(a1ctx, f"a1xc{i}", [128, D]) for i in range(2)]
        xnb = [mkbuf(a1ctx, f"a1xn{i}", [128, D]) for i in range(2)]
        x2h = [mkbuf(a1ctx, f"a1h{i}", [128, D], BF16) for i in range(4)]
        x2l = [mkbuf(a1ctx, f"a1l{i}", [128, D], BF16) for i in range(4)]
        xTh = mkbuf(a1ctx, "a1xTh", [128, 8 * 512], BF16)
        xTl = mkbuf(a1ctx, "a1xTl", [128, 8 * 512], BF16)
        st1 = mkbuf(a1ctx, "a1st1", [128, 4])
        st2 = mkbuf(a1ctx, "a1st2", [128, 2])
        sq2 = mkbuf(a1ctx, "a1sq2", [128, 2])
        rsy = mkbuf(a1ctx, "a1rsy", [128, 2])
        rtmp = mkbuf(a1ctx, "a1rt", [128, 2])
        ubf = [mkbuf(a1ctx, f"a1u{i}", [128, TT]) for i in range(2)]
        gbf = [mkbuf(a1ctx, f"a1g{i}", [128, TT]) for i in range(2)]
        uhb = [mkbuf(a1ctx, f"a1uh{i}", [128, TT], BF16) for i in range(2)]
        ulb = [mkbuf(a1ctx, f"a1ul{i}", [128, TT], BF16) for i in range(2)]

        dsX = DStream("a1x")
        sDV = mksem("a1dv")
        sAC = mksem("a1ac")
        sPE = mksem("a1pe")
        sEV = mksem("a1ev")
        sSG = mksem("a1sg")
        sUE = mksem("a1ue")
        dsXS = DStream("a1xs")
        dsUS = DStream("a1us")

        def a1_sub(v, t):
            if t > 0:
                v.wait_ge(sPE.h, 32 * (t - 1) + 16)
            for st in range(4):
                g = 4 * t + st
                dsX.wait(v, g)
                v.reduce_sum(st1[:, 0:1], xb[g % 2][:], axis=AX.X)
                v.drain()
                v.tensor_scalar(st1[:, 1:2], st1[:, 0:1], R1024, None, ALU.mult)
                v.drain()
                v.tensor_scalar(xcb[g % 2][:], xb[g % 2][:], st1[:, 1:2], None,
                                ALU.subtract).then_inc(sDV.h, 1)
                v.drain()
                v.wait_ge(sAC.h, 2 * g + 1)
                v.tensor_scalar(st2[:, 0:1], st1[:, 2:3], R1024, 1e-5, ALU.mult,
                                ALU.add)
                v.drain()
                v.tensor_scalar(st1[:, 3:4], st1[:, 2:3], R1024, None, ALU.mult)
                v.drain()
                v.reciprocal(rtmp[:, 0:1], st2[:, 0:1])
                v.drain()
                v.tensor_tensor(st2[:, 1:2], st1[:, 3:4], rtmp[:, 0:1], ALU.mult)
                v.drain()
                v.tensor_scalar(st2[:, 1:2], st2[:, 1:2], 1e-6, None,
                                ALU.add).then_inc(sDV.h, 1)
                v.drain()
                v.wait_ge(sAC.h, 2 * g + 2)
                v.reciprocal(rsy[:], sq2[:])
                v.drain()
                v.tensor_tensor(rtmp[:], rsy[:], rsy[:], ALU.mult)
                v.drain()
                v.tensor_tensor(rtmp[:], rtmp[:], st2[:], ALU.mult)
                v.drain()
                v.tensor_scalar(rtmp[:], rtmp[:], -0.5, 1.5, ALU.mult, ALU.add)
                v.drain()
                v.tensor_tensor(rsy[:], rsy[:], rtmp[:], ALU.mult)
                v.drain()
                if g >= 2:
                    dsXS.wait(v, g - 2)
                v.tensor_scalar(xnb[g % 2][:], xcb[g % 2][:], rsy[:, 0:1], None,
                                ALU.mult)
                v.drain()
                v.reduce_max(st1[:, 0:1], xnb[g % 2][:], axis=AX.X,
                             apply_absolute_value=True)
                v.drain()
                v.tensor_tensor(mx_xn[:], mx_xn[:], st1[:, 0:1], ALU.max)
                v.drain()
                v.tensor_scalar(x2h[st][:], xnb[g % 2][:], rsy[:, 1:2], None, ALU.mult)
                v.drain()
                v.scalar_tensor_tensor(x2l[st][:], xnb[g % 2][:], rsy[:, 1:2],
                                       x2h[st][:], ALU.mult,
                                       ALU.subtract).then_inc(sDV.h, 1)
                v.drain()

        def a1_acts(sc, t):
            for st in range(4):
                g = 4 * t + st
                sc.wait_ge(sDV.h, 3 * g + 1)
                sc.activation(xb[g % 2][:], xcb[g % 2][:], AF.Square,
                              accum_out=st1[:, 2:3]).then_inc(sAC.h, 1)
                sc.drain()
                sc.wait_ge(sDV.h, 3 * g + 2)
                sc.activation(sq2[:], st2[:], AF.Sqrt).then_inc(sAC.h, 1)
                sc.drain()

        def a1_tr(te, t):
            te.wait_ge(sDV.h, 3 * (4 * t + 3) + 3)
            for dk in range(8):
                for j, src in ((0, x2h), (1, x2l)):
                    b = 16 * t + dk * 2 + j
                    if b >= 2:
                        te.wait_ge(sEV.h, b - 1)
                    bank = pb[6 + (b % 2)]
                    for st in range(4):
                        inst = te.matmul(bank[:, st * 128:(st + 1) * 128],
                                         src[st][:, dk * 128:(dk + 1) * 128],
                                         identb[:], start=(st == 0), stop=(st == 3))
                    inst.then_inc(sPE.h, 1)

        def a1_trev(sc, t):
            for dk in range(8):
                for j, dst in ((0, xTh), (1, xTl)):
                    b = 16 * t + dk * 2 + j
                    sc.wait_ge(sPE.h, 32 * t + dk * 2 + j + 1)
                    bank = pb[6 + (b % 2)]
                    sc.copy(dst[:, dk * 512:(dk + 1) * 512], bank[:]).then_inc(sEV.h, 1)
                    sc.drain()

        def a1_mm(te, t):
            te.wait_ge(sEV.h, 16 * (t + 1))
            for i in range(16):
                gidx = 16 * t + i
                if gidx >= 2:
                    te.wait_ge(sUE.h, gidx - 1)
                psI = pb[2 * (i % 2)]
                psG = pb[2 * (i % 2) + 1]
                for wh, wl, ps in ((inTh, inTl, psI), (gaTh, gaTl, psG)):
                    for k in range(8):
                        col = (i * 8 + k) * 128
                        te.matmul(ps[:], wh[:, col:col + 128],
                                  xTh[:, k * 512:(k + 1) * 512],
                                  start=(k == 0), stop=False)
                        te.matmul(ps[:], wh[:, col:col + 128],
                                  xTl[:, k * 512:(k + 1) * 512],
                                  start=False, stop=False)
                        inst = te.matmul(ps[:], wl[:, col:col + 128],
                                         xTh[:, k * 512:(k + 1) * 512],
                                         start=False, stop=(k == 7))
                inst.then_inc(sPE.h, 1)

        def a1_sig(sc, t):
            for i in range(16):
                gidx = 16 * t + i
                sc.wait_ge(sPE.h, 32 * t + 16 + i + 1)
                if gidx >= 2:
                    sc.wait_ge(sUE.h, gidx - 1)
                psG = pb[2 * (i % 2) + 1]
                sc.activation(gbf[i % 2][:], psG[:], AF.Sigmoid,
                              bias=bias_ig[:, 16 + i:17 + i]).then_inc(sSG.h, 1)
                sc.drain()

        def a1_uep(v, t):
            for i in range(16):
                gidx = 16 * t + i
                v.wait_ge(sSG.h, gidx + 1)
                if gidx >= 2:
                    dsUS.wait_upto(v, 2 * (gidx - 2) + 1)
                psI = pb[2 * (i % 2)]
                v.scalar_tensor_tensor(ubf[i % 2][:], psI[:], bias_ig[:, i:i + 1],
                                       gbf[i % 2][:], ALU.add, ALU.mult)
                v.drain()
                v.tensor_copy(uhb[i % 2][:], ubf[i % 2][:])
                v.drain()
                v.tensor_sub(ulb[i % 2][:], ubf[i % 2][:],
                             uhb[i % 2][:]).then_inc(sUE.h, 1)
                v.drain()

        @block.sync
        def _(sync):
            wf(sync, fvA1)
            for g in range(NST):
                if g >= 2:
                    sync.wait_ge(sAC.h, 2 * (g - 2) + 1)
                dsX.dma(sync, xb[g % 2][:], x_h[g * 128:(g + 1) * 128, :])

        def a1_gp_xn(gp, t):
            for st in range(4):
                g = 4 * t + st
                gp.wait_ge(sDV.h, 3 * g + 3)
                dsXS.dma(gp, xn_h[g * 128:(g + 1) * 128, :], xnb[g % 2][:])

        def a1_gp_u(gp, t):
            for i in range(16):
                gidx = 16 * t + i
                gp.wait_ge(sUE.h, gidx + 1)
                dsUS.dma(gp, uhi_h[i * 128:(i + 1) * 128, t * TT:(t + 1) * TT],
                         uhb[i % 2][:])
                dsUS.dma(gp, ulo_h[i * 128:(i + 1) * 128, t * TT:(t + 1) * TT],
                         ulb[i % 2][:])

        @block.gpsimd
        def _(gp):
            a1_gp_xn(gp, 0)
            for t in range(NTT):
                if t + 1 < NTT:
                    a1_gp_xn(gp, t + 1)
                a1_gp_u(gp, t)

        @block.vector
        def _(v):
            wf(v, fvA1)
            a1_sub(v, 0)

        @block.scalar
        def _(sc):
            wf(sc, fvA1)
            a1_acts(sc, 0)

        for t in range(NTT):
            @block.tensor
            def _(te, t=t):
                if t == 0:
                    wf(te, fvA1)
                a1_tr(te, t)
                a1_mm(te, t)

            @block.scalar
            def _(sc, t=t):
                if t == 0:
                    a1_trev(sc, 0)
                if t + 1 < NTT:
                    a1_acts(sc, t + 1)
                a1_sig(sc, t)
                if t + 1 < NTT:
                    a1_trev(sc, t + 1)

            @block.vector
            def _(v, t=t):
                if t + 1 < NTT:
                    a1_sub(v, t + 1)
                a1_uep(v, t)

        sDV.v = 3 * NST
        sAC.v = 2 * NST
        sPE.v = 32 * NTT
        sEV.v = 16 * NTT
        sSG.v = 16 * NTT
        sUE.v = 16 * NTT

        fvAR1 = fence_here(*dsUS.finals(), *dsXS.finals(), (sUE, sUE.v))
        a1ctx.close()

        s_ard = mksem("ard")
        s_arc = mksem("arc")
        s_cc1 = mksem("cc1")

        @block.gpsimd
        def _(gp):
            gp.wait_ge(fence.h, fvAR1)
            gp.dma_start(out=scr_h[0:128][:, None], in_=mx_xn[:],
                         allow_slow_non_contiguous=True).then_inc(s_ard.h, 16)
            gp.wait_ge(s_ard.h, 16)
            gp.dma_start(out=row128[:], in_=scr_h[0:128][None, :]).then_inc(s_ard.h, 16)

        @block.vector
        def _(v):
            v.wait_ge(s_ard.h, 32)
            v.reduce_max(sc4[0:1, 0:1], row128[:], axis=AX.X)
            v.drain()
            v.tensor_copy(row8[:], sc4[0:1, 0:1].broadcast_to([1, 8])).then_inc(s_arc.h, 1)
            v.drain()

        @block.gpsimd
        def _(gp):
            gp.wait_ge(s_arc.h, 1)
            gp.dma_start(out=cc_i[0][:], in_=row8[0:1, :]).then_inc(s_ard.h, 16)
            gp.wait_ge(s_ard.h, 48)
            gp.collective_compute("AllReduce", ALU.max,
                                  replica_groups=[list(range(NCORES))],
                                  ins=[cc_i[0][:]], outs=[cc_o[0][:]]).then_inc(s_cc1.h, 1)
            gp.wait_ge(s_cc1.h, 1)
            gp.dma_start(out=scal[0:1, 2:3],
                         in_=cc_o[0][0:1][None, :]).then_inc(s_ard.h, 16)

        @block.scalar
        def _(sc):
            sc.wait_ge(s_ard.h, 64)
            sc.activation(scal[0:1, 4:5], scal[0:1, 0:1],
                          AF.Sigmoid).then_inc(s_arc.h, 1)
            sc.drain()

        @block.vector
        def _(v):
            v.wait_ge(s_arc.h, 2)
            v.tensor_tensor(scal[0:1, 5:6], scal[0:1, 4:5], scal[0:1, 2:3], ALU.mult)
            v.drain()
            v.tensor_scalar(scal[0:1, 5:6], scal[0:1, 5:6], QI, None, ALU.mult)
            v.drain()
            v.reciprocal(scal[0:1, 6:7], scal[0:1, 5:6]).then_inc(s_arc.h, 1)
            v.drain()

        @block.gpsimd
        def _(gp):
            gp.wait_ge(s_arc.h, 3)
            gp.dma_start(out=scr_h[200:202][None, :],
                         in_=scal[0:1, 5:7]).then_inc(s_ard.h, 16)
            gp.wait_ge(s_ard.h, 80)
            gp.dma_start(out=sxrep[:], in_=scr_h[200:201][None, :]
                         .broadcast_to([128, 1])).then_inc(s_ard.h, 16)
            gp.dma_start(out=invsx[:], in_=scr_h[201:202][None, :]
                         .broadcast_to([128, 1])).then_inc(s_ard.h, 16)
        s_ard.v = 112
        s_arc.v = 3

        a2ctx = gctx.enter_context(ExitStack())
        wThs = [mkbuf(a2ctx, f"wTh{i}", [128, 16 * 8 * 128], BF16) for i in range(2)]
        wTls = [mkbuf(a2ctx, f"wTl{i}", [128, 16 * 8 * 128], BF16) for i in range(2)]
        uTh = [mkbuf(a2ctx, f"uTh{i}", [128, 16 * TT], BF16) for i in range(2)]
        uTl = [mkbuf(a2ctx, f"uTl{i}", [128, 16 * TT], BF16) for i in range(2)]
        obuf = [mkbuf(a2ctx, "a2ob", [128, TT])]
        cbuf = [mkbuf(a2ctx, "a2c0", [128, TT])]
        dbuf = [mkbuf(a2ctx, "a2d0", [128, TT])]
        hsb = mkbuf(a2ctx, "a2hs", [128, TT])
        ypbh = [mkbuf(a2ctx, "a2yh0", [128, TT], BF16)]
        ypbl = [mkbuf(a2ctx, "a2yl0", [128, TT], BF16)]
        a2_wsl = mkbuf(a2ctx, "a2pw", [128, 512])
        a2_bhi = mkbuf(a2ctx, "a2ph", [128, 512], BF16)
        a2_blo = mkbuf(a2ctx, "a2pl", [128, 512], BF16)

        dsU = DStream("a2u")
        dsCD = DStream("a2cd")
        dsST = DStream("a2st")
        dsPL = DStream("a2pl")
        sPS = mksem("a2ps")
        sPP = mksem("a2pp")
        sPEv = mksem("a2pe")
        sMP = mksem("a2mp")
        sME = mksem("a2me")
        sCDu = mksem("a2cdu")

        A2SPECS = [("c", 0), ("c", 1), ("d", 0), ("d", 1), ("b", 0), ("b", 1)]
        prep_ld_idx = [dict() for _ in range(6)]

        def a2_prep_load1(eng, k, h):
            kind, half = A2SPECS[k]
            wsrc = {"c": W["c_w"], "d": W["d_w"], "b": W["b_w"]}[kind]
            bS = 32 * k
            si, q = h // 4, h % 4
            if h >= 1:
                eng.wait_ge(sPS.h, bS + h)
            prep_ld_idx[k][h] = dsPL.dma(
                eng, a2_wsl[:],
                wsrc[half * 1024 + si * 128: half * 1024 + (si + 1) * 128,
                     q * 512:(q + 1) * 512])

        def a2_prep_evac1(sc, k, h):
            bP = 64 * k
            for j, dst in ((0, wThs[k % 2]), (1, wTls[k % 2])):
                b = 2 * h + j
                si, q = h // 4, h % 4
                sc.wait_ge(sPP.h, bP + b + 1)
                bank = pb[6 + (b % 2)]
                col0 = (si * 16 + q * 4) * 128
                sc.copy(dst[:, col0:col0 + 512], bank[:]).then_inc(sPEv.h, 1)
                sc.drain()

        def a2_prep_actwork(sc, k, first=False):
            if first:
                wf(sc, fvAR1)
            a2_prep_load1(sc, k, 0)
            a2_prep_load1(sc, k, 1)
            for h in range(32):
                a2_prep_evac1(sc, k, h)
                if h + 2 < 32:
                    a2_prep_load1(sc, k, h + 2)

        def a2_prep_splits(v, k, hs, first=False):
            if first:
                wf(v, fvAR1)
            bP = 64 * k
            for h in hs:
                dsPL.wait(v, prep_ld_idx[k][h])
                if h >= 1:
                    v.wait_ge(sPP.h, bP + 2 * h)
                elif bP > 0:
                    v.wait_ge(sPP.h, bP)
                v.tensor_copy(a2_bhi[:], a2_wsl[:])
                v.drain()
                v.tensor_sub(a2_blo[:], a2_wsl[:],
                             a2_bhi[:]).then_inc(sPS.h, 1)
                v.drain()

        def a2_prep_trs(te, k, hs):
            bS = 32 * k
            bE = 64 * k
            for h in hs:
                te.wait_ge(sPS.h, bS + h + 1)
                for jj, buf in ((0, a2_bhi), (1, a2_blo)):
                    b = 2 * h + jj
                    if bE + b >= 2:
                        te.wait_ge(sPEv.h, bE + b - 1)
                    bank = pb[6 + (b % 2)]
                    for c in range(4):
                        inst = te.matmul(bank[:, c * 128:(c + 1) * 128],
                                         buf[:, c * 128:(c + 1) * 128],
                                         identb[:], start=(c == 0), stop=(c == 3))
                    inst.then_inc(sPP.h, 1)



        def a2_main(k):
            kind, half = A2SPECS[k]
            whbm = {"c": c_hb, "d": d_hb}.get(kind)
            bias_off = {"b": 0, "c": 16, "d": 32}[kind]
            mp0 = 64 * k
            me0 = 64 * k
            st_per = 2 if kind == "b" else 1
            st0 = len(dsST.chk)
            bUi = len(dsU.chk)
            bCDi = len(dsCD.chk)
            wTh = wThs[k % 2]
            wTl = wTls[k % 2]

            @block.sync
            def _(sync):
                if k == 0:
                    wf(sync, fvAR1)
                for t in range(NTT):
                    if t >= 2:
                        sync.wait_ge(sMP.h, mp0 + 8 * (t - 1))
                    elif mp0 - 8 * (1 - t) > 0:
                        sync.wait_ge(sMP.h, mp0 - 8 * (1 - t))
                    dsU.dma(sync,
                            uTh[t % 2][:].rearrange("p (kk n) -> p kk n", kk=16),
                            uhi_h.rearrange("(kk p) n -> p kk n", p=128)
                            [:, :, t * TT:(t + 1) * TT])
                    dsU.dma(sync,
                            uTl[t % 2][:].rearrange("p (kk n) -> p kk n", kk=16),
                            ulo_h.rearrange("(kk p) n -> p kk n", p=128)
                            [:, :, t * TT:(t + 1) * TT])
                    if kind == "b":
                        cdu0 = 128 if half == 1 else 0
                        for j in range(8):
                            idx = 8 * t + j
                            if idx >= 1:
                                sync.wait_ge(sCDu.h, cdu0 + 2 * idx)
                            elif cdu0 > 0:
                                sync.wait_ge(sCDu.h, cdu0)
                            dsCD.dma(sync, cbuf[0][:],
                                     c_hb[half * 1024 + j * 128:
                                          half * 1024 + (j + 1) * 128,
                                          t * TT:(t + 1) * TT])
                            dsCD.dma(sync, dbuf[0][:],
                                     d_hb[half * 1024 + j * 128:
                                          half * 1024 + (j + 1) * 128,
                                          t * TT:(t + 1) * TT])

            @block.tensor
            def _(te):
                te.wait_ge(sPEv.h, 64 * (k + 1))
                for t in range(NTT):
                    dsU.wait_upto(te, bUi + 2 * t + 1)
                    for j in range(8):
                        idx = 8 * t + j
                        if idx >= 3:
                            te.wait_ge(sME.h, me0 + idx - 2)
                        elif me0 > 0:
                            te.wait_ge(sME.h, me0)
                        ps = pb[idx % 3]
                        for kk in range(16):
                            col = (j * 16 + kk) * 128
                            te.matmul(ps[:], wTh[:, col:col + 128],
                                      uTh[t % 2][:, kk * TT:(kk + 1) * TT],
                                      start=(kk == 0), stop=False)
                            te.matmul(ps[:], wTh[:, col:col + 128],
                                      uTl[t % 2][:, kk * TT:(kk + 1) * TT],
                                      start=False, stop=False)
                            inst = te.matmul(ps[:], wTl[:, col:col + 128],
                                             uTh[t % 2][:, kk * TT:(kk + 1) * TT],
                                             start=False, stop=(kk == 15))
                        inst.then_inc(sMP.h, 1)
                        if k + 1 < 6 and idx % 2 == 1:
                            a2_prep_trs(te, k + 1, [idx // 2])

            @block.gpsimd
            def _(gp):
                if k == 0:
                    wf(gp, fvAR1)
                for t in range(NTT):
                    for j in range(8):
                        idx = 8 * t + j
                        gp.wait_ge(sME.h, me0 + idx + 1)
                        row = half * 1024 + j * 128
                        if kind != "b":
                            dsST.dma(gp, whbm[row:row + 128, t * TT:(t + 1) * TT],
                                     obuf[0][:])
                        else:
                            dsST.dma(gp, yph_h[row:row + 128, t * TT:(t + 1) * TT],
                                     ypbh[0][:])
                            dsST.dma(gp, ypl_h[row:row + 128, t * TT:(t + 1) * TT],
                                     ypbl[0][:])

            @block.vector
            def _(v):
                if k == 0:
                    wf(v, fvAR1)
                for t in range(NTT):
                    for j in range(8):
                        idx = 8 * t + j
                        v.wait_ge(sMP.h, mp0 + idx + 1)
                        if idx >= 1:
                            dsST.wait_upto(v, st0 + st_per * idx - 1)
                        elif st0 > 0:
                            dsST.wait_upto(v, st0 - 1)
                        ps = pb[idx % 3]
                        bcol = bias_off + half * 8 + j
                        inst = v.tensor_scalar(obuf[0][:], ps[:],
                                               bias_bcd[:, bcol:bcol + 1], None,
                                               ALU.add)
                        if kind != "b":
                            inst.then_inc(sME.h, 1)
                        v.drain()
                        if kind != "b" and k + 1 < 6 and idx % 2 == 1:
                            a2_prep_splits(v, k + 1, [idx // 2])
                        if kind == "b":
                            dsCD.wait_upto(v, bCDi + 2 * idx + 1)
                            init = (0.0 if (t % 4 == 0)
                                    else state[:, half * 8 + j: half * 8 + j + 1])
                            v.tensor_tensor_scan(
                                hsb[:],
                                a_sb[:, half * 8 + j: half * 8 + j + 1]
                                .broadcast_to([128, TT]),
                                obuf[0][:], init, ALU.mult, ALU.add)
                            v.drain()
                            v.tensor_copy(state[:, half * 8 + j: half * 8 + j + 1],
                                          hsb[:, TT - 1: TT])
                            v.drain()
                            v.tensor_tensor(hsb[:], hsb[:], cbuf[0][:],
                                            ALU.mult).then_inc(sCDu.h, 1)
                            v.drain()
                            v.tensor_tensor(hsb[:], hsb[:], dbuf[0][:],
                                            ALU.add).then_inc(sCDu.h, 1)
                            v.drain()
                            v.tensor_copy(ypbh[0][:], hsb[:])
                            v.drain()
                            v.tensor_sub(ypbl[0][:], hsb[:],
                                         ypbh[0][:]).then_inc(sME.h, 1)
                            v.drain()
                            if k + 1 < 6 and idx % 2 == 1:
                                a2_prep_splits(v, k + 1, [idx // 2])

        @block.scalar
        def _(sc):
            a2_prep_actwork(sc, 0, first=True)

        @block.vector
        def _(v):
            a2_prep_splits(v, 0, range(32), first=True)

        @block.tensor
        def _(te):
            wf(te, fvAR1)
            a2_prep_trs(te, 0, range(32))

        for k in range(6):
            if k + 1 < 6:
                @block.scalar
                def _(sc, k=k):
                    a2_prep_actwork(sc, k + 1)
            a2_main(k)

        fv = fence_here((sMP, 64 * 6), (sME, 64 * 6), (sPEv, 64 * 6),
                        *dsST.finals())
        a2ctx.close()

        a3ctx = gctx.enter_context(ExitStack())
        obrep = mkbuf(a3ctx, "obrep", [128, D])
        owTh = mkbuf(a3ctx, "owTh", [128, 16 * 8 * 128], BF16)
        owTl = mkbuf(a3ctx, "owTl", [128, 16 * 8 * 128], BF16)
        a3prctx = a3ctx.enter_context(ExitStack())
        pr_ow = prep_weight(a3prctx, "pow", W["out_w"], 8, I2, [owTh, owTl], fv,
                            layout="k",
                            extra_dma=[(obrep[:],
                                        W["out_b"][None, :].broadcast_to([128, D]))])
        fvA3 = fence_here((pr_ow.sE, pr_ow.n))
        a3prctx.close()

        ypTh = [mkbuf(a3ctx, f"a3yh{i}", [128, 16 * 128], BF16) for i in range(2)]
        ypTl = [mkbuf(a3ctx, f"a3yl{i}", [128, 16 * 128], BF16) for i in range(2)]
        xch = [mkbuf(a3ctx, f"a3x{i}", [128, D]) for i in range(2)]
        xnch = [mkbuf(a3ctx, f"a3n{i}", [128, D]) for i in range(2)]
        preb = mkbuf(a3ctx, "a3pre", [128, D])
        accb = [mkbuf(a3ctx, f"a3a{i}", [128, D]) for i in range(2)]

        dsY = DStream("a3y")
        dsXA = DStream("a3xa")
        sP3 = mksem("a3p")
        sE3 = mksem("a3e")
        dsS3 = DStream("a3s")

        @block.sync
        def _(sync):
            wf(sync, fvA3)
            for m in range(32):
                if m >= 2:
                    sync.wait_ge(sP3.h, m - 1)
                dsY.dma(sync,
                        ypTh[m % 2][:].rearrange("p (k n) -> p k n", k=16),
                        yph_h.rearrange("(k p) n -> p k n", p=128)
                        [:, :, m * 128:(m + 1) * 128])
                dsY.dma(sync,
                        ypTl[m % 2][:].rearrange("p (k n) -> p k n", k=16),
                        ypl_h.rearrange("(k p) n -> p k n", p=128)
                        [:, :, m * 128:(m + 1) * 128])

        @block.scalar
        def _(sc):
            wf(sc, fvA3)
            for m in range(32):
                if m >= 2:
                    sc.wait_ge(sE3.h, m - 1)
                dsXA.dma(sc, xch[m % 2][:], x_h[m * 128:(m + 1) * 128, :])
                dsXA.dma(sc, xnch[m % 2][:], xn_h[m * 128:(m + 1) * 128, :])

        @block.tensor
        def _(te):
            wf(te, fvA3)
            for m in range(32):
                dsY.wait_upto(te, 2 * m + 1)
                if m >= 2:
                    te.wait_ge(sE3.h, m - 1)
                psA = pb[2 * (m % 2)]
                psB = pb[2 * (m % 2) + 1]
                for k in range(16):
                    yh = ypTh[m % 2][:, k * 128:(k + 1) * 128]
                    yl = ypTl[m % 2][:, k * 128:(k + 1) * 128]
                    oh = owTh[:, k * 1024:(k + 1) * 1024]
                    ol = owTl[:, k * 1024:(k + 1) * 1024]
                    te.matmul(psA[:], yh, oh[:, 0:512], start=(k == 0), stop=False)
                    te.matmul(psB[:], yh, oh[:, 512:1024], start=(k == 0), stop=False)
                    te.matmul(psA[:], yh, ol[:, 0:512], start=False, stop=False)
                    te.matmul(psB[:], yh, ol[:, 512:1024], start=False, stop=False)
                    te.matmul(psA[:], yl, oh[:, 0:512], start=False, stop=(k == 15))
                    inst = te.matmul(psB[:], yl, oh[:, 512:1024], start=False,
                                     stop=(k == 15))
                inst.then_inc(sP3.h, 1)
            sP3.v = 32

        @block.gpsimd
        def _(gp):
            wf(gp, fvA3)
            for m in range(32):
                gp.wait_ge(sE3.h, m + 1)
                dsS3.dma(gp, acc_h[m * 128:(m + 1) * 128, :], accb[m % 2][:])

        @block.vector
        def _(v):
            wf(v, fvA3)
            for m in range(32):
                v.wait_ge(sP3.h, m + 1)
                dsXA.wait_upto(v, 2 * m + 1)
                if m >= 2:
                    dsS3.wait(v, m - 2)
                psA = pb[2 * (m % 2)]
                psB = pb[2 * (m % 2) + 1]
                v.tensor_add(preb[:], xch[m % 2][:], xnch[m % 2][:])
                v.drain()
                v.tensor_add(preb[:], preb[:], obrep[:])
                v.drain()
                v.tensor_add(accb[m % 2][:, 0:512], psA[:], preb[:, 0:512])
                v.drain()
                v.tensor_add(accb[m % 2][:, 512:1024], psB[:],
                             preb[:, 512:1024]).then_inc(sE3.h, 1)
                v.drain()
            sE3.v = 32

        fv_endA3 = fence_here(*dsS3.finals(), (s_ard, 112), (s_arc, 3))
        a3ctx.close()

        fctx = gctx.enter_context(ExitStack())
        qdT = mkbuf(fctx, "qdT", [128, 22 * 8 * 128], BF16)
        f1ctx = fctx.enter_context(ExitStack())
        qwgT = mkbuf(f1ctx, "qwgT", [128, 8 * 22 * 128], BF16)
        qwuT = mkbuf(f1ctx, "qwuT", [128, 8 * 22 * 128], BF16)
        fprctx = f1ctx.enter_context(ExitStack())
        pr_qg = prep_weight(fprctx, "pqg", W["w_gate"], 22, D, [qwgT], fv_endA3,
                            mode="quant", layout="k", sw_dst=swg,
                            sw_mul_ap=sxrep[:, 0:1])
        pr_qu = prep_weight(fprctx, "pqu", W["w_up"], 22, D, [qwuT], fv_endA3,
                            mode="quant", layout="k", sw_dst=swu,
                            sw_mul_ap=sxrep[:, 0:1])
        fvF1 = fence_here((pr_qg.sE, pr_qg.n), (pr_qu.sE, pr_qu.n))
        fprctx.close()

        xnf = [mkbuf(f1ctx, f"f1x{i}", [128, D]) for i in range(2)]
        q16b = mkbuf(f1ctx, "f1q16", [128, D], F16)
        qxb = [mkbuf(f1ctx, f"f1qx{i}", [128, D], BF16) for i in range(4)]
        qxT = mkbuf(f1ctx, "f1qxT", [128, 8 * TT], BF16)
        grb = [mkbuf(f1ctx, f"f1gr{i}", [128, TT]) for i in range(2)]
        slb = [mkbuf(f1ctx, f"f1sl{i}", [128, TT]) for i in range(2)]
        hb = [mkbuf(f1ctx, f"f1h{i}", [128, TT]) for i in range(2)]
        fst = mkbuf(f1ctx, "f1st", [128, 1])

        dsXF = DStream("f1x")
        sQ = mksem("f1q")
        sPF = mksem("f1p")
        sEF = mksem("f1e")
        sSL = mksem("f1sl")
        sHE = mksem("f1he")
        dsHS = DStream("f1hs")

        def f1_quant(v, t):
            if t > 0:
                v.wait_ge(sPF.h, 30 * (t - 1) + 8)
            for st in range(4):
                g = 4 * t + st
                dsXF.wait(v, g)
                v.tensor_scalar(q16b[:], xnf[g % 2][:], invsx[:, 0:1], MAGIC,
                                ALU.mult, ALU.add)
                v.drain()
                v.tensor_scalar(q16b[:], q16b[:], MAGIC - 128.0, MAGIC + 127.0,
                                ALU.max, ALU.min)
                v.drain()
                v.tensor_scalar(qxb[st][:], q16b[:], -MAGIC, None,
                                ALU.add).then_inc(sQ.h, 1)
                v.drain()

        def f1_tr(te, t):
            te.wait_ge(sQ.h, 4 * (t + 1))
            for dk in range(8):
                b = 8 * t + dk
                if b >= 2:
                    te.wait_ge(sEF.h, b - 1)
                bank = pb[6 + (b % 2)]
                for st in range(4):
                    inst = te.matmul(bank[:, st * 128:(st + 1) * 128],
                                     qxb[st][:, dk * 128:(dk + 1) * 128],
                                     identb[:], start=(st == 0), stop=(st == 3))
                inst.then_inc(sPF.h, 1)

        def f1_trev(sc, t):
            for dk in range(8):
                b = 8 * t + dk
                sc.wait_ge(sPF.h, 30 * t + dk + 1)
                bank = pb[6 + (b % 2)]
                sc.copy(qxT[:, dk * 512:(dk + 1) * 512], bank[:]).then_inc(sEF.h, 1)
                sc.drain()

        def f1_mm(te, t):
            te.wait_ge(sEF.h, 8 * (t + 1))
            for i in range(22):
                gidx = 22 * t + i
                if gidx >= 2:
                    te.wait_ge(sHE.h, 2 * gidx - 2)
                psG = pb[2 * (i % 2)]
                psU = pb[2 * (i % 2) + 1]
                for wT, ps in ((qwgT, psG), (qwuT, psU)):
                    for k in range(8):
                        col = (k * 22 + i) * 128
                        inst = te.matmul(ps[:], wT[:, col:col + 128],
                                         qxT[:, k * 512:(k + 1) * 512],
                                         start=(k == 0), stop=(k == 7))
                inst.then_inc(sPF.h, 1)

        def f1_sil(sc, t):
            for i in range(22):
                gidx = 22 * t + i
                sc.wait_ge(sHE.h, 2 * gidx + 1)
                if gidx >= 2:
                    sc.wait_ge(sHE.h, 2 * gidx - 2)
                sc.activation(slb[i % 2][:], grb[i % 2][:], AF.Silu).then_inc(sSL.h, 1)
                sc.drain()

        def f1_hep(v, t):
            for i in range(22):
                gidx = 22 * t + i
                v.wait_ge(sPF.h, 30 * t + 8 + i + 1)
                if gidx >= 2:
                    v.wait_ge(sSL.h, gidx - 1)
                    dsHS.wait(v, gidx - 2)
                psG = pb[2 * (i % 2)]
                psU = pb[2 * (i % 2) + 1]
                v.tensor_scalar(grb[i % 2][:], psG[:], swg[:, i:i + 1], None,
                                ALU.mult).then_inc(sHE.h, 1)
                v.drain()
                v.wait_ge(sSL.h, gidx + 1)
                v.scalar_tensor_tensor(hb[i % 2][:], psU[:], swu[:, i:i + 1],
                                       slb[i % 2][:], ALU.mult, ALU.mult)
                v.drain()
                v.reduce_max(fst[:, 0:1], hb[i % 2][:], axis=AX.X,
                             apply_absolute_value=True)
                v.drain()
                v.tensor_tensor(mx_h[:], mx_h[:], fst[:, 0:1],
                                ALU.max).then_inc(sHE.h, 1)
                v.drain()

        @block.sync
        def _(sync):
            wf(sync, fvF1)
            for g in range(NST):
                if g >= 2:
                    sync.wait_ge(sQ.h, g - 1)
                dsXF.dma(sync, xnf[g % 2][:], xn_h[g * 128:(g + 1) * 128, :])

        @block.gpsimd
        def _(gp):
            wf(gp, fvF1)
            for t in range(NTT):
                for i in range(22):
                    gidx = 22 * t + i
                    gp.wait_ge(sHE.h, 2 * gidx + 2)
                    dsHS.dma(gp, hp_h[i * 128:(i + 1) * 128, t * TT:(t + 1) * TT],
                             hb[i % 2][:])

        @block.vector
        def _(v):
            wf(v, fvF1)
            f1_quant(v, 0)

        for t in range(NTT):
            @block.tensor
            def _(te, t=t):
                if t == 0:
                    wf(te, fvF1)
                f1_tr(te, t)
                f1_mm(te, t)

            @block.scalar
            def _(sc, t=t):
                if t == 0:
                    wf(sc, fvF1)
                    f1_trev(sc, 0)
                f1_sil(sc, t)
                if t + 1 < NTT:
                    f1_trev(sc, t + 1)

            @block.vector
            def _(v, t=t):
                if t + 1 < NTT:
                    f1_quant(v, t + 1)
                f1_hep(v, t)

        sQ.v = 4 * NTT
        sPF.v = 30 * NTT
        sEF.v = 8 * NTT
        sSL.v = 22 * NTT
        sHE.v = 2 * 22 * NTT

        fv_dp = fence_here(*dsHS.finals(), (sHE, sHE.v))
        f1ctx.close()

        swdrep = mkbuf(fctx, "swdrep", [128, D])
        dpctx = fctx.enter_context(ExitStack())
        pr_qd = prep_weight(dpctx, "pqd", W["w_down"], 8, IF, [qdT], fv_dp,
                            mode="quant", layout="k", sw_dst=swd8)

        s_a2d = mksem("ar2d")
        s_a2c = mksem("ar2c")
        s_cc2 = mksem("cc2")

        @block.gpsimd
        def _(gp):
            gp.wait_ge(fence.h, fv_dp)
            gp.dma_start(out=scr_h[256:384][:, None], in_=mx_h[:],
                         allow_slow_non_contiguous=True).then_inc(s_a2d.h, 16)
            gp.wait_ge(s_a2d.h, 16)
            gp.dma_start(out=row128[:], in_=scr_h[256:384][None, :]).then_inc(s_a2d.h, 16)

        @block.vector
        def _(v):
            v.wait_ge(s_a2d.h, 32)
            v.reduce_max(sc4[0:1, 0:1], row128[:], axis=AX.X)
            v.drain()
            v.tensor_copy(row8[:], sc4[0:1, 0:1].broadcast_to([1, 8])).then_inc(s_a2c.h, 1)
            v.drain()

        @block.gpsimd
        def _(gp):
            gp.wait_ge(s_a2c.h, 1)
            gp.dma_start(out=cc_i[1][:], in_=row8[0:1, :]).then_inc(s_a2d.h, 16)
            gp.wait_ge(s_a2d.h, 48)
            gp.collective_compute("AllReduce", ALU.max,
                                  replica_groups=[list(range(NCORES))],
                                  ins=[cc_i[1][:]], outs=[cc_o[1][:]]).then_inc(s_cc2.h, 1)
            gp.wait_ge(s_cc2.h, 1)
            gp.dma_start(out=scal[0:1, 8:9],
                         in_=cc_o[1][0:1][None, :]).then_inc(s_a2d.h, 16)

        @block.scalar
        def _(sc):
            sc.wait_ge(s_a2d.h, 64)
            sc.activation(scal[0:1, 9:10], scal[0:1, 1:2],
                          AF.Sigmoid).then_inc(s_a2c.h, 1)
            sc.drain()

        @block.vector
        def _(v):
            v.wait_ge(s_a2c.h, 2)
            v.tensor_tensor(scal[0:1, 10:11], scal[0:1, 9:10], scal[0:1, 8:9], ALU.mult)
            v.drain()
            v.tensor_scalar(scal[0:1, 10:11], scal[0:1, 10:11], QI, None, ALU.mult)
            v.drain()
            v.reciprocal(scal[0:1, 11:12], scal[0:1, 10:11]).then_inc(s_a2c.h, 1)
            v.drain()

        @block.gpsimd
        def _(gp):
            gp.wait_ge(s_a2c.h, 3)
            gp.dma_start(out=scr_h[210:212][None, :],
                         in_=scal[0:1, 10:12]).then_inc(s_a2d.h, 16)
            gp.wait_ge(s_a2d.h, 80)
            gp.dma_start(out=sdrep[:], in_=scr_h[210:211][None, :]
                         .broadcast_to([128, 1])).then_inc(s_a2d.h, 16)
            gp.dma_start(out=invsd[:], in_=scr_h[211:212][None, :]
                         .broadcast_to([128, 1])).then_inc(s_a2d.h, 16)
            gp.wait_ge(s_a2d.h, 112)
            gp.wait_ge(pr_qd.sE.h, pr_qd.n)
            gp.dma_start(out=scr_h[1024:2048].rearrange("(c p) -> p c", p=128),
                         in_=swd8[:],
                         allow_slow_non_contiguous=True).then_inc(s_a2d.h, 16)
            gp.wait_ge(s_a2d.h, 128)
            gp.dma_start(out=swdrep[:], in_=scr_h[1024:2048][None, :]
                         .broadcast_to([128, D])).then_inc(s_a2d.h, 16)

        @block.vector
        def _(v):
            v.wait_ge(s_a2d.h, 144)
            v.tensor_scalar(swdrep[:], swdrep[:], sdrep[:, 0:1], None,
                            ALU.mult).then_inc(s_a2c.h, 1)
            v.drain()

        fvF2 = fence_here((s_a2d, 144), (s_a2c, 4))
        dpctx.close()

        hsl = [mkbuf(fctx, f"f2h{i}", [128, IF]) for i in range(2)]
        hq16 = mkbuf(fctx, "f2q16", [128, IF], F16)
        qhb = [mkbuf(fctx, f"f2qh{i}", [128, IF], BF16) for i in range(2)]
        accf = [mkbuf(fctx, f"f2a{i}", [128, D]) for i in range(2)]
        outb = [mkbuf(fctx, f"f2o{i}", [128, D]) for i in range(2)]

        dsHF = DStream("f2h")
        dsAF = DStream("f2a")
        sQF = mksem("f2q")
        sPD = mksem("f2p")
        sED = mksem("f2e")
        dsOUT = DStream("f2out")

        @block.sync
        def _(sync):
            wf(sync, fvF2)
            for m in range(32):
                if m >= 2:
                    sync.wait_ge(sQF.h, m - 1)
                dsHF.dma(sync,
                         hsl[m % 2][:].rearrange("p (k n) -> p k n", k=22),
                         hp_h.rearrange("(k p) n -> p k n", p=128)
                         [:, :, m * 128:(m + 1) * 128])

        @block.scalar
        def _(sc):
            wf(sc, fvF2)
            for m in range(32):
                if m >= 2:
                    sc.wait_ge(sED.h, m - 1)
                dsAF.dma(sc, accf[m % 2][:], acc_h[m * 128:(m + 1) * 128, :])

        def f2_ep(v, mm):
            v.wait_ge(sPD.h, mm + 1)
            dsAF.wait(v, mm)
            if mm >= 2:
                dsOUT.wait(v, mm - 2)
            psA = pb[2 * (mm % 2)]
            psB = pb[2 * (mm % 2) + 1]
            v.tensor_tensor(outb[mm % 2][:, 0:512], psA[:], swdrep[:, 0:512], ALU.mult)
            v.drain()
            v.tensor_tensor(outb[mm % 2][:, 512:1024], psB[:], swdrep[:, 512:1024],
                            ALU.mult)
            v.drain()
            v.tensor_add(outb[mm % 2][:], outb[mm % 2][:],
                         accf[mm % 2][:]).then_inc(sED.h, 1)
            v.drain()

        @block.gpsimd
        def _(gp):
            wf(gp, fvF2)
            for m in range(32):
                gp.wait_ge(sED.h, m + 1)
                dsOUT.dma(gp, out_h[m * 128:(m + 1) * 128, :], outb[m % 2][:])

        @block.vector
        def _(v):
            wf(v, fvF2)
            for m in range(32):
                dsHF.wait(v, m)
                if m >= 2:
                    v.wait_ge(sPD.h, m - 1)
                v.tensor_scalar(hq16[:], hsl[m % 2][:], invsd[:, 0:1], MAGIC,
                                ALU.mult, ALU.add)
                v.drain()
                v.tensor_scalar(hq16[:], hq16[:], MAGIC - 128.0, MAGIC + 127.0,
                                ALU.max, ALU.min)
                v.drain()
                v.tensor_scalar(qhb[m % 2][:], hq16[:], -MAGIC, None,
                                ALU.add).then_inc(sQF.h, 1)
                v.drain()
                if m >= 1:
                    f2_ep(v, m - 1)
            f2_ep(v, 31)
            sQF.v = 32
            sED.v = 32

        @block.tensor
        def _(te):
            wf(te, fvF2)
            for m in range(32):
                te.wait_ge(sQF.h, m + 1)
                if m >= 2:
                    te.wait_ge(sED.h, m - 1)
                psA = pb[2 * (m % 2)]
                psB = pb[2 * (m % 2) + 1]
                for k in range(22):
                    qh = qhb[m % 2][:, k * 128:(k + 1) * 128]
                    te.matmul(psA[:], qh, qdT[:, (k * 8) * 128:(k * 8) * 128 + 512],
                              start=(k == 0), stop=(k == 21))
                    inst = te.matmul(psB[:], qh,
                                     qdT[:, (k * 8 + 4) * 128:(k * 8 + 4) * 128 + 512],
                                     start=(k == 0), stop=(k == 21))
                inst.then_inc(sPD.h, 1)
            sPD.v = 32

        @block.sync
        def _(sync):
            for sm, v in dsOUT.finals():
                sync.wait_ge(sm.h, v)

        fctx.close()
    return nc


_NC_CACHE = {}


def kernel(L0_attn_x, L0_attn_norm_w, L0_attn_in_w, L0_attn_in_b, L0_attn_gate_w,
           L0_attn_gate_b, L0_attn_b_w, L0_attn_b_b, L0_attn_c_w, L0_attn_c_b,
           L0_attn_d_w, L0_attn_d_b, L0_attn_out_w, L0_attn_out_b, L0_attn_a_log,
           L0_ffn_clip_x, L0_ffn_clip_down, L0_ffn_w_gate, L0_ffn_w_up,
           L0_ffn_w_down):
    if "nc" not in _NC_CACHE:
        _NC_CACHE["nc"] = build_nc()
    nc = _NC_CACHE["nc"]
    x = np.ascontiguousarray(np.asarray(L0_attn_x, dtype=np.float32))
    common = {
        "identb": np.eye(128, dtype=ml_dtypes.bfloat16),
        "norm_w": np.ascontiguousarray(L0_attn_norm_w, dtype=np.float32),
        "in_w": np.ascontiguousarray(L0_attn_in_w, dtype=np.float32),
        "in_b": np.ascontiguousarray(L0_attn_in_b, dtype=np.float32),
        "gate_w": np.ascontiguousarray(L0_attn_gate_w, dtype=np.float32),
        "gate_b": np.ascontiguousarray(L0_attn_gate_b, dtype=np.float32),
        "b_w": np.ascontiguousarray(L0_attn_b_w, dtype=np.float32),
        "b_b": np.ascontiguousarray(L0_attn_b_b, dtype=np.float32),
        "c_w": np.ascontiguousarray(L0_attn_c_w, dtype=np.float32),
        "c_b": np.ascontiguousarray(L0_attn_c_b, dtype=np.float32),
        "d_w": np.ascontiguousarray(L0_attn_d_w, dtype=np.float32),
        "d_b": np.ascontiguousarray(L0_attn_d_b, dtype=np.float32),
        "out_w": np.ascontiguousarray(L0_attn_out_w, dtype=np.float32),
        "out_b": np.ascontiguousarray(L0_attn_out_b, dtype=np.float32),
        "a_log": np.ascontiguousarray(L0_attn_a_log, dtype=np.float32),
        "clip_x": np.ascontiguousarray(L0_ffn_clip_x, dtype=np.float32),
        "clip_down": np.ascontiguousarray(L0_ffn_clip_down, dtype=np.float32),
        "w_gate": np.ascontiguousarray(L0_ffn_w_gate, dtype=np.float32),
        "w_up": np.ascontiguousarray(L0_ffn_w_up, dtype=np.float32),
        "w_down": np.ascontiguousarray(L0_ffn_w_down, dtype=np.float32),
    }
    in_maps = []
    for c in range(NCORES):
        m = dict(common)
        m["x"] = np.ascontiguousarray(x[2 * c:2 * c + 2].reshape(T, D))
        in_maps.append(m)
    res = run_bass_kernel_spmd(nc, in_maps, core_ids=list(range(NCORES)))
    out = np.concatenate(
        [res.results[c]["out"].reshape(2, 2048, D) for c in range(NCORES)], axis=0)
    return np.ascontiguousarray(out.astype(np.float32))


# version 06
# speedup vs baseline: 1.0862x; 1.0862x over previous
import numpy as np
import ml_dtypes
import concourse.bass as bass
import concourse.mybir as mybir
from concourse.bass_utils import run_bass_kernel_spmd
from contextlib import ExitStack
from types import SimpleNamespace

F32 = mybir.dt.float32
F16 = mybir.dt.float16
BF16 = mybir.dt.bfloat16
AF = mybir.ActivationFunctionType
ALU = mybir.AluOpType
AX = mybir.AxisListType

NCORES = 8
D = 1024
I2 = 2048
IF = 2816
T = 4096
TT = 512
NTT = 8
NST = 32
MAGIC = 1536.0
R1024 = 1.0 / 1024.0
QI = 1.0 / 127.0


def build_nc():
    nc = bass.Bass("TRN2", target_bir_lowering=False, debug=False)

    x_h = nc.dram_tensor("x", [T, D], F32, kind="ExternalInput").ap()
    identb_h = nc.dram_tensor("identb", [128, 128], BF16, kind="ExternalInput").ap()
    identf_h = nc.dram_tensor("identf", [128, 128], F32, kind="ExternalInput").ap()
    W = {}
    for nm, shape in [
        ("norm_w", [D]), ("in_w", [I2, D]), ("in_b", [I2]),
        ("gate_w", [I2, D]), ("gate_b", [I2]),
        ("b_w", [I2, I2]), ("b_b", [I2]), ("c_w", [I2, I2]), ("c_b", [I2]),
        ("d_w", [I2, I2]), ("d_b", [I2]),
        ("out_w", [D, I2]), ("out_b", [D]), ("a_log", [I2]),
        ("clip_x", [1]), ("clip_down", [1]),
        ("w_gate", [IF, D]), ("w_up", [IF, D]), ("w_down", [D, IF]),
    ]:
        W[nm] = nc.dram_tensor(nm, shape, F32, kind="ExternalInput").ap()
    out_h = nc.dram_tensor("out", [T, D], F32, kind="ExternalOutput").ap()

    xn_h = nc.dram_tensor("xn_i", [T, D], F32, kind="Internal").ap()
    uhi_h = nc.dram_tensor("uhi_i", [I2, T], BF16, kind="Internal").ap()
    ulo_h = nc.dram_tensor("ulo_i", [I2, T], BF16, kind="Internal").ap()
    c_hb = nc.dram_tensor("c_i", [I2, T], F32, kind="Internal").ap()
    d_hb = nc.dram_tensor("d_i", [I2, T], F32, kind="Internal").ap()
    yph_h = nc.dram_tensor("yph_i", [I2, T], BF16, kind="Internal").ap()
    ypl_h = nc.dram_tensor("ypl_i", [I2, T], BF16, kind="Internal").ap()
    hp_h = nc.dram_tensor("hp_i", [IF, T], F32, kind="Internal").ap()
    acc_h = nc.dram_tensor("acc_i", [T, D], F32, kind="Internal").ap()
    scr_h = nc.dram_tensor("scr_i", [4096], F32, kind="Internal").ap()
    cc_i = [nc.dram_tensor(f"cci{k}", [8], F32, kind="Internal").ap() for k in range(2)]
    cc_o = [nc.dram_tensor(f"cco{k}", [8], F32, kind="Internal").ap() for k in range(2)]

    gctx = ExitStack()
    with gctx:
        def mkbuf(ctx, name, shape, dt=F32):
            return ctx.enter_context(nc.sbuf_tensor(name, shape, dt))

        _semno = [0]

        def mksem(tag):
            _semno[0] += 1
            return SimpleNamespace(
                h=gctx.enter_context(nc.semaphore(f"s{_semno[0]}_{tag}")), v=0)

        _shared = {}

        def shsem(tag):
            if tag not in _shared:
                _shared[tag] = mksem(tag)
            return _shared[tag]

        class DStream:

            def __init__(self, tag):
                self.s = [mksem(tag + "0"), mksem(tag + "1")]
                self.chk = []

            def dma(self, eng, out, in_, **kw):
                sm = self.s[len(self.chk) % 2]
                if sm.v >= 16:
                    eng.wait_ge(sm.h, sm.v)
                eng.dma_start(out=out, in_=in_, **kw).then_inc(sm.h, 16)
                sm.v += 16
                self.chk.append((sm, sm.v))
                return len(self.chk) - 1

            def wait(self, eng, idx):
                sm, v = self.chk[idx]
                eng.wait_ge(sm.h, v)

            def wait_upto(self, eng, idx):
                assert idx < len(self.chk), (idx, len(self.chk))
                best = {}
                for sm, v in self.chk[:idx + 1]:
                    best[id(sm)] = (sm, v)
                for sm, v in best.values():
                    eng.wait_ge(sm.h, v)

            def finals(self):
                best = {}
                for sm, v in self.chk:
                    best[id(sm)] = (sm, v)
                return list(best.values())

        identf = mkbuf(gctx, "identf_s", [128, 128], F32)
        bias_bcd = mkbuf(gctx, "bias_bcd", [128, 48])
        a_sb = mkbuf(gctx, "a_sb", [128, 16])
        _smalls = mkbuf(gctx, "smalls", [128, 80])
        swg = _smalls[:, 0:22]
        swu = _smalls[:, 22:44]
        swd8 = _smalls[:, 44:52]
        state = _smalls[:, 52:68]
        invsx = _smalls[:, 68:69]
        invsd = _smalls[:, 69:70]
        sxrep = _smalls[:, 70:71]
        sdrep = _smalls[:, 71:72]
        mx_xn = _smalls[:, 72:73]
        mx_h = _smalls[:, 73:74]
        _rows = mkbuf(gctx, "rowsml", [1, 160])
        scal = _rows[:, 0:16]
        row128 = _rows[:, 16:144]
        row8 = _rows[:, 144:152]
        sc4 = _rows[:, 152:156]
        a1ctx = gctx.enter_context(ExitStack())
        bias_ig = mkbuf(a1ctx, "bias_ig", [128, 32])
        identb = mkbuf(a1ctx, "identb_s", [128, 128], BF16)

        pb = [gctx.enter_context(nc.psum_tensor(f"pb{i}", [128, 512], F32))
              for i in range(8)]
        block = gctx.enter_context(nc.Block())
        fence = mksem("fence")

        def fence_here(*wait_list):
            fence.v += 1
            fv = fence.v

            @block.gpsimd
            def _(g):
                for s, v in wait_list:
                    g.wait_ge(s.h, v)
                g.sem_inc(fence.h, 1)
            return fv

        def wf(eng, fv):
            eng.wait_ge(fence.h, fv)

        ds_const = DStream("const")
        s_p0a = mksem("p0a")
        s_p0v = mksem("p0v")

        @block.sync
        def _(sync):
            for dst, src in [
                (identb[:], identb_h[:]),
                (identf[:], identf_h[:]),
                (bias_ig[:, 0:16], W["in_b"].rearrange("(c p) -> p c", p=128)),
                (bias_ig[:, 16:32], W["gate_b"].rearrange("(c p) -> p c", p=128)),
                (bias_bcd[:, 0:16], W["b_b"].rearrange("(c p) -> p c", p=128)),
                (bias_bcd[:, 16:32], W["c_b"].rearrange("(c p) -> p c", p=128)),
                (bias_bcd[:, 32:48], W["d_b"].rearrange("(c p) -> p c", p=128)),
                (a_sb[:], W["a_log"].rearrange("(c p) -> p c", p=128)),
                (scal[0:1, 0:1], W["clip_x"][None, :]),
                (scal[0:1, 1:2], W["clip_down"][None, :]),
            ]:
                ds_const.dma(sync, dst, src, allow_slow_non_contiguous=True)

        @block.scalar
        def _(sc):
            for sm, v in ds_const.finals():
                sc.wait_ge(sm.h, v)
            sc.activation(a_sb[:], a_sb[:], AF.Sigmoid, scale=-1.0).then_inc(s_p0a.h, 1)
        s_p0a.v = 1

        @block.vector
        def _(v):
            v.memset(mx_xn[:], 0.0)
            v.drain()
            v.memset(mx_h[:], 0.0).then_inc(s_p0v.h, 1)
        s_p0v.v = 1
        fv0 = fence_here(*ds_const.finals(), (s_p0a, 1), (s_p0v, 1))

        def prep_weight(ctx, tag, src_ap, n_oc, Kdim, dsts, fv, mode="hilo",
                        layout="oc", fold_rep=None, row_off=0, sw_dst=None,
                        sw_mul_ap=None, extra_dma=(), scratch=None,
                        load_engine="sync", pe_guard=None, ident=None):
            nk = Kdim // 128
            grp = "B" if tag in ("pga", "pqu") else "A"
            if "prL" + grp not in _shared:
                _shared["prL" + grp] = DStream("prL" + grp)
            dsL = _shared["prL" + grp]
            sS = shsem("prS" + grp)
            sP = shsem("prP" + grp)
            sE = shsem("prE" + grp)
            bS, bP, bE = sS.v, sP.v, sE.v
            pb_base = 6 if grp == "A" else 4
            ld_idx = {}
            nsrc = 2 if mode == "hilo" else 1
            if scratch is None:
                wsl = [mkbuf(ctx, f"{tag}_w{i}", [128, Kdim]) for i in range(2)]
                bsl = [[mkbuf(ctx, f"{tag}_b{j}{i}", [128, Kdim], BF16)
                        for i in range(2)] for j in range(nsrc)]
                q16 = (mkbuf(ctx, f"{tag}_q16", [128, Kdim], F16)
                       if mode == "quant" else None)
                rst = mkbuf(ctx, f"{tag}_rst", [128, 2]) if mode == "quant" else None
            else:
                wsl, bsl, q16, rst = scratch
            kbs = [(k0, min(4, nk - k0)) for k0 in range(0, nk, 4)]
            nbb = len(kbs) * nsrc
            n_batches = n_oc * nbb

            def emit_loads(eng):
                if fv is not None:
                    wf(eng, fv)
                for dst, src in extra_dma:
                    ld_idx.setdefault("x", []).append(dsL.dma(eng, dst, src))
                for s in range(n_oc):
                    if s >= 2:
                        eng.wait_ge(sS.h, bS + s - 1)
                    ld_idx[s] = dsL.dma(
                        eng, wsl[s % 2][:],
                        src_ap[row_off + s * 128: row_off + (s + 1) * 128, :])

            if load_engine == "sync":
                @block.sync
                def _(sync):
                    emit_loads(sync)
            else:
                @block.scalar
                def _(sc):
                    emit_loads(sc)

            @block.vector
            def _(v):
                if fv is not None:
                    wf(v, fv)
                for s in range(n_oc):
                    dsL.wait(v, ld_idx[s])
                    if s >= 2:
                        v.wait_ge(sP.h, bP + (s - 1) * nbb)
                    elif bP > 0:
                        v.wait_ge(sP.h, bP)
                    if mode == "hilo":
                        if fold_rep is not None:
                            for xi in ld_idx.get("x", []):
                                dsL.wait(v, xi)
                            v.tensor_tensor(wsl[s % 2][:], wsl[s % 2][:], fold_rep,
                                            ALU.mult)
                            v.drain()
                        v.tensor_copy(bsl[0][s % 2][:], wsl[s % 2][:])
                        v.drain()
                        v.tensor_sub(bsl[1][s % 2][:], wsl[s % 2][:],
                                     bsl[0][s % 2][:]).then_inc(sS.h, 1)
                        v.drain()
                    else:
                        v.reduce_max(rst[:, 0:1], wsl[s % 2][:], axis=AX.X,
                                     apply_absolute_value=True)
                        v.drain()
                        v.reciprocal(rst[:, 1:2], rst[:, 0:1])
                        v.drain()
                        v.tensor_scalar(rst[:, 1:2], rst[:, 1:2], 127.0, None, ALU.mult)
                        v.drain()
                        v.tensor_scalar(q16[:], wsl[s % 2][:], rst[:, 1:2], MAGIC,
                                        ALU.mult, ALU.add)
                        v.drain()
                        v.tensor_scalar(bsl[0][s % 2][:], q16[:], -MAGIC, None, ALU.add)
                        v.drain()
                        if sw_mul_ap is not None:
                            v.tensor_scalar(sw_dst[:, s:s + 1], rst[:, 0:1],
                                            sw_mul_ap, QI, ALU.mult,
                                            ALU.mult).then_inc(sS.h, 1)
                        else:
                            v.tensor_scalar(sw_dst[:, s:s + 1], rst[:, 0:1],
                                            QI, None, ALU.mult).then_inc(sS.h, 1)
                        v.drain()
                sS.v = bS + n_oc

            @block.tensor
            def _(te):
                if fv is not None:
                    wf(te, fv)
                if pe_guard is not None:
                    te.wait_ge(pe_guard[0].h, pe_guard[1])
                for xi in ld_idx.get("x", []):
                    dsL.wait(te, xi)
                b = 0
                for s in range(n_oc):
                    te.wait_ge(sS.h, bS + s + 1)
                    for j in range(nsrc):
                        for (k0, kn) in kbs:
                            if b >= 2:
                                te.wait_ge(sE.h, bE + b - 1)
                            bank = pb[pb_base + (b % 2)]
                            idt = ident if ident is not None else identb
                            for c in range(kn):
                                inst = te.matmul(
                                    bank[:, c * 128:(c + 1) * 128],
                                    bsl[j][s % 2][:, (k0 + c) * 128:(k0 + c + 1) * 128],
                                    idt[:], start=(c == 0), stop=(c == kn - 1))
                            inst.then_inc(sP.h, 1)
                            b += 1
                sP.v = bP + n_batches

            @block.scalar
            def _(scx):
                if fv is not None:
                    wf(scx, fv)
                b = 0
                for s in range(n_oc):
                    for j in range(nsrc):
                        for (k0, kn) in kbs:
                            scx.wait_ge(sP.h, bP + b + 1)
                            bank = pb[pb_base + (b % 2)]
                            dst = dsts[j]
                            if layout == "oc":
                                col0 = (s * nk + k0) * 128
                                scx.copy(dst[:, col0:col0 + kn * 128],
                                         bank[:, 0:kn * 128]).then_inc(sE.h, 1)
                            else:
                                dv = dst.rearrange("p (k s2 c) -> p k s2 c",
                                                   k=nk, s2=n_oc)
                                scx.copy(dv[:, k0:k0 + kn, s, :],
                                         bank[:, 0:kn * 128].rearrange(
                                             "p (a c) -> p a c", a=kn)).then_inc(sE.h, 1)
                            scx.drain()
                            b += 1
                sE.v = bE + n_batches
            return SimpleNamespace(sP=sP, sE=sE, n=bE + n_batches)

        nwrep = mkbuf(a1ctx, "nwrep", [128, D])
        inTh = mkbuf(a1ctx, "inTh", [128, 16 * 8 * 128], BF16)
        inTl = mkbuf(a1ctx, "inTl", [128, 16 * 8 * 128], BF16)
        gaTh = mkbuf(a1ctx, "gaTh", [128, 16 * 8 * 128], BF16)
        gaTl = mkbuf(a1ctx, "gaTl", [128, 16 * 8 * 128], BF16)

        prctx = a1ctx.enter_context(ExitStack())
        pr_in = prep_weight(prctx, "pin", W["in_w"], 16, D, [inTh, inTl], fv0,
                            fold_rep=nwrep[:],
                            extra_dma=[(nwrep[:],
                                        W["norm_w"][None, :].broadcast_to([128, D]))])
        pr_ga = prep_weight(prctx, "pga", W["gate_w"], 16, D, [gaTh, gaTl], fv0,
                            fold_rep=nwrep[:])
        fvA1 = fence_here((pr_in.sE, pr_in.n), (pr_ga.sE, pr_ga.n))
        prctx.close()

        xb = [mkbuf(a1ctx, f"a1x{i}", [128, D]) for i in range(2)]
        xcb = [mkbuf(a1ctx, f"a1xc{i}", [128, D]) for i in range(2)]
        xnb = [mkbuf(a1ctx, f"a1xn{i}", [128, D]) for i in range(2)]
        x2h = [mkbuf(a1ctx, f"a1h{i}", [128, D], BF16) for i in range(4)]
        x2l = [mkbuf(a1ctx, f"a1l{i}", [128, D], BF16) for i in range(4)]
        xTh = mkbuf(a1ctx, "a1xTh", [128, 8 * 512], BF16)
        xTl = mkbuf(a1ctx, "a1xTl", [128, 8 * 512], BF16)
        st1 = mkbuf(a1ctx, "a1st1", [128, 4])
        st2 = mkbuf(a1ctx, "a1st2", [128, 2])
        sq2 = mkbuf(a1ctx, "a1sq2", [128, 2])
        rsy = mkbuf(a1ctx, "a1rsy", [128, 2])
        rtmp = mkbuf(a1ctx, "a1rt", [128, 2])
        ubf = [mkbuf(a1ctx, f"a1u{i}", [128, TT]) for i in range(2)]
        gbf = [mkbuf(a1ctx, f"a1g{i}", [128, TT]) for i in range(2)]
        uhb = [mkbuf(a1ctx, f"a1uh{i}", [128, TT], BF16) for i in range(2)]
        ulb = [mkbuf(a1ctx, f"a1ul{i}", [128, TT], BF16) for i in range(2)]

        dsX = DStream("a1x")
        sDV = mksem("a1dv")
        sAC = mksem("a1ac")
        sPE = mksem("a1pe")
        sEV = mksem("a1ev")
        sSG = mksem("a1sg")
        sUE = mksem("a1ue")
        dsXS = DStream("a1xs")
        dsUS = DStream("a1us")

        def a1_sub(v, t):
            if t > 0:
                v.wait_ge(sPE.h, 32 * (t - 1) + 16)
            for st in range(4):
                g = 4 * t + st
                dsX.wait(v, g)
                v.reduce_sum(st1[:, 0:1], xb[g % 2][:], axis=AX.X)
                v.drain()
                v.tensor_scalar(st1[:, 1:2], st1[:, 0:1], R1024, None, ALU.mult)
                v.drain()
                v.tensor_scalar(xcb[g % 2][:], xb[g % 2][:], st1[:, 1:2], None,
                                ALU.subtract).then_inc(sDV.h, 1)
                v.drain()
                v.wait_ge(sAC.h, 2 * g + 1)
                v.tensor_scalar(st2[:, 0:1], st1[:, 2:3], R1024, 1e-5, ALU.mult,
                                ALU.add)
                v.drain()
                v.tensor_scalar(st1[:, 3:4], st1[:, 2:3], R1024, None, ALU.mult)
                v.drain()
                v.reciprocal(rtmp[:, 0:1], st2[:, 0:1])
                v.drain()
                v.tensor_tensor(st2[:, 1:2], st1[:, 3:4], rtmp[:, 0:1], ALU.mult)
                v.drain()
                v.tensor_scalar(st2[:, 1:2], st2[:, 1:2], 1e-6, None,
                                ALU.add).then_inc(sDV.h, 1)
                v.drain()
                v.wait_ge(sAC.h, 2 * g + 2)
                v.reciprocal(rsy[:], sq2[:])
                v.drain()
                v.tensor_tensor(rtmp[:], rsy[:], rsy[:], ALU.mult)
                v.drain()
                v.tensor_tensor(rtmp[:], rtmp[:], st2[:], ALU.mult)
                v.drain()
                v.tensor_scalar(rtmp[:], rtmp[:], -0.5, 1.5, ALU.mult, ALU.add)
                v.drain()
                v.tensor_tensor(rsy[:], rsy[:], rtmp[:], ALU.mult)
                v.drain()
                if g >= 2:
                    dsXS.wait(v, g - 2)
                v.tensor_scalar(xnb[g % 2][:], xcb[g % 2][:], rsy[:, 0:1], None,
                                ALU.mult)
                v.drain()
                v.reduce_max(st1[:, 0:1], xnb[g % 2][:], axis=AX.X,
                             apply_absolute_value=True)
                v.drain()
                v.tensor_tensor(mx_xn[:], mx_xn[:], st1[:, 0:1], ALU.max)
                v.drain()
                v.tensor_scalar(x2h[st][:], xnb[g % 2][:], rsy[:, 1:2], None, ALU.mult)
                v.drain()
                v.scalar_tensor_tensor(x2l[st][:], xnb[g % 2][:], rsy[:, 1:2],
                                       x2h[st][:], ALU.mult,
                                       ALU.subtract).then_inc(sDV.h, 1)
                v.drain()

        def a1_acts(sc, t):
            for st in range(4):
                g = 4 * t + st
                sc.wait_ge(sDV.h, 3 * g + 1)
                sc.activation(xb[g % 2][:], xcb[g % 2][:], AF.Square,
                              accum_out=st1[:, 2:3]).then_inc(sAC.h, 1)
                sc.drain()
                sc.wait_ge(sDV.h, 3 * g + 2)
                sc.activation(sq2[:], st2[:], AF.Sqrt).then_inc(sAC.h, 1)
                sc.drain()

        def a1_tr(te, t):
            te.wait_ge(sDV.h, 3 * (4 * t + 3) + 3)
            for dk in range(8):
                for j, src in ((0, x2h), (1, x2l)):
                    b = 16 * t + dk * 2 + j
                    if b >= 2:
                        te.wait_ge(sEV.h, b - 1)
                    bank = pb[6 + (b % 2)]
                    for st in range(4):
                        inst = te.matmul(bank[:, st * 128:(st + 1) * 128],
                                         src[st][:, dk * 128:(dk + 1) * 128],
                                         identb[:], start=(st == 0), stop=(st == 3))
                    inst.then_inc(sPE.h, 1)

        def a1_trev(sc, t):
            for dk in range(8):
                for j, dst in ((0, xTh), (1, xTl)):
                    b = 16 * t + dk * 2 + j
                    sc.wait_ge(sPE.h, 32 * t + dk * 2 + j + 1)
                    bank = pb[6 + (b % 2)]
                    sc.copy(dst[:, dk * 512:(dk + 1) * 512], bank[:]).then_inc(sEV.h, 1)
                    sc.drain()

        def a1_mm(te, t):
            te.wait_ge(sEV.h, 16 * (t + 1))
            for i in range(16):
                gidx = 16 * t + i
                if gidx >= 2:
                    te.wait_ge(sUE.h, gidx - 1)
                psI = pb[2 * (i % 2)]
                psG = pb[2 * (i % 2) + 1]
                for wh, wl, ps in ((inTh, inTl, psI), (gaTh, gaTl, psG)):
                    for k in range(8):
                        col = (i * 8 + k) * 128
                        te.matmul(ps[:], wh[:, col:col + 128],
                                  xTh[:, k * 512:(k + 1) * 512],
                                  start=(k == 0), stop=False)
                        te.matmul(ps[:], wh[:, col:col + 128],
                                  xTl[:, k * 512:(k + 1) * 512],
                                  start=False, stop=False)
                        inst = te.matmul(ps[:], wl[:, col:col + 128],
                                         xTh[:, k * 512:(k + 1) * 512],
                                         start=False, stop=(k == 7))
                inst.then_inc(sPE.h, 1)

        def a1_sig(sc, t):
            for i in range(16):
                gidx = 16 * t + i
                sc.wait_ge(sPE.h, 32 * t + 16 + i + 1)
                if gidx >= 2:
                    sc.wait_ge(sUE.h, gidx - 1)
                psG = pb[2 * (i % 2) + 1]
                sc.activation(gbf[i % 2][:], psG[:], AF.Sigmoid,
                              bias=bias_ig[:, 16 + i:17 + i]).then_inc(sSG.h, 1)
                sc.drain()

        def a1_uep(v, t):
            for i in range(16):
                gidx = 16 * t + i
                v.wait_ge(sSG.h, gidx + 1)
                if gidx >= 2:
                    dsUS.wait_upto(v, 2 * (gidx - 2) + 1)
                psI = pb[2 * (i % 2)]
                v.scalar_tensor_tensor(ubf[i % 2][:], psI[:], bias_ig[:, i:i + 1],
                                       gbf[i % 2][:], ALU.add, ALU.mult)
                v.drain()
                v.tensor_copy(uhb[i % 2][:], ubf[i % 2][:])
                v.drain()
                v.tensor_sub(ulb[i % 2][:], ubf[i % 2][:],
                             uhb[i % 2][:]).then_inc(sUE.h, 1)
                v.drain()

        @block.sync
        def _(sync):
            wf(sync, fvA1)
            for g in range(NST):
                if g >= 2:
                    sync.wait_ge(sAC.h, 2 * (g - 2) + 1)
                dsX.dma(sync, xb[g % 2][:], x_h[g * 128:(g + 1) * 128, :])

        def a1_gp_xn(gp, t):
            for st in range(4):
                g = 4 * t + st
                gp.wait_ge(sDV.h, 3 * g + 3)
                dsXS.dma(gp, xn_h[g * 128:(g + 1) * 128, :], xnb[g % 2][:])

        def a1_gp_u(gp, t):
            for i in range(16):
                gidx = 16 * t + i
                gp.wait_ge(sUE.h, gidx + 1)
                dsUS.dma(gp, uhi_h[i * 128:(i + 1) * 128, t * TT:(t + 1) * TT],
                         uhb[i % 2][:])
                dsUS.dma(gp, ulo_h[i * 128:(i + 1) * 128, t * TT:(t + 1) * TT],
                         ulb[i % 2][:])

        @block.gpsimd
        def _(gp):
            a1_gp_xn(gp, 0)
            for t in range(NTT):
                if t + 1 < NTT:
                    a1_gp_xn(gp, t + 1)
                a1_gp_u(gp, t)

        @block.vector
        def _(v):
            wf(v, fvA1)
            a1_sub(v, 0)

        @block.scalar
        def _(sc):
            wf(sc, fvA1)
            a1_acts(sc, 0)

        for t in range(NTT):
            @block.tensor
            def _(te, t=t):
                if t == 0:
                    wf(te, fvA1)
                a1_tr(te, t)
                a1_mm(te, t)

            @block.scalar
            def _(sc, t=t):
                if t == 0:
                    a1_trev(sc, 0)
                if t + 1 < NTT:
                    a1_acts(sc, t + 1)
                a1_sig(sc, t)
                if t + 1 < NTT:
                    a1_trev(sc, t + 1)

            @block.vector
            def _(v, t=t):
                if t + 1 < NTT:
                    a1_sub(v, t + 1)
                a1_uep(v, t)

        sDV.v = 3 * NST
        sAC.v = 2 * NST
        sPE.v = 32 * NTT
        sEV.v = 16 * NTT
        sSG.v = 16 * NTT
        sUE.v = 16 * NTT

        fvAR1 = fence_here(*dsUS.finals(), *dsXS.finals(), (sUE, sUE.v))
        a1ctx.close()

        s_ard = mksem("ard")
        s_arc = mksem("arc")
        s_cc1 = mksem("cc1")

        @block.gpsimd
        def _(gp):
            gp.wait_ge(fence.h, fvAR1)
            gp.dma_start(out=scr_h[0:128][:, None], in_=mx_xn[:],
                         allow_slow_non_contiguous=True).then_inc(s_ard.h, 16)
            gp.wait_ge(s_ard.h, 16)
            gp.dma_start(out=row128[:], in_=scr_h[0:128][None, :]).then_inc(s_ard.h, 16)

        @block.vector
        def _(v):
            v.wait_ge(s_ard.h, 32)
            v.reduce_max(sc4[0:1, 0:1], row128[:], axis=AX.X)
            v.drain()
            v.tensor_copy(row8[:], sc4[0:1, 0:1].broadcast_to([1, 8])).then_inc(s_arc.h, 1)
            v.drain()

        @block.gpsimd
        def _(gp):
            gp.wait_ge(s_arc.h, 1)
            gp.dma_start(out=cc_i[0][:], in_=row8[0:1, :]).then_inc(s_ard.h, 16)
            gp.wait_ge(s_ard.h, 48)
            gp.collective_compute("AllReduce", ALU.max,
                                  replica_groups=[list(range(NCORES))],
                                  ins=[cc_i[0][:]], outs=[cc_o[0][:]]).then_inc(s_cc1.h, 1)
            gp.wait_ge(s_cc1.h, 1)
            gp.dma_start(out=scal[0:1, 2:3],
                         in_=cc_o[0][0:1][None, :]).then_inc(s_ard.h, 16)

        @block.scalar
        def _(sc):
            sc.wait_ge(s_ard.h, 64)
            sc.activation(scal[0:1, 4:5], scal[0:1, 0:1],
                          AF.Sigmoid).then_inc(s_arc.h, 1)
            sc.drain()

        @block.vector
        def _(v):
            v.wait_ge(s_arc.h, 2)
            v.tensor_tensor(scal[0:1, 5:6], scal[0:1, 4:5], scal[0:1, 2:3], ALU.mult)
            v.drain()
            v.tensor_scalar(scal[0:1, 5:6], scal[0:1, 5:6], QI, None, ALU.mult)
            v.drain()
            v.reciprocal(scal[0:1, 6:7], scal[0:1, 5:6]).then_inc(s_arc.h, 1)
            v.drain()

        @block.gpsimd
        def _(gp):
            gp.wait_ge(s_arc.h, 3)
            gp.dma_start(out=scr_h[200:202][None, :],
                         in_=scal[0:1, 5:7]).then_inc(s_ard.h, 16)
            gp.wait_ge(s_ard.h, 80)
            gp.dma_start(out=sxrep[:], in_=scr_h[200:201][None, :]
                         .broadcast_to([128, 1])).then_inc(s_ard.h, 16)
            gp.dma_start(out=invsx[:], in_=scr_h[201:202][None, :]
                         .broadcast_to([128, 1])).then_inc(s_ard.h, 16)
        s_ard.v = 112
        s_arc.v = 3

        a2ctx = gctx.enter_context(ExitStack())
        wThs = [mkbuf(a2ctx, f"wTh{i}", [128, 16 * 8 * 128], BF16) for i in range(2)]
        wTls = [mkbuf(a2ctx, f"wTl{i}", [128, 16 * 8 * 128], BF16) for i in range(2)]
        uTh = [mkbuf(a2ctx, f"uTh{i}", [128, 16 * TT], BF16) for i in range(2)]
        uTl = [mkbuf(a2ctx, f"uTl{i}", [128, 16 * TT], BF16) for i in range(2)]
        obuf = [mkbuf(a2ctx, "a2ob", [128, TT])] * 2
        cbuf = [mkbuf(a2ctx, f"a2c{i}", [128, TT]) for i in range(2)]
        dbuf = [mkbuf(a2ctx, f"a2d{i}", [128, TT]) for i in range(2)]
        ypbh = [mkbuf(a2ctx, "a2yh0", [128, TT], BF16)]
        ypbl = [mkbuf(a2ctx, "a2yl0", [128, TT], BF16)]
        a2_wsl = mkbuf(a2ctx, "a2pw", [128, 512])

        dsU = DStream("a2u")
        dsCD = DStream("a2cd")
        dsST = DStream("a2st")
        dsPL = DStream("a2pl")
        sPP = mksem("a2pp")
        sEvH = mksem("a2eh")
        sPEv = mksem("a2pe")
        sMP = mksem("a2mp")
        sME = mksem("a2me")
        sCDu = mksem("a2cdu")

        A2SPECS = [("c", 0), ("c", 1), ("d", 0), ("d", 1), ("b", 0), ("b", 1)]
        prep_ld_idx = [dict() for _ in range(6)]

        def a2_prep_load1(eng, k, h):
            kind, half = A2SPECS[k]
            wsrc = {"c": W["c_w"], "d": W["d_w"], "b": W["b_w"]}[kind]
            bP = 32 * k
            si, q = h // 4, h % 4
            if h >= 1:
                eng.wait_ge(sPP.h, bP + h)
            prep_ld_idx[k][h] = dsPL.dma(
                eng, a2_wsl[:],
                wsrc[half * 1024 + si * 128: half * 1024 + (si + 1) * 128,
                     q * 512:(q + 1) * 512])

        def a2_prep_hi1(sc, k, h):
            bP = 32 * k
            si, q = h // 4, h % 4
            sc.wait_ge(sPP.h, bP + h + 1)
            bank = pb[6 + (h % 2)]
            col0 = (si * 16 + q * 4) * 128
            sc.copy(wThs[k % 2][:, col0:col0 + 512],
                    bank[:]).then_inc(sEvH.h, 1)
            sc.drain()

        def a2_prep_lo1(v, k, h):
            bE = 32 * k
            si, q = h // 4, h % 4
            v.wait_ge(sEvH.h, bE + h + 1)
            bank = pb[6 + (h % 2)]
            col0 = (si * 16 + q * 4) * 128
            v.tensor_sub(wTls[k % 2][:, col0:col0 + 512], bank[:],
                         wThs[k % 2][:, col0:col0 + 512]).then_inc(sPEv.h, 1)
            v.drain()

        def a2_prep_actwork(sc, k, first=False):
            if first:
                wf(sc, fvAR1)
            a2_prep_load1(sc, k, 0)
            a2_prep_load1(sc, k, 1)
            for h in range(32):
                a2_prep_hi1(sc, k, h)
                if h + 2 < 32:
                    a2_prep_load1(sc, k, h + 2)

        def a2_prep_los(v, k, hs, first=False):
            if first:
                wf(v, fvAR1)
            for h in hs:
                a2_prep_lo1(v, k, h)

        def a2_prep_trs(te, k, hs):
            bE = 32 * k
            for h in hs:
                dsPL.wait(te, prep_ld_idx[k][h])
                if bE + h >= 2:
                    te.wait_ge(sPEv.h, bE + h - 1)
                bank = pb[6 + (h % 2)]
                for c in range(4):
                    inst = te.matmul(bank[:, c * 128:(c + 1) * 128],
                                     a2_wsl[:, c * 128:(c + 1) * 128],
                                     identf[:], start=(c == 0), stop=(c == 3))
                inst.then_inc(sPP.h, 1)



        def a2_main(k):
            kind, half = A2SPECS[k]
            whbm = {"c": c_hb, "d": d_hb}.get(kind)
            bias_off = {"b": 0, "c": 16, "d": 32}[kind]
            mp0 = 64 * k
            me0 = 64 * k
            st_per = 2 if kind == "b" else 1
            st0 = len(dsST.chk)
            bUi = len(dsU.chk)
            bCDi = len(dsCD.chk)
            wTh = wThs[k % 2]
            wTl = wTls[k % 2]

            @block.sync
            def _(sync):
                if k == 0:
                    wf(sync, fvAR1)
                for t in range(NTT):
                    if t >= 2:
                        sync.wait_ge(sMP.h, mp0 + 8 * (t - 1))
                    elif mp0 - 8 * (1 - t) > 0:
                        sync.wait_ge(sMP.h, mp0 - 8 * (1 - t))
                    dsU.dma(sync,
                            uTh[t % 2][:].rearrange("p (kk n) -> p kk n", kk=16),
                            uhi_h.rearrange("(kk p) n -> p kk n", p=128)
                            [:, :, t * TT:(t + 1) * TT])
                    dsU.dma(sync,
                            uTl[t % 2][:].rearrange("p (kk n) -> p kk n", kk=16),
                            ulo_h.rearrange("(kk p) n -> p kk n", p=128)
                            [:, :, t * TT:(t + 1) * TT])
                    if kind == "b":
                        cdu0 = 128 if half == 1 else 0
                        for j in range(8):
                            idx = 8 * t + j
                            if idx >= 2:
                                sync.wait_ge(sCDu.h, cdu0 + 2 * (idx - 1))
                            elif cdu0 > 0:
                                sync.wait_ge(sCDu.h, cdu0)
                            dsCD.dma(sync, cbuf[idx % 2][:],
                                     c_hb[half * 1024 + j * 128:
                                          half * 1024 + (j + 1) * 128,
                                          t * TT:(t + 1) * TT])
                            dsCD.dma(sync, dbuf[idx % 2][:],
                                     d_hb[half * 1024 + j * 128:
                                          half * 1024 + (j + 1) * 128,
                                          t * TT:(t + 1) * TT])

            @block.tensor
            def _(te):
                te.wait_ge(sPEv.h, 32 * (k + 1))
                for t in range(NTT):
                    dsU.wait_upto(te, bUi + 2 * t + 1)
                    for j in range(8):
                        idx = 8 * t + j
                        if idx >= 3:
                            te.wait_ge(sME.h, me0 + idx - 2)
                        elif me0 > 0:
                            te.wait_ge(sME.h, me0)
                        ps = pb[idx % 3]
                        for kk in range(16):
                            col = (j * 16 + kk) * 128
                            te.matmul(ps[:], wTh[:, col:col + 128],
                                      uTh[t % 2][:, kk * TT:(kk + 1) * TT],
                                      start=(kk == 0), stop=False)
                            te.matmul(ps[:], wTh[:, col:col + 128],
                                      uTl[t % 2][:, kk * TT:(kk + 1) * TT],
                                      start=False, stop=False)
                            inst = te.matmul(ps[:], wTl[:, col:col + 128],
                                             uTh[t % 2][:, kk * TT:(kk + 1) * TT],
                                             start=False, stop=(kk == 15))
                        inst.then_inc(sMP.h, 1)
                        if k + 1 < 6 and idx % 2 == 1:
                            a2_prep_trs(te, k + 1, [idx // 2])

            @block.gpsimd
            def _(gp):
                if k == 0:
                    wf(gp, fvAR1)
                for t in range(NTT):
                    for j in range(8):
                        idx = 8 * t + j
                        gp.wait_ge(sME.h, me0 + idx + 1)
                        row = half * 1024 + j * 128
                        if kind != "b":
                            dsST.dma(gp, whbm[row:row + 128, t * TT:(t + 1) * TT],
                                     obuf[idx % 2][:])
                        else:
                            dsST.dma(gp, yph_h[row:row + 128, t * TT:(t + 1) * TT],
                                     ypbh[0][:])
                            dsST.dma(gp, ypl_h[row:row + 128, t * TT:(t + 1) * TT],
                                     ypbl[0][:])

            @block.vector
            def _(v):
                if k == 0:
                    wf(v, fvAR1)
                for t in range(NTT):
                    for j in range(8):
                        idx = 8 * t + j
                        v.wait_ge(sMP.h, mp0 + idx + 1)
                        if kind != "b" and idx >= 1:
                            dsST.wait_upto(v, st0 + st_per * idx - 1)
                        elif st0 > 0 and idx < 1:
                            dsST.wait_upto(v, st0 - 1)
                        ps = pb[idx % 3]
                        bcol = bias_off + half * 8 + j
                        inst = v.tensor_scalar(obuf[idx % 2][:], ps[:],
                                               bias_bcd[:, bcol:bcol + 1], None,
                                               ALU.add)
                        if kind != "b":
                            inst.then_inc(sME.h, 1)
                        v.drain()
                        if kind != "b" and k + 1 < 6 and idx % 2 == 1:
                            a2_prep_los(v, k + 1, [idx // 2])
                        if kind == "b":
                            dsCD.wait_upto(v, bCDi + 2 * idx + 1)
                            hsv = pb[3]
                            init = (0.0 if (t % 4 == 0)
                                    else state[:, half * 8 + j: half * 8 + j + 1])
                            v.tensor_tensor_scan(
                                hsv[:],
                                a_sb[:, half * 8 + j: half * 8 + j + 1]
                                .broadcast_to([128, TT]),
                                obuf[idx % 2][:], init, ALU.mult, ALU.add)
                            v.drain()
                            v.tensor_copy(state[:, half * 8 + j: half * 8 + j + 1],
                                          hsv[:, TT - 1: TT])
                            v.drain()
                            v.tensor_tensor(hsv[:], hsv[:], cbuf[idx % 2][:],
                                            ALU.mult).then_inc(sCDu.h, 1)
                            v.drain()
                            v.tensor_tensor(hsv[:], hsv[:], dbuf[idx % 2][:],
                                            ALU.add).then_inc(sCDu.h, 1)
                            v.drain()
                            if idx >= 1:
                                dsST.wait_upto(v, st0 + 2 * idx - 1)
                            v.tensor_copy(ypbh[0][:], hsv[:])
                            v.drain()
                            v.tensor_sub(ypbl[0][:], hsv[:],
                                         ypbh[0][:]).then_inc(sME.h, 1)
                            v.drain()
                            if k + 1 < 6 and idx % 2 == 1:
                                a2_prep_los(v, k + 1, [idx // 2])

        @block.scalar
        def _(sc):
            a2_prep_actwork(sc, 0, first=True)

        @block.vector
        def _(v):
            a2_prep_los(v, 0, range(32), first=True)

        @block.tensor
        def _(te):
            wf(te, fvAR1)
            a2_prep_trs(te, 0, range(32))

        for k in range(6):
            if k + 1 < 6:
                @block.scalar
                def _(sc, k=k):
                    a2_prep_actwork(sc, k + 1)
            a2_main(k)

        fv = fence_here((sMP, 64 * 6), (sME, 64 * 6), (sPEv, 32 * 6),
                        *dsST.finals())
        a2ctx.close()

        a3ctx = gctx.enter_context(ExitStack())
        obrep = mkbuf(a3ctx, "obrep", [128, D])
        idb3 = mkbuf(a3ctx, "idb3", [128, 128], BF16)
        owTh = mkbuf(a3ctx, "owTh", [128, 16 * 8 * 128], BF16)
        owTl = mkbuf(a3ctx, "owTl", [128, 16 * 8 * 128], BF16)
        a3prctx = a3ctx.enter_context(ExitStack())
        pr_ow = prep_weight(a3prctx, "pow", W["out_w"], 8, I2, [owTh, owTl], fv,
                            layout="k", ident=idb3,
                            extra_dma=[(obrep[:],
                                        W["out_b"][None, :].broadcast_to([128, D])),
                                       (idb3[:], identb_h[:])])
        fvA3 = fence_here((pr_ow.sE, pr_ow.n))
        a3prctx.close()

        ypTh = [mkbuf(a3ctx, f"a3yh{i}", [128, 16 * 128], BF16) for i in range(2)]
        ypTl = [mkbuf(a3ctx, f"a3yl{i}", [128, 16 * 128], BF16) for i in range(2)]
        xch = [mkbuf(a3ctx, f"a3x{i}", [128, D]) for i in range(2)]
        xnch = [mkbuf(a3ctx, f"a3n{i}", [128, D]) for i in range(2)]
        preb = mkbuf(a3ctx, "a3pre", [128, D])
        accb = [mkbuf(a3ctx, f"a3a{i}", [128, D]) for i in range(2)]

        dsY = DStream("a3y")
        dsXA = DStream("a3xa")
        sP3 = mksem("a3p")
        sE3 = mksem("a3e")
        dsS3 = DStream("a3s")

        @block.sync
        def _(sync):
            wf(sync, fvA3)
            for m in range(32):
                if m >= 2:
                    sync.wait_ge(sP3.h, m - 1)
                dsY.dma(sync,
                        ypTh[m % 2][:].rearrange("p (k n) -> p k n", k=16),
                        yph_h.rearrange("(k p) n -> p k n", p=128)
                        [:, :, m * 128:(m + 1) * 128])
                dsY.dma(sync,
                        ypTl[m % 2][:].rearrange("p (k n) -> p k n", k=16),
                        ypl_h.rearrange("(k p) n -> p k n", p=128)
                        [:, :, m * 128:(m + 1) * 128])

        @block.scalar
        def _(sc):
            wf(sc, fvA3)
            for m in range(32):
                if m >= 2:
                    sc.wait_ge(sE3.h, m - 1)
                dsXA.dma(sc, xch[m % 2][:], x_h[m * 128:(m + 1) * 128, :])
                dsXA.dma(sc, xnch[m % 2][:], xn_h[m * 128:(m + 1) * 128, :])

        @block.tensor
        def _(te):
            wf(te, fvA3)
            for m in range(32):
                dsY.wait_upto(te, 2 * m + 1)
                if m >= 2:
                    te.wait_ge(sE3.h, m - 1)
                psA = pb[2 * (m % 2)]
                psB = pb[2 * (m % 2) + 1]
                for k in range(16):
                    yh = ypTh[m % 2][:, k * 128:(k + 1) * 128]
                    yl = ypTl[m % 2][:, k * 128:(k + 1) * 128]
                    oh = owTh[:, k * 1024:(k + 1) * 1024]
                    ol = owTl[:, k * 1024:(k + 1) * 1024]
                    te.matmul(psA[:], yh, oh[:, 0:512], start=(k == 0), stop=False)
                    te.matmul(psB[:], yh, oh[:, 512:1024], start=(k == 0), stop=False)
                    te.matmul(psA[:], yh, ol[:, 0:512], start=False, stop=False)
                    te.matmul(psB[:], yh, ol[:, 512:1024], start=False, stop=False)
                    te.matmul(psA[:], yl, oh[:, 0:512], start=False, stop=(k == 15))
                    inst = te.matmul(psB[:], yl, oh[:, 512:1024], start=False,
                                     stop=(k == 15))
                inst.then_inc(sP3.h, 1)
            sP3.v = 32

        @block.gpsimd
        def _(gp):
            wf(gp, fvA3)
            for m in range(32):
                gp.wait_ge(sE3.h, m + 1)
                dsS3.dma(gp, acc_h[m * 128:(m + 1) * 128, :], accb[m % 2][:])

        @block.vector
        def _(v):
            wf(v, fvA3)
            for m in range(32):
                v.wait_ge(sP3.h, m + 1)
                dsXA.wait_upto(v, 2 * m + 1)
                if m >= 2:
                    dsS3.wait(v, m - 2)
                psA = pb[2 * (m % 2)]
                psB = pb[2 * (m % 2) + 1]
                v.tensor_add(preb[:], xch[m % 2][:], xnch[m % 2][:])
                v.drain()
                v.tensor_add(preb[:], preb[:], obrep[:])
                v.drain()
                v.tensor_add(accb[m % 2][:, 0:512], psA[:], preb[:, 0:512])
                v.drain()
                v.tensor_add(accb[m % 2][:, 512:1024], psB[:],
                             preb[:, 512:1024]).then_inc(sE3.h, 1)
                v.drain()
            sE3.v = 32

        fv_endA3 = fence_here(*dsS3.finals(), (s_ard, 112), (s_arc, 3))
        a3ctx.close()

        fctx = gctx.enter_context(ExitStack())
        idbf = mkbuf(fctx, "idbf", [128, 128], BF16)
        qdT = mkbuf(fctx, "qdT", [128, 22 * 8 * 128], BF16)
        f1ctx = fctx.enter_context(ExitStack())
        qwgT = mkbuf(f1ctx, "qwgT", [128, 8 * 22 * 128], BF16)
        qwuT = mkbuf(f1ctx, "qwuT", [128, 8 * 22 * 128], BF16)
        fprctx = f1ctx.enter_context(ExitStack())
        pr_qg = prep_weight(fprctx, "pqg", W["w_gate"], 22, D, [qwgT], fv_endA3,
                            mode="quant", layout="k", sw_dst=swg,
                            sw_mul_ap=sxrep[:, 0:1], ident=idbf,
                            extra_dma=[(idbf[:], identb_h[:])])
        pr_qu = prep_weight(fprctx, "pqu", W["w_up"], 22, D, [qwuT], fv_endA3,
                            mode="quant", layout="k", sw_dst=swu,
                            sw_mul_ap=sxrep[:, 0:1], ident=idbf)
        fvF1 = fence_here((pr_qg.sE, pr_qg.n), (pr_qu.sE, pr_qu.n))
        fprctx.close()

        xnf = [mkbuf(f1ctx, f"f1x{i}", [128, D]) for i in range(2)]
        q16b = mkbuf(f1ctx, "f1q16", [128, D], F16)
        qxb = [mkbuf(f1ctx, f"f1qx{i}", [128, D], BF16) for i in range(4)]
        qxT = mkbuf(f1ctx, "f1qxT", [128, 8 * TT], BF16)
        grb = [mkbuf(f1ctx, f"f1gr{i}", [128, TT]) for i in range(2)]
        slb = [mkbuf(f1ctx, f"f1sl{i}", [128, TT]) for i in range(2)]
        hb = [mkbuf(f1ctx, f"f1h{i}", [128, TT]) for i in range(2)]
        fst = mkbuf(f1ctx, "f1st", [128, 1])

        dsXF = DStream("f1x")
        sQ = mksem("f1q")
        sPF = mksem("f1p")
        sEF = mksem("f1e")
        sSL = mksem("f1sl")
        sHE = mksem("f1he")
        dsHS = DStream("f1hs")

        def f1_quant(v, t):
            if t > 0:
                v.wait_ge(sPF.h, 30 * (t - 1) + 8)
            for st in range(4):
                g = 4 * t + st
                dsXF.wait(v, g)
                v.tensor_scalar(q16b[:], xnf[g % 2][:], invsx[:, 0:1], MAGIC,
                                ALU.mult, ALU.add)
                v.drain()
                v.tensor_scalar(q16b[:], q16b[:], MAGIC - 128.0, MAGIC + 127.0,
                                ALU.max, ALU.min)
                v.drain()
                v.tensor_scalar(qxb[st][:], q16b[:], -MAGIC, None,
                                ALU.add).then_inc(sQ.h, 1)
                v.drain()

        def f1_tr(te, t):
            te.wait_ge(sQ.h, 4 * (t + 1))
            for dk in range(8):
                b = 8 * t + dk
                if b >= 2:
                    te.wait_ge(sEF.h, b - 1)
                bank = pb[6 + (b % 2)]
                for st in range(4):
                    inst = te.matmul(bank[:, st * 128:(st + 1) * 128],
                                     qxb[st][:, dk * 128:(dk + 1) * 128],
                                     idbf[:], start=(st == 0), stop=(st == 3))
                inst.then_inc(sPF.h, 1)

        def f1_trev(sc, t):
            for dk in range(8):
                b = 8 * t + dk
                sc.wait_ge(sPF.h, 30 * t + dk + 1)
                bank = pb[6 + (b % 2)]
                sc.copy(qxT[:, dk * 512:(dk + 1) * 512], bank[:]).then_inc(sEF.h, 1)
                sc.drain()

        def f1_mm(te, t):
            te.wait_ge(sEF.h, 8 * (t + 1))
            for i in range(22):
                gidx = 22 * t + i
                if gidx >= 2:
                    te.wait_ge(sHE.h, 2 * gidx - 2)
                psG = pb[2 * (i % 2)]
                psU = pb[2 * (i % 2) + 1]
                for wT, ps in ((qwgT, psG), (qwuT, psU)):
                    for k in range(8):
                        col = (k * 22 + i) * 128
                        inst = te.matmul(ps[:], wT[:, col:col + 128],
                                         qxT[:, k * 512:(k + 1) * 512],
                                         start=(k == 0), stop=(k == 7))
                inst.then_inc(sPF.h, 1)

        def f1_sil(sc, t):
            for i in range(22):
                gidx = 22 * t + i
                sc.wait_ge(sHE.h, 2 * gidx + 1)
                if gidx >= 2:
                    sc.wait_ge(sHE.h, 2 * gidx - 2)
                sc.activation(slb[i % 2][:], grb[i % 2][:], AF.Silu).then_inc(sSL.h, 1)
                sc.drain()

        def f1_hep(v, t):
            for i in range(22):
                gidx = 22 * t + i
                v.wait_ge(sPF.h, 30 * t + 8 + i + 1)
                if gidx >= 2:
                    v.wait_ge(sSL.h, gidx - 1)
                    dsHS.wait(v, gidx - 2)
                psG = pb[2 * (i % 2)]
                psU = pb[2 * (i % 2) + 1]
                v.tensor_scalar(grb[i % 2][:], psG[:], swg[:, i:i + 1], None,
                                ALU.mult).then_inc(sHE.h, 1)
                v.drain()
                v.wait_ge(sSL.h, gidx + 1)
                v.scalar_tensor_tensor(hb[i % 2][:], psU[:], swu[:, i:i + 1],
                                       slb[i % 2][:], ALU.mult, ALU.mult)
                v.drain()
                v.reduce_max(fst[:, 0:1], hb[i % 2][:], axis=AX.X,
                             apply_absolute_value=True)
                v.drain()
                v.tensor_tensor(mx_h[:], mx_h[:], fst[:, 0:1],
                                ALU.max).then_inc(sHE.h, 1)
                v.drain()

        @block.sync
        def _(sync):
            wf(sync, fvF1)
            for g in range(NST):
                if g >= 2:
                    sync.wait_ge(sQ.h, g - 1)
                dsXF.dma(sync, xnf[g % 2][:], xn_h[g * 128:(g + 1) * 128, :])

        @block.gpsimd
        def _(gp):
            wf(gp, fvF1)
            for t in range(NTT):
                for i in range(22):
                    gidx = 22 * t + i
                    gp.wait_ge(sHE.h, 2 * gidx + 2)
                    dsHS.dma(gp, hp_h[i * 128:(i + 1) * 128, t * TT:(t + 1) * TT],
                             hb[i % 2][:])

        @block.vector
        def _(v):
            wf(v, fvF1)
            f1_quant(v, 0)

        for t in range(NTT):
            @block.tensor
            def _(te, t=t):
                if t == 0:
                    wf(te, fvF1)
                f1_tr(te, t)
                f1_mm(te, t)

            @block.scalar
            def _(sc, t=t):
                if t == 0:
                    wf(sc, fvF1)
                    f1_trev(sc, 0)
                f1_sil(sc, t)
                if t + 1 < NTT:
                    f1_trev(sc, t + 1)

            @block.vector
            def _(v, t=t):
                if t + 1 < NTT:
                    f1_quant(v, t + 1)
                f1_hep(v, t)

        sQ.v = 4 * NTT
        sPF.v = 30 * NTT
        sEF.v = 8 * NTT
        sSL.v = 22 * NTT
        sHE.v = 2 * 22 * NTT

        fv_dp = fence_here(*dsHS.finals(), (sHE, sHE.v))
        f1ctx.close()

        swdrep = mkbuf(fctx, "swdrep", [128, D])
        dpctx = fctx.enter_context(ExitStack())
        pr_qd = prep_weight(dpctx, "pqd", W["w_down"], 8, IF, [qdT], fv_dp,
                            mode="quant", layout="k", sw_dst=swd8, ident=idbf)

        s_a2d = mksem("ar2d")
        s_a2c = mksem("ar2c")
        s_cc2 = mksem("cc2")

        @block.gpsimd
        def _(gp):
            gp.wait_ge(fence.h, fv_dp)
            gp.dma_start(out=scr_h[256:384][:, None], in_=mx_h[:],
                         allow_slow_non_contiguous=True).then_inc(s_a2d.h, 16)
            gp.wait_ge(s_a2d.h, 16)
            gp.dma_start(out=row128[:], in_=scr_h[256:384][None, :]).then_inc(s_a2d.h, 16)

        @block.vector
        def _(v):
            v.wait_ge(s_a2d.h, 32)
            v.reduce_max(sc4[0:1, 0:1], row128[:], axis=AX.X)
            v.drain()
            v.tensor_copy(row8[:], sc4[0:1, 0:1].broadcast_to([1, 8])).then_inc(s_a2c.h, 1)
            v.drain()

        @block.gpsimd
        def _(gp):
            gp.wait_ge(s_a2c.h, 1)
            gp.dma_start(out=cc_i[1][:], in_=row8[0:1, :]).then_inc(s_a2d.h, 16)
            gp.wait_ge(s_a2d.h, 48)
            gp.collective_compute("AllReduce", ALU.max,
                                  replica_groups=[list(range(NCORES))],
                                  ins=[cc_i[1][:]], outs=[cc_o[1][:]]).then_inc(s_cc2.h, 1)
            gp.wait_ge(s_cc2.h, 1)
            gp.dma_start(out=scal[0:1, 8:9],
                         in_=cc_o[1][0:1][None, :]).then_inc(s_a2d.h, 16)

        @block.scalar
        def _(sc):
            sc.wait_ge(s_a2d.h, 64)
            sc.activation(scal[0:1, 9:10], scal[0:1, 1:2],
                          AF.Sigmoid).then_inc(s_a2c.h, 1)
            sc.drain()

        @block.vector
        def _(v):
            v.wait_ge(s_a2c.h, 2)
            v.tensor_tensor(scal[0:1, 10:11], scal[0:1, 9:10], scal[0:1, 8:9], ALU.mult)
            v.drain()
            v.tensor_scalar(scal[0:1, 10:11], scal[0:1, 10:11], QI, None, ALU.mult)
            v.drain()
            v.reciprocal(scal[0:1, 11:12], scal[0:1, 10:11]).then_inc(s_a2c.h, 1)
            v.drain()

        @block.gpsimd
        def _(gp):
            gp.wait_ge(s_a2c.h, 3)
            gp.dma_start(out=scr_h[210:212][None, :],
                         in_=scal[0:1, 10:12]).then_inc(s_a2d.h, 16)
            gp.wait_ge(s_a2d.h, 80)
            gp.dma_start(out=sdrep[:], in_=scr_h[210:211][None, :]
                         .broadcast_to([128, 1])).then_inc(s_a2d.h, 16)
            gp.dma_start(out=invsd[:], in_=scr_h[211:212][None, :]
                         .broadcast_to([128, 1])).then_inc(s_a2d.h, 16)
            gp.wait_ge(s_a2d.h, 112)
            gp.wait_ge(pr_qd.sE.h, pr_qd.n)
            gp.dma_start(out=scr_h[1024:2048].rearrange("(c p) -> p c", p=128),
                         in_=swd8[:],
                         allow_slow_non_contiguous=True).then_inc(s_a2d.h, 16)
            gp.wait_ge(s_a2d.h, 128)
            gp.dma_start(out=swdrep[:], in_=scr_h[1024:2048][None, :]
                         .broadcast_to([128, D])).then_inc(s_a2d.h, 16)

        @block.vector
        def _(v):
            v.wait_ge(s_a2d.h, 144)
            v.tensor_scalar(swdrep[:], swdrep[:], sdrep[:, 0:1], None,
                            ALU.mult).then_inc(s_a2c.h, 1)
            v.drain()

        fvF2 = fence_here((s_a2d, 144), (s_a2c, 4))
        dpctx.close()

        hsl = [mkbuf(fctx, f"f2h{i}", [128, IF]) for i in range(2)]
        hq16 = mkbuf(fctx, "f2q16", [128, IF], F16)
        qhb = [mkbuf(fctx, f"f2qh{i}", [128, IF], BF16) for i in range(2)]
        accf = [mkbuf(fctx, f"f2a{i}", [128, D]) for i in range(2)]
        outb = [mkbuf(fctx, f"f2o{i}", [128, D]) for i in range(2)]

        dsHF = DStream("f2h")
        dsAF = DStream("f2a")
        sQF = mksem("f2q")
        sPD = mksem("f2p")
        sED = mksem("f2e")
        dsOUT = DStream("f2out")

        @block.sync
        def _(sync):
            wf(sync, fvF2)
            for m in range(32):
                if m >= 2:
                    sync.wait_ge(sQF.h, m - 1)
                dsHF.dma(sync,
                         hsl[m % 2][:].rearrange("p (k n) -> p k n", k=22),
                         hp_h.rearrange("(k p) n -> p k n", p=128)
                         [:, :, m * 128:(m + 1) * 128])

        @block.scalar
        def _(sc):
            wf(sc, fvF2)
            for m in range(32):
                if m >= 2:
                    sc.wait_ge(sED.h, m - 1)
                dsAF.dma(sc, accf[m % 2][:], acc_h[m * 128:(m + 1) * 128, :])

        def f2_ep(v, mm):
            v.wait_ge(sPD.h, mm + 1)
            dsAF.wait(v, mm)
            if mm >= 2:
                dsOUT.wait(v, mm - 2)
            psA = pb[2 * (mm % 2)]
            psB = pb[2 * (mm % 2) + 1]
            v.tensor_tensor(outb[mm % 2][:, 0:512], psA[:], swdrep[:, 0:512], ALU.mult)
            v.drain()
            v.tensor_tensor(outb[mm % 2][:, 512:1024], psB[:], swdrep[:, 512:1024],
                            ALU.mult)
            v.drain()
            v.tensor_add(outb[mm % 2][:], outb[mm % 2][:],
                         accf[mm % 2][:]).then_inc(sED.h, 1)
            v.drain()

        @block.gpsimd
        def _(gp):
            wf(gp, fvF2)
            for m in range(32):
                gp.wait_ge(sED.h, m + 1)
                dsOUT.dma(gp, out_h[m * 128:(m + 1) * 128, :], outb[m % 2][:])

        @block.vector
        def _(v):
            wf(v, fvF2)
            for m in range(32):
                dsHF.wait(v, m)
                if m >= 2:
                    v.wait_ge(sPD.h, m - 1)
                v.tensor_scalar(hq16[:], hsl[m % 2][:], invsd[:, 0:1], MAGIC,
                                ALU.mult, ALU.add)
                v.drain()
                v.tensor_scalar(hq16[:], hq16[:], MAGIC - 128.0, MAGIC + 127.0,
                                ALU.max, ALU.min)
                v.drain()
                v.tensor_scalar(qhb[m % 2][:], hq16[:], -MAGIC, None,
                                ALU.add).then_inc(sQF.h, 1)
                v.drain()
                if m >= 1:
                    f2_ep(v, m - 1)
            f2_ep(v, 31)
            sQF.v = 32
            sED.v = 32

        @block.tensor
        def _(te):
            wf(te, fvF2)
            for m in range(32):
                te.wait_ge(sQF.h, m + 1)
                if m >= 2:
                    te.wait_ge(sED.h, m - 1)
                psA = pb[2 * (m % 2)]
                psB = pb[2 * (m % 2) + 1]
                for k in range(22):
                    qh = qhb[m % 2][:, k * 128:(k + 1) * 128]
                    te.matmul(psA[:], qh, qdT[:, (k * 8) * 128:(k * 8) * 128 + 512],
                              start=(k == 0), stop=(k == 21))
                    inst = te.matmul(psB[:], qh,
                                     qdT[:, (k * 8 + 4) * 128:(k * 8 + 4) * 128 + 512],
                                     start=(k == 0), stop=(k == 21))
                inst.then_inc(sPD.h, 1)
            sPD.v = 32

        @block.sync
        def _(sync):
            for sm, v in dsOUT.finals():
                sync.wait_ge(sm.h, v)

        fctx.close()
    return nc


_NC_CACHE = {}


def kernel(L0_attn_x, L0_attn_norm_w, L0_attn_in_w, L0_attn_in_b, L0_attn_gate_w,
           L0_attn_gate_b, L0_attn_b_w, L0_attn_b_b, L0_attn_c_w, L0_attn_c_b,
           L0_attn_d_w, L0_attn_d_b, L0_attn_out_w, L0_attn_out_b, L0_attn_a_log,
           L0_ffn_clip_x, L0_ffn_clip_down, L0_ffn_w_gate, L0_ffn_w_up,
           L0_ffn_w_down):
    if "nc" not in _NC_CACHE:
        _NC_CACHE["nc"] = build_nc()
    nc = _NC_CACHE["nc"]
    x = np.ascontiguousarray(np.asarray(L0_attn_x, dtype=np.float32))
    common = {
        "identb": np.eye(128, dtype=ml_dtypes.bfloat16),
        "identf": np.eye(128, dtype=np.float32),
        "norm_w": np.ascontiguousarray(L0_attn_norm_w, dtype=np.float32),
        "in_w": np.ascontiguousarray(L0_attn_in_w, dtype=np.float32),
        "in_b": np.ascontiguousarray(L0_attn_in_b, dtype=np.float32),
        "gate_w": np.ascontiguousarray(L0_attn_gate_w, dtype=np.float32),
        "gate_b": np.ascontiguousarray(L0_attn_gate_b, dtype=np.float32),
        "b_w": np.ascontiguousarray(L0_attn_b_w, dtype=np.float32),
        "b_b": np.ascontiguousarray(L0_attn_b_b, dtype=np.float32),
        "c_w": np.ascontiguousarray(L0_attn_c_w, dtype=np.float32),
        "c_b": np.ascontiguousarray(L0_attn_c_b, dtype=np.float32),
        "d_w": np.ascontiguousarray(L0_attn_d_w, dtype=np.float32),
        "d_b": np.ascontiguousarray(L0_attn_d_b, dtype=np.float32),
        "out_w": np.ascontiguousarray(L0_attn_out_w, dtype=np.float32),
        "out_b": np.ascontiguousarray(L0_attn_out_b, dtype=np.float32),
        "a_log": np.ascontiguousarray(L0_attn_a_log, dtype=np.float32),
        "clip_x": np.ascontiguousarray(L0_ffn_clip_x, dtype=np.float32),
        "clip_down": np.ascontiguousarray(L0_ffn_clip_down, dtype=np.float32),
        "w_gate": np.ascontiguousarray(L0_ffn_w_gate, dtype=np.float32),
        "w_up": np.ascontiguousarray(L0_ffn_w_up, dtype=np.float32),
        "w_down": np.ascontiguousarray(L0_ffn_w_down, dtype=np.float32),
    }
    in_maps = []
    for c in range(NCORES):
        m = dict(common)
        m["x"] = np.ascontiguousarray(x[2 * c:2 * c + 2].reshape(T, D))
        in_maps.append(m)
    res = run_bass_kernel_spmd(nc, in_maps, core_ids=list(range(NCORES)))
    out = np.concatenate(
        [res.results[c]["out"].reshape(2, 2048, D) for c in range(NCORES)], axis=0)
    return np.ascontiguousarray(out.astype(np.float32))


# version 07
# speedup vs baseline: 1.0899x; 1.0034x over previous
import numpy as np
import ml_dtypes
import concourse.bass as bass
import concourse.mybir as mybir
from concourse.bass_utils import run_bass_kernel_spmd
from contextlib import ExitStack
from types import SimpleNamespace

F32 = mybir.dt.float32
F16 = mybir.dt.float16
BF16 = mybir.dt.bfloat16
AF = mybir.ActivationFunctionType
ALU = mybir.AluOpType
AX = mybir.AxisListType

NCORES = 8
D = 1024
I2 = 2048
IF = 2816
T = 4096
TT = 512
NTT = 8
NST = 32
MAGIC = 1536.0
R1024 = 1.0 / 1024.0
QI = 1.0 / 127.0


def build_nc():
    nc = bass.Bass("TRN2", target_bir_lowering=False, debug=False)

    x_h = nc.dram_tensor("x", [T, D], F32, kind="ExternalInput").ap()
    identb_h = nc.dram_tensor("identb", [128, 128], BF16, kind="ExternalInput").ap()
    identf_h = nc.dram_tensor("identf", [128, 128], F32, kind="ExternalInput").ap()
    W = {}
    for nm, shape in [
        ("norm_w", [D]), ("in_w", [I2, D]), ("in_b", [I2]),
        ("gate_w", [I2, D]), ("gate_b", [I2]),
        ("b_w", [I2, I2]), ("b_b", [I2]), ("c_w", [I2, I2]), ("c_b", [I2]),
        ("d_w", [I2, I2]), ("d_b", [I2]),
        ("out_w", [D, I2]), ("out_b", [D]), ("a_log", [I2]),
        ("clip_x", [1]), ("clip_down", [1]),
        ("w_gate", [IF, D]), ("w_up", [IF, D]), ("w_down", [D, IF]),
    ]:
        W[nm] = nc.dram_tensor(nm, shape, F32, kind="ExternalInput").ap()
    out_h = nc.dram_tensor("out", [T, D], F32, kind="ExternalOutput").ap()

    xn_h = nc.dram_tensor("xn_i", [T, D], F32, kind="Internal").ap()
    uhi_h = nc.dram_tensor("uhi_i", [I2, T], BF16, kind="Internal").ap()
    ulo_h = nc.dram_tensor("ulo_i", [I2, T], BF16, kind="Internal").ap()
    c_hb = nc.dram_tensor("c_i", [I2, T], F32, kind="Internal").ap()
    d_hb = nc.dram_tensor("d_i", [I2, T], F32, kind="Internal").ap()
    yph_h = nc.dram_tensor("yph_i", [I2, T], BF16, kind="Internal").ap()
    ypl_h = nc.dram_tensor("ypl_i", [I2, T], BF16, kind="Internal").ap()
    hp_h = nc.dram_tensor("hp_i", [IF, T], F32, kind="Internal").ap()
    acc_h = nc.dram_tensor("acc_i", [T, D], F32, kind="Internal").ap()
    scr_h = nc.dram_tensor("scr_i", [4096], F32, kind="Internal").ap()
    cc_i = [nc.dram_tensor(f"cci{k}", [8], F32, kind="Internal").ap() for k in range(2)]
    cc_o = [nc.dram_tensor(f"cco{k}", [8], F32, kind="Internal").ap() for k in range(2)]

    gctx = ExitStack()
    with gctx:
        def mkbuf(ctx, name, shape, dt=F32):
            return ctx.enter_context(nc.sbuf_tensor(name, shape, dt))

        _semno = [0]

        def mksem(tag):
            _semno[0] += 1
            return SimpleNamespace(
                h=gctx.enter_context(nc.semaphore(f"s{_semno[0]}_{tag}")), v=0)

        _shared = {}

        def shsem(tag):
            if tag not in _shared:
                _shared[tag] = mksem(tag)
            return _shared[tag]

        class DStream:

            def __init__(self, tag):
                self.s = [mksem(tag + "0"), mksem(tag + "1")]
                self.chk = []

            def dma(self, eng, out, in_, **kw):
                sm = self.s[len(self.chk) % 2]
                if sm.v >= 16:
                    eng.wait_ge(sm.h, sm.v)
                eng.dma_start(out=out, in_=in_, **kw).then_inc(sm.h, 16)
                sm.v += 16
                self.chk.append((sm, sm.v))
                return len(self.chk) - 1

            def wait(self, eng, idx):
                sm, v = self.chk[idx]
                eng.wait_ge(sm.h, v)

            def wait_upto(self, eng, idx):
                assert idx < len(self.chk), (idx, len(self.chk))
                best = {}
                for sm, v in self.chk[:idx + 1]:
                    best[id(sm)] = (sm, v)
                for sm, v in best.values():
                    eng.wait_ge(sm.h, v)

            def finals(self):
                best = {}
                for sm, v in self.chk:
                    best[id(sm)] = (sm, v)
                return list(best.values())

        identf = mkbuf(gctx, "identf_s", [128, 128], F32)
        bias_bcd = mkbuf(gctx, "bias_bcd", [128, 48])
        a_sb = mkbuf(gctx, "a_sb", [128, 16])
        _smalls = mkbuf(gctx, "smalls", [128, 80])
        swg = _smalls[:, 0:22]
        swu = _smalls[:, 22:44]
        swd8 = _smalls[:, 44:52]
        state = _smalls[:, 52:68]
        invsx = _smalls[:, 68:69]
        invsd = _smalls[:, 69:70]
        sxrep = _smalls[:, 70:71]
        sdrep = _smalls[:, 71:72]
        mx_xn = _smalls[:, 72:73]
        mx_h = _smalls[:, 73:74]
        _rows = mkbuf(gctx, "rowsml", [1, 160])
        scal = _rows[:, 0:16]
        row128 = _rows[:, 16:144]
        row8 = _rows[:, 144:152]
        sc4 = _rows[:, 152:156]
        a1ctx = gctx.enter_context(ExitStack())
        bias_ig = mkbuf(a1ctx, "bias_ig", [128, 32])
        identb = mkbuf(a1ctx, "identb_s", [128, 128], BF16)

        pb = [gctx.enter_context(nc.psum_tensor(f"pb{i}", [128, 512], F32))
              for i in range(8)]
        block = gctx.enter_context(nc.Block())
        fence = mksem("fence")

        def fence_here(*wait_list):
            fence.v += 1
            fv = fence.v

            @block.gpsimd
            def _(g):
                for s, v in wait_list:
                    g.wait_ge(s.h, v)
                g.sem_inc(fence.h, 1)
            return fv

        def wf(eng, fv):
            eng.wait_ge(fence.h, fv)

        ds_const = DStream("const")
        s_p0a = mksem("p0a")
        s_p0v = mksem("p0v")

        @block.sync
        def _(sync):
            for dst, src in [
                (identb[:], identb_h[:]),
                (identf[:], identf_h[:]),
                (bias_ig[:, 0:16], W["in_b"].rearrange("(c p) -> p c", p=128)),
                (bias_ig[:, 16:32], W["gate_b"].rearrange("(c p) -> p c", p=128)),
                (bias_bcd[:, 0:16], W["b_b"].rearrange("(c p) -> p c", p=128)),
                (bias_bcd[:, 16:32], W["c_b"].rearrange("(c p) -> p c", p=128)),
                (bias_bcd[:, 32:48], W["d_b"].rearrange("(c p) -> p c", p=128)),
                (a_sb[:], W["a_log"].rearrange("(c p) -> p c", p=128)),
                (scal[0:1, 0:1], W["clip_x"][None, :]),
                (scal[0:1, 1:2], W["clip_down"][None, :]),
            ]:
                ds_const.dma(sync, dst, src, allow_slow_non_contiguous=True)

        @block.scalar
        def _(sc):
            for sm, v in ds_const.finals():
                sc.wait_ge(sm.h, v)
            sc.activation(a_sb[:], a_sb[:], AF.Sigmoid, scale=-1.0).then_inc(s_p0a.h, 1)
        s_p0a.v = 1

        @block.vector
        def _(v):
            v.memset(mx_xn[:], 0.0)
            v.drain()
            v.memset(mx_h[:], 0.0).then_inc(s_p0v.h, 1)
        s_p0v.v = 1
        fv0 = fence_here(*ds_const.finals(), (s_p0a, 1), (s_p0v, 1))

        def prep_weight(ctx, tag, src_ap, n_oc, Kdim, dsts, fv, mode="hilo",
                        layout="oc", fold_rep=None, row_off=0, sw_dst=None,
                        sw_mul_ap=None, extra_dma=(), scratch=None,
                        load_engine="sync", pe_guard=None, ident=None):
            nk = Kdim // 128
            grp = "B" if tag in ("pga", "pqu") else "A"
            if "prL" + grp not in _shared:
                _shared["prL" + grp] = DStream("prL" + grp)
            dsL = _shared["prL" + grp]
            sS = shsem("prS" + grp)
            sP = shsem("prP" + grp)
            sE = shsem("prE" + grp)
            bS, bP, bE = sS.v, sP.v, sE.v
            pb_base = 6 if grp == "A" else 4
            ld_idx = {}
            nsrc = 2 if mode == "hilo" else 1
            if scratch is None:
                wsl = [mkbuf(ctx, f"{tag}_w{i}", [128, Kdim]) for i in range(2)]
                bsl = [[mkbuf(ctx, f"{tag}_b{j}{i}", [128, Kdim], BF16)
                        for i in range(2)] for j in range(nsrc)]
                q16 = (mkbuf(ctx, f"{tag}_q16", [128, Kdim], F16)
                       if mode == "quant" else None)
                rst = mkbuf(ctx, f"{tag}_rst", [128, 2]) if mode == "quant" else None
            else:
                wsl, bsl, q16, rst = scratch
            kbs = [(k0, min(4, nk - k0)) for k0 in range(0, nk, 4)]
            nbb = len(kbs) * nsrc
            n_batches = n_oc * nbb

            def emit_loads(eng):
                if fv is not None:
                    wf(eng, fv)
                for dst, src in extra_dma:
                    ld_idx.setdefault("x", []).append(dsL.dma(eng, dst, src))
                for s in range(n_oc):
                    if s >= 2:
                        eng.wait_ge(sS.h, bS + s - 1)
                    ld_idx[s] = dsL.dma(
                        eng, wsl[s % 2][:],
                        src_ap[row_off + s * 128: row_off + (s + 1) * 128, :])

            if load_engine == "sync":
                @block.sync
                def _(sync):
                    emit_loads(sync)
            else:
                @block.scalar
                def _(sc):
                    emit_loads(sc)

            @block.vector
            def _(v):
                if fv is not None:
                    wf(v, fv)
                for s in range(n_oc):
                    dsL.wait(v, ld_idx[s])
                    if s >= 2:
                        v.wait_ge(sP.h, bP + (s - 1) * nbb)
                    elif bP > 0:
                        v.wait_ge(sP.h, bP)
                    if mode == "hilo":
                        if fold_rep is not None:
                            for xi in ld_idx.get("x", []):
                                dsL.wait(v, xi)
                            v.tensor_tensor(wsl[s % 2][:], wsl[s % 2][:], fold_rep,
                                            ALU.mult)
                            v.drain()
                        v.tensor_copy(bsl[0][s % 2][:], wsl[s % 2][:])
                        v.drain()
                        v.tensor_sub(bsl[1][s % 2][:], wsl[s % 2][:],
                                     bsl[0][s % 2][:]).then_inc(sS.h, 1)
                        v.drain()
                    else:
                        v.reduce_max(rst[:, 0:1], wsl[s % 2][:], axis=AX.X,
                                     apply_absolute_value=True)
                        v.drain()
                        v.reciprocal(rst[:, 1:2], rst[:, 0:1])
                        v.drain()
                        v.tensor_scalar(rst[:, 1:2], rst[:, 1:2], 127.0, None, ALU.mult)
                        v.drain()
                        v.tensor_scalar(q16[:], wsl[s % 2][:], rst[:, 1:2], MAGIC,
                                        ALU.mult, ALU.add)
                        v.drain()
                        v.tensor_scalar(bsl[0][s % 2][:], q16[:], -MAGIC, None, ALU.add)
                        v.drain()
                        if sw_mul_ap is not None:
                            v.tensor_scalar(sw_dst[:, s:s + 1], rst[:, 0:1],
                                            sw_mul_ap, QI, ALU.mult,
                                            ALU.mult).then_inc(sS.h, 1)
                        else:
                            v.tensor_scalar(sw_dst[:, s:s + 1], rst[:, 0:1],
                                            QI, None, ALU.mult).then_inc(sS.h, 1)
                        v.drain()
                sS.v = bS + n_oc

            @block.tensor
            def _(te):
                if fv is not None:
                    wf(te, fv)
                if pe_guard is not None:
                    te.wait_ge(pe_guard[0].h, pe_guard[1])
                for xi in ld_idx.get("x", []):
                    dsL.wait(te, xi)
                b = 0
                for s in range(n_oc):
                    te.wait_ge(sS.h, bS + s + 1)
                    for j in range(nsrc):
                        for (k0, kn) in kbs:
                            if b >= 2:
                                te.wait_ge(sE.h, bE + b - 1)
                            bank = pb[pb_base + (b % 2)]
                            idt = ident if ident is not None else identb
                            for c in range(kn):
                                inst = te.matmul(
                                    bank[:, c * 128:(c + 1) * 128],
                                    bsl[j][s % 2][:, (k0 + c) * 128:(k0 + c + 1) * 128],
                                    idt[:], start=(c == 0), stop=(c == kn - 1))
                            inst.then_inc(sP.h, 1)
                            b += 1
                sP.v = bP + n_batches

            @block.scalar
            def _(scx):
                if fv is not None:
                    wf(scx, fv)
                b = 0
                for s in range(n_oc):
                    for j in range(nsrc):
                        for (k0, kn) in kbs:
                            scx.wait_ge(sP.h, bP + b + 1)
                            bank = pb[pb_base + (b % 2)]
                            dst = dsts[j]
                            if layout == "oc":
                                col0 = (s * nk + k0) * 128
                                scx.copy(dst[:, col0:col0 + kn * 128],
                                         bank[:, 0:kn * 128]).then_inc(sE.h, 1)
                            else:
                                dv = dst.rearrange("p (k s2 c) -> p k s2 c",
                                                   k=nk, s2=n_oc)
                                scx.copy(dv[:, k0:k0 + kn, s, :],
                                         bank[:, 0:kn * 128].rearrange(
                                             "p (a c) -> p a c", a=kn)).then_inc(sE.h, 1)
                            scx.drain()
                            b += 1
                sE.v = bE + n_batches
            return SimpleNamespace(sP=sP, sE=sE, n=bE + n_batches)

        inTh = mkbuf(a1ctx, "inTh", [128, 16 * 8 * 128], BF16)
        inTl = mkbuf(a1ctx, "inTl", [128, 16 * 8 * 128], BF16)
        gaTh = mkbuf(a1ctx, "gaTh", [128, 16 * 8 * 128], BF16)
        gaTl = mkbuf(a1ctx, "gaTl", [128, 16 * 8 * 128], BF16)

        prctx = a1ctx.enter_context(ExitStack())
        nwrep = mkbuf(prctx, "nwrep", [128, D])
        pr_in = prep_weight(prctx, "pin", W["in_w"], 16, D, [inTh, inTl], fv0,
                            fold_rep=nwrep[:],
                            extra_dma=[(nwrep[:],
                                        W["norm_w"][None, :].broadcast_to([128, D]))])
        pr_ga = prep_weight(prctx, "pga", W["gate_w"], 16, D, [gaTh, gaTl], fv0,
                            fold_rep=nwrep[:])
        fvA1 = fence_here((pr_in.sE, pr_in.n), (pr_ga.sE, pr_ga.n))
        prctx.close()

        xb = [mkbuf(a1ctx, f"a1x{i}", [128, D]) for i in range(2)]
        xcb = [mkbuf(a1ctx, f"a1xc{i}", [128, D]) for i in range(2)]
        x2h = [mkbuf(a1ctx, f"a1h{i}", [128, D], BF16) for i in range(4)]
        x2l = [mkbuf(a1ctx, f"a1l{i}", [128, D], BF16) for i in range(4)]
        xTh = [mkbuf(a1ctx, f"a1xTh{i}", [128, 8 * 512], BF16) for i in range(2)]
        xTl = [mkbuf(a1ctx, f"a1xTl{i}", [128, 8 * 512], BF16) for i in range(2)]
        st1 = mkbuf(a1ctx, "a1st1", [128, 4])
        st2 = mkbuf(a1ctx, "a1st2", [128, 2])
        sq2 = mkbuf(a1ctx, "a1sq2", [128, 2])
        rsy = mkbuf(a1ctx, "a1rsy", [128, 2])
        rtmp = mkbuf(a1ctx, "a1rt", [128, 2])
        ubf = [mkbuf(a1ctx, f"a1u{i}", [128, TT]) for i in range(2)]
        gbf = [mkbuf(a1ctx, f"a1g{i}", [128, TT]) for i in range(2)]
        uhb = [mkbuf(a1ctx, f"a1uh{i}", [128, TT], BF16) for i in range(2)]
        ulb = [mkbuf(a1ctx, f"a1ul{i}", [128, TT], BF16) for i in range(2)]

        dsX = DStream("a1x")
        sDV = mksem("a1dv")
        sAC = mksem("a1ac")
        sPE = mksem("a1pe")
        sPT = mksem("a1pt")
        sEV = mksem("a1ev")
        sSG = mksem("a1sg")
        sUE = mksem("a1ue")
        dsXS = DStream("a1xs")
        dsUS = DStream("a1us")

        def a1_sub(v, t):
            if t > 0:
                v.wait_ge(sPT.h, 16 * (t - 1) + 16)
            for st in range(4):
                g = 4 * t + st
                dsX.wait(v, g)
                if g >= 2:
                    dsXS.wait(v, g - 2)
                v.reduce_sum(st1[:, 0:1], xb[g % 2][:], axis=AX.X)
                v.drain()
                v.tensor_scalar(st1[:, 1:2], st1[:, 0:1], R1024, None, ALU.mult)
                v.drain()
                v.tensor_scalar(xcb[g % 2][:], xb[g % 2][:], st1[:, 1:2], None,
                                ALU.subtract).then_inc(sDV.h, 1)
                v.drain()
                v.wait_ge(sAC.h, 2 * g + 1)
                v.tensor_scalar(st2[:, 0:1], st1[:, 2:3], R1024, 1e-5, ALU.mult,
                                ALU.add)
                v.drain()
                v.tensor_scalar(st1[:, 3:4], st1[:, 2:3], R1024, None, ALU.mult)
                v.drain()
                v.reciprocal(rtmp[:, 0:1], st2[:, 0:1])
                v.drain()
                v.tensor_tensor(st2[:, 1:2], st1[:, 3:4], rtmp[:, 0:1], ALU.mult)
                v.drain()
                v.tensor_scalar(st2[:, 1:2], st2[:, 1:2], 1e-6, None,
                                ALU.add).then_inc(sDV.h, 1)
                v.drain()
                v.wait_ge(sAC.h, 2 * g + 2)
                v.reciprocal(rsy[:], sq2[:])
                v.drain()
                v.tensor_tensor(rtmp[:], rsy[:], rsy[:], ALU.mult)
                v.drain()
                v.tensor_tensor(rtmp[:], rtmp[:], st2[:], ALU.mult)
                v.drain()
                v.tensor_scalar(rtmp[:], rtmp[:], -0.5, 1.5, ALU.mult, ALU.add)
                v.drain()
                v.tensor_tensor(rsy[:], rsy[:], rtmp[:], ALU.mult)
                v.drain()
                v.tensor_scalar(xcb[g % 2][:], xcb[g % 2][:], rsy[:, 0:1], None,
                                ALU.mult)
                v.drain()
                v.reduce_max(st1[:, 0:1], xcb[g % 2][:], axis=AX.X,
                             apply_absolute_value=True)
                v.drain()
                v.tensor_tensor(mx_xn[:], mx_xn[:], st1[:, 0:1], ALU.max)
                v.drain()
                v.tensor_scalar(x2h[st][:], xcb[g % 2][:], rsy[:, 1:2], None, ALU.mult)
                v.drain()
                v.scalar_tensor_tensor(x2l[st][:], xcb[g % 2][:], rsy[:, 1:2],
                                       x2h[st][:], ALU.mult,
                                       ALU.subtract).then_inc(sDV.h, 1)
                v.drain()

        def a1_acts(sc, t):
            for st in range(4):
                g = 4 * t + st
                sc.wait_ge(sDV.h, 3 * g + 1)
                sc.activation(xb[g % 2][:], xcb[g % 2][:], AF.Square,
                              accum_out=st1[:, 2:3]).then_inc(sAC.h, 1)
                sc.drain()
                sc.wait_ge(sDV.h, 3 * g + 2)
                sc.activation(sq2[:], st2[:], AF.Sqrt).then_inc(sAC.h, 1)
                sc.drain()

        def a1_tr(te, t, bs):
            for bl in bs:
                dk, j = bl // 2, bl % 2
                src = x2h if j == 0 else x2l
                b = 16 * t + bl
                if bl == 0:
                    te.wait_ge(sDV.h, 3 * (4 * t + 3) + 3)
                if b >= 2:
                    te.wait_ge(sEV.h, b - 1)
                bank = pb[6 + (b % 2)]
                for st in range(4):
                    inst = te.matmul(bank[:, st * 128:(st + 1) * 128],
                                     src[st][:, dk * 128:(dk + 1) * 128],
                                     identb[:], start=(st == 0), stop=(st == 3))
                inst.then_inc(sPT.h, 1)

        def a1_trev(sc, t, bs):
            for bl in bs:
                dk, j = bl // 2, bl % 2
                dst = xTh[t % 2] if j == 0 else xTl[t % 2]
                b = 16 * t + bl
                sc.wait_ge(sPT.h, b + 1)
                bank = pb[6 + (b % 2)]
                sc.copy(dst[:, dk * 512:(dk + 1) * 512], bank[:]).then_inc(sEV.h, 1)
                sc.drain()

        def a1_mm(te, t):
            te.wait_ge(sEV.h, 16 * (t + 1))
            for i in range(16):
                gidx = 16 * t + i
                if gidx >= 2:
                    te.wait_ge(sUE.h, gidx - 1)
                psI = pb[2 * (i % 2)]
                psG = pb[2 * (i % 2) + 1]
                for wh, wl, ps in ((inTh, inTl, psI), (gaTh, gaTl, psG)):
                    for k in range(8):
                        col = (i * 8 + k) * 128
                        te.matmul(ps[:], wh[:, col:col + 128],
                                  xTh[t % 2][:, k * 512:(k + 1) * 512],
                                  start=(k == 0), stop=False)
                        te.matmul(ps[:], wh[:, col:col + 128],
                                  xTl[t % 2][:, k * 512:(k + 1) * 512],
                                  start=False, stop=False)
                        inst = te.matmul(ps[:], wl[:, col:col + 128],
                                         xTh[t % 2][:, k * 512:(k + 1) * 512],
                                         start=False, stop=(k == 7))
                inst.then_inc(sPE.h, 1)
                if t + 1 < NTT and i >= 8:
                    a1_tr(te, t + 1, [2 * (i - 8), 2 * (i - 8) + 1])

        def a1_sig(sc, t):
            for i in range(16):
                gidx = 16 * t + i
                sc.wait_ge(sPE.h, 16 * t + i + 1)
                if gidx >= 2:
                    sc.wait_ge(sUE.h, gidx - 1)
                psG = pb[2 * (i % 2) + 1]
                sc.activation(gbf[i % 2][:], psG[:], AF.Sigmoid,
                              bias=bias_ig[:, 16 + i:17 + i]).then_inc(sSG.h, 1)
                sc.drain()
                if t + 1 < NTT and i >= 8:
                    a1_trev(sc, t + 1, [2 * (i - 8), 2 * (i - 8) + 1])

        def a1_uep(v, t):
            for i in range(16):
                gidx = 16 * t + i
                v.wait_ge(sSG.h, gidx + 1)
                if gidx >= 2:
                    dsUS.wait_upto(v, 2 * (gidx - 2) + 1)
                psI = pb[2 * (i % 2)]
                v.scalar_tensor_tensor(ubf[i % 2][:], psI[:], bias_ig[:, i:i + 1],
                                       gbf[i % 2][:], ALU.add, ALU.mult)
                v.drain()
                v.tensor_copy(uhb[i % 2][:], ubf[i % 2][:])
                v.drain()
                v.tensor_sub(ulb[i % 2][:], ubf[i % 2][:],
                             uhb[i % 2][:]).then_inc(sUE.h, 1)
                v.drain()

        @block.sync
        def _(sync):
            wf(sync, fvA1)
            for g in range(NST):
                if g >= 2:
                    sync.wait_ge(sAC.h, 2 * (g - 2) + 1)
                dsX.dma(sync, xb[g % 2][:], x_h[g * 128:(g + 1) * 128, :])

        def a1_gp_xn(gp, t):
            for st in range(4):
                g = 4 * t + st
                gp.wait_ge(sDV.h, 3 * g + 3)
                dsXS.dma(gp, xn_h[g * 128:(g + 1) * 128, :], xcb[g % 2][:])

        def a1_gp_u(gp, t):
            for i in range(16):
                gidx = 16 * t + i
                gp.wait_ge(sUE.h, gidx + 1)
                dsUS.dma(gp, uhi_h[i * 128:(i + 1) * 128, t * TT:(t + 1) * TT],
                         uhb[i % 2][:])
                dsUS.dma(gp, ulo_h[i * 128:(i + 1) * 128, t * TT:(t + 1) * TT],
                         ulb[i % 2][:])

        @block.gpsimd
        def _(gp):
            a1_gp_xn(gp, 0)
            for t in range(NTT):
                if t + 1 < NTT:
                    a1_gp_xn(gp, t + 1)
                a1_gp_u(gp, t)

        @block.vector
        def _(v):
            wf(v, fvA1)
            a1_sub(v, 0)

        @block.scalar
        def _(sc):
            wf(sc, fvA1)
            a1_acts(sc, 0)

        for t in range(NTT):
            @block.tensor
            def _(te, t=t):
                if t == 0:
                    wf(te, fvA1)
                    a1_tr(te, 0, list(range(16)))
                a1_mm(te, t)

            @block.scalar
            def _(sc, t=t):
                if t == 0:
                    a1_trev(sc, 0, list(range(16)))
                if t + 1 < NTT:
                    a1_acts(sc, t + 1)
                a1_sig(sc, t)

            @block.vector
            def _(v, t=t):
                if t + 1 < NTT:
                    a1_sub(v, t + 1)
                a1_uep(v, t)

        sDV.v = 3 * NST
        sAC.v = 2 * NST
        sPE.v = 16 * NTT
        sPT.v = 16 * NTT
        sEV.v = 16 * NTT
        sSG.v = 16 * NTT
        sUE.v = 16 * NTT

        fvAR1 = fence_here(*dsUS.finals(), *dsXS.finals(), (sUE, sUE.v))
        a1ctx.close()

        s_ard = mksem("ard")
        s_arc = mksem("arc")
        s_cc1 = mksem("cc1")

        @block.gpsimd
        def _(gp):
            gp.wait_ge(fence.h, fvAR1)
            gp.dma_start(out=scr_h[0:128][:, None], in_=mx_xn[:],
                         allow_slow_non_contiguous=True).then_inc(s_ard.h, 16)
            gp.wait_ge(s_ard.h, 16)
            gp.dma_start(out=row128[:], in_=scr_h[0:128][None, :]).then_inc(s_ard.h, 16)

        @block.vector
        def _(v):
            v.wait_ge(s_ard.h, 32)
            v.reduce_max(sc4[0:1, 0:1], row128[:], axis=AX.X)
            v.drain()
            v.tensor_copy(row8[:], sc4[0:1, 0:1].broadcast_to([1, 8])).then_inc(s_arc.h, 1)
            v.drain()

        @block.gpsimd
        def _(gp):
            gp.wait_ge(s_arc.h, 1)
            gp.dma_start(out=cc_i[0][:], in_=row8[0:1, :]).then_inc(s_ard.h, 16)
            gp.wait_ge(s_ard.h, 48)
            gp.collective_compute("AllReduce", ALU.max,
                                  replica_groups=[list(range(NCORES))],
                                  ins=[cc_i[0][:]], outs=[cc_o[0][:]]).then_inc(s_cc1.h, 1)
            gp.wait_ge(s_cc1.h, 1)
            gp.dma_start(out=scal[0:1, 2:3],
                         in_=cc_o[0][0:1][None, :]).then_inc(s_ard.h, 16)

        @block.scalar
        def _(sc):
            sc.wait_ge(s_ard.h, 64)
            sc.activation(scal[0:1, 4:5], scal[0:1, 0:1],
                          AF.Sigmoid).then_inc(s_arc.h, 1)
            sc.drain()

        @block.vector
        def _(v):
            v.wait_ge(s_arc.h, 2)
            v.tensor_tensor(scal[0:1, 5:6], scal[0:1, 4:5], scal[0:1, 2:3], ALU.mult)
            v.drain()
            v.tensor_scalar(scal[0:1, 5:6], scal[0:1, 5:6], QI, None, ALU.mult)
            v.drain()
            v.reciprocal(scal[0:1, 6:7], scal[0:1, 5:6]).then_inc(s_arc.h, 1)
            v.drain()

        @block.gpsimd
        def _(gp):
            gp.wait_ge(s_arc.h, 3)
            gp.dma_start(out=scr_h[200:202][None, :],
                         in_=scal[0:1, 5:7]).then_inc(s_ard.h, 16)
            gp.wait_ge(s_ard.h, 80)
            gp.dma_start(out=sxrep[:], in_=scr_h[200:201][None, :]
                         .broadcast_to([128, 1])).then_inc(s_ard.h, 16)
            gp.dma_start(out=invsx[:], in_=scr_h[201:202][None, :]
                         .broadcast_to([128, 1])).then_inc(s_ard.h, 16)
        s_ard.v = 112
        s_arc.v = 3

        a2ctx = gctx.enter_context(ExitStack())
        wThs = [mkbuf(a2ctx, f"wTh{i}", [128, 16 * 8 * 128], BF16) for i in range(2)]
        wTls = [mkbuf(a2ctx, f"wTl{i}", [128, 16 * 8 * 128], BF16) for i in range(2)]
        uTh = [mkbuf(a2ctx, f"uTh{i}", [128, 16 * TT], BF16) for i in range(2)]
        uTl = [mkbuf(a2ctx, f"uTl{i}", [128, 16 * TT], BF16) for i in range(2)]
        obuf = [mkbuf(a2ctx, "a2ob", [128, TT])] * 2
        cbuf = [mkbuf(a2ctx, f"a2c{i}", [128, TT]) for i in range(2)]
        dbuf = [mkbuf(a2ctx, f"a2d{i}", [128, TT]) for i in range(2)]
        ypbh = [mkbuf(a2ctx, "a2yh0", [128, TT], BF16)]
        ypbl = [mkbuf(a2ctx, "a2yl0", [128, TT], BF16)]
        a2_wsl = mkbuf(a2ctx, "a2pw", [128, 512])

        dsU = DStream("a2u")
        dsCD = DStream("a2cd")
        dsST = DStream("a2st")
        dsPL = DStream("a2pl")
        sPP = mksem("a2pp")
        sEvH = mksem("a2eh")
        sPEv = mksem("a2pe")
        sMP = mksem("a2mp")
        sME = mksem("a2me")
        sCDu = mksem("a2cdu")

        A2SPECS = [("c", 0), ("c", 1), ("d", 0), ("d", 1), ("b", 0), ("b", 1)]
        prep_ld_idx = [dict() for _ in range(6)]

        def a2_prep_load1(eng, k, h):
            kind, half = A2SPECS[k]
            wsrc = {"c": W["c_w"], "d": W["d_w"], "b": W["b_w"]}[kind]
            bP = 32 * k
            si, q = h // 4, h % 4
            if h >= 1:
                eng.wait_ge(sPP.h, bP + h)
            prep_ld_idx[k][h] = dsPL.dma(
                eng, a2_wsl[:],
                wsrc[half * 1024 + si * 128: half * 1024 + (si + 1) * 128,
                     q * 512:(q + 1) * 512])

        def a2_prep_hi1(sc, k, h):
            bP = 32 * k
            si, q = h // 4, h % 4
            sc.wait_ge(sPP.h, bP + h + 1)
            bank = pb[6 + (h % 2)]
            col0 = (si * 16 + q * 4) * 128
            sc.copy(wThs[k % 2][:, col0:col0 + 512],
                    bank[:]).then_inc(sEvH.h, 1)
            sc.drain()

        def a2_prep_lo1(v, k, h):
            bE = 32 * k
            si, q = h // 4, h % 4
            v.wait_ge(sEvH.h, bE + h + 1)
            bank = pb[6 + (h % 2)]
            col0 = (si * 16 + q * 4) * 128
            v.tensor_sub(wTls[k % 2][:, col0:col0 + 512], bank[:],
                         wThs[k % 2][:, col0:col0 + 512]).then_inc(sPEv.h, 1)
            v.drain()

        def a2_prep_actwork(sc, k, first=False):
            if first:
                wf(sc, fvAR1)
            a2_prep_load1(sc, k, 0)
            a2_prep_load1(sc, k, 1)
            for h in range(32):
                a2_prep_hi1(sc, k, h)
                if h + 2 < 32:
                    a2_prep_load1(sc, k, h + 2)

        def a2_prep_los(v, k, hs, first=False):
            if first:
                wf(v, fvAR1)
            for h in hs:
                a2_prep_lo1(v, k, h)

        def a2_prep_trs(te, k, hs):
            bE = 32 * k
            for h in hs:
                dsPL.wait(te, prep_ld_idx[k][h])
                if bE + h >= 2:
                    te.wait_ge(sPEv.h, bE + h - 1)
                bank = pb[6 + (h % 2)]
                for c in range(4):
                    inst = te.matmul(bank[:, c * 128:(c + 1) * 128],
                                     a2_wsl[:, c * 128:(c + 1) * 128],
                                     identf[:], start=(c == 0), stop=(c == 3))
                inst.then_inc(sPP.h, 1)



        def a2_main(k):
            kind, half = A2SPECS[k]
            whbm = {"c": c_hb, "d": d_hb}.get(kind)
            bias_off = {"b": 0, "c": 16, "d": 32}[kind]
            mp0 = 64 * k
            me0 = 64 * k
            st_per = 2 if kind == "b" else 1
            st0 = len(dsST.chk)
            bUi = len(dsU.chk)
            bCDi = len(dsCD.chk)
            wTh = wThs[k % 2]
            wTl = wTls[k % 2]

            @block.sync
            def _(sync):
                if k == 0:
                    wf(sync, fvAR1)
                for t in range(NTT):
                    if t >= 2:
                        sync.wait_ge(sMP.h, mp0 + 8 * (t - 1))
                    elif mp0 - 8 * (1 - t) > 0:
                        sync.wait_ge(sMP.h, mp0 - 8 * (1 - t))
                    dsU.dma(sync,
                            uTh[t % 2][:].rearrange("p (kk n) -> p kk n", kk=16),
                            uhi_h.rearrange("(kk p) n -> p kk n", p=128)
                            [:, :, t * TT:(t + 1) * TT])
                    dsU.dma(sync,
                            uTl[t % 2][:].rearrange("p (kk n) -> p kk n", kk=16),
                            ulo_h.rearrange("(kk p) n -> p kk n", p=128)
                            [:, :, t * TT:(t + 1) * TT])
                    if kind == "b":
                        cdu0 = 128 if half == 1 else 0
                        for j in range(8):
                            idx = 8 * t + j
                            if idx >= 2:
                                sync.wait_ge(sCDu.h, cdu0 + 2 * (idx - 1))
                            elif cdu0 > 0:
                                sync.wait_ge(sCDu.h, cdu0)
                            dsCD.dma(sync, cbuf[idx % 2][:],
                                     c_hb[half * 1024 + j * 128:
                                          half * 1024 + (j + 1) * 128,
                                          t * TT:(t + 1) * TT])
                            dsCD.dma(sync, dbuf[idx % 2][:],
                                     d_hb[half * 1024 + j * 128:
                                          half * 1024 + (j + 1) * 128,
                                          t * TT:(t + 1) * TT])

            @block.tensor
            def _(te):
                te.wait_ge(sPEv.h, 32 * (k + 1))
                for t in range(NTT):
                    dsU.wait_upto(te, bUi + 2 * t + 1)
                    for j in range(8):
                        idx = 8 * t + j
                        if idx >= 3:
                            te.wait_ge(sME.h, me0 + idx - 2)
                        elif me0 > 0:
                            te.wait_ge(sME.h, me0)
                        ps = pb[idx % 3]
                        for kk in range(16):
                            col = (j * 16 + kk) * 128
                            te.matmul(ps[:], wTh[:, col:col + 128],
                                      uTh[t % 2][:, kk * TT:(kk + 1) * TT],
                                      start=(kk == 0), stop=False)
                            te.matmul(ps[:], wTh[:, col:col + 128],
                                      uTl[t % 2][:, kk * TT:(kk + 1) * TT],
                                      start=False, stop=False)
                            inst = te.matmul(ps[:], wTl[:, col:col + 128],
                                             uTh[t % 2][:, kk * TT:(kk + 1) * TT],
                                             start=False, stop=(kk == 15))
                        inst.then_inc(sMP.h, 1)
                        if k + 1 < 6 and idx % 2 == 1:
                            a2_prep_trs(te, k + 1, [idx // 2])

            @block.gpsimd
            def _(gp):
                if k == 0:
                    wf(gp, fvAR1)
                for t in range(NTT):
                    for j in range(8):
                        idx = 8 * t + j
                        gp.wait_ge(sME.h, me0 + idx + 1)
                        row = half * 1024 + j * 128
                        if kind != "b":
                            dsST.dma(gp, whbm[row:row + 128, t * TT:(t + 1) * TT],
                                     obuf[idx % 2][:])
                        else:
                            dsST.dma(gp, yph_h[row:row + 128, t * TT:(t + 1) * TT],
                                     ypbh[0][:])
                            dsST.dma(gp, ypl_h[row:row + 128, t * TT:(t + 1) * TT],
                                     ypbl[0][:])

            @block.vector
            def _(v):
                if k == 0:
                    wf(v, fvAR1)
                for t in range(NTT):
                    for j in range(8):
                        idx = 8 * t + j
                        v.wait_ge(sMP.h, mp0 + idx + 1)
                        if kind != "b" and idx >= 1:
                            dsST.wait_upto(v, st0 + st_per * idx - 1)
                        elif st0 > 0 and idx < 1:
                            dsST.wait_upto(v, st0 - 1)
                        ps = pb[idx % 3]
                        bcol = bias_off + half * 8 + j
                        inst = v.tensor_scalar(obuf[idx % 2][:], ps[:],
                                               bias_bcd[:, bcol:bcol + 1], None,
                                               ALU.add)
                        if kind != "b":
                            inst.then_inc(sME.h, 1)
                        v.drain()
                        if kind != "b" and k + 1 < 6 and idx % 2 == 1:
                            a2_prep_los(v, k + 1, [idx // 2])
                        if kind == "b":
                            dsCD.wait_upto(v, bCDi + 2 * idx + 1)
                            hsv = pb[3]
                            init = (0.0 if (t % 4 == 0)
                                    else state[:, half * 8 + j: half * 8 + j + 1])
                            v.tensor_tensor_scan(
                                hsv[:],
                                a_sb[:, half * 8 + j: half * 8 + j + 1]
                                .broadcast_to([128, TT]),
                                obuf[idx % 2][:], init, ALU.mult, ALU.add)
                            v.drain()
                            v.tensor_copy(state[:, half * 8 + j: half * 8 + j + 1],
                                          hsv[:, TT - 1: TT])
                            v.drain()
                            v.tensor_tensor(hsv[:], hsv[:], cbuf[idx % 2][:],
                                            ALU.mult).then_inc(sCDu.h, 1)
                            v.drain()
                            v.tensor_tensor(hsv[:], hsv[:], dbuf[idx % 2][:],
                                            ALU.add).then_inc(sCDu.h, 1)
                            v.drain()
                            if idx >= 1:
                                dsST.wait_upto(v, st0 + 2 * idx - 1)
                            v.tensor_copy(ypbh[0][:], hsv[:])
                            v.drain()
                            v.tensor_sub(ypbl[0][:], hsv[:],
                                         ypbh[0][:]).then_inc(sME.h, 1)
                            v.drain()
                            if k + 1 < 6 and idx % 2 == 1:
                                a2_prep_los(v, k + 1, [idx // 2])

        @block.scalar
        def _(sc):
            a2_prep_actwork(sc, 0, first=True)

        @block.vector
        def _(v):
            a2_prep_los(v, 0, range(32), first=True)

        @block.tensor
        def _(te):
            wf(te, fvAR1)
            a2_prep_trs(te, 0, range(32))

        for k in range(6):
            if k + 1 < 6:
                @block.scalar
                def _(sc, k=k):
                    a2_prep_actwork(sc, k + 1)
            a2_main(k)

        fv = fence_here((sMP, 64 * 6), (sME, 64 * 6), (sPEv, 32 * 6),
                        *dsST.finals())
        a2ctx.close()

        a3ctx = gctx.enter_context(ExitStack())
        obrep = mkbuf(a3ctx, "obrep", [128, D])
        idb3 = mkbuf(a3ctx, "idb3", [128, 128], BF16)
        owTh = mkbuf(a3ctx, "owTh", [128, 16 * 8 * 128], BF16)
        owTl = mkbuf(a3ctx, "owTl", [128, 16 * 8 * 128], BF16)
        a3prctx = a3ctx.enter_context(ExitStack())
        pr_ow = prep_weight(a3prctx, "pow", W["out_w"], 8, I2, [owTh, owTl], fv,
                            layout="k", ident=idb3,
                            extra_dma=[(obrep[:],
                                        W["out_b"][None, :].broadcast_to([128, D])),
                                       (idb3[:], identb_h[:])])
        fvA3 = fence_here((pr_ow.sE, pr_ow.n))
        a3prctx.close()

        ypTh = [mkbuf(a3ctx, f"a3yh{i}", [128, 16 * 128], BF16) for i in range(2)]
        ypTl = [mkbuf(a3ctx, f"a3yl{i}", [128, 16 * 128], BF16) for i in range(2)]
        xch = [mkbuf(a3ctx, f"a3x{i}", [128, D]) for i in range(2)]
        xnch = [mkbuf(a3ctx, f"a3n{i}", [128, D]) for i in range(2)]
        preb = mkbuf(a3ctx, "a3pre", [128, D])
        accb = [mkbuf(a3ctx, f"a3a{i}", [128, D]) for i in range(2)]

        dsY = DStream("a3y")
        dsXA = DStream("a3xa")
        sP3 = mksem("a3p")
        sE3 = mksem("a3e")
        dsS3 = DStream("a3s")

        @block.sync
        def _(sync):
            wf(sync, fvA3)
            for m in range(32):
                if m >= 2:
                    sync.wait_ge(sP3.h, m - 1)
                dsY.dma(sync,
                        ypTh[m % 2][:].rearrange("p (k n) -> p k n", k=16),
                        yph_h.rearrange("(k p) n -> p k n", p=128)
                        [:, :, m * 128:(m + 1) * 128])
                dsY.dma(sync,
                        ypTl[m % 2][:].rearrange("p (k n) -> p k n", k=16),
                        ypl_h.rearrange("(k p) n -> p k n", p=128)
                        [:, :, m * 128:(m + 1) * 128])

        @block.scalar
        def _(sc):
            wf(sc, fvA3)
            for m in range(32):
                if m >= 2:
                    sc.wait_ge(sE3.h, m - 1)
                dsXA.dma(sc, xch[m % 2][:], x_h[m * 128:(m + 1) * 128, :])
                dsXA.dma(sc, xnch[m % 2][:], xn_h[m * 128:(m + 1) * 128, :])

        @block.tensor
        def _(te):
            wf(te, fvA3)
            for m in range(32):
                dsY.wait_upto(te, 2 * m + 1)
                if m >= 2:
                    te.wait_ge(sE3.h, m - 1)
                psA = pb[2 * (m % 2)]
                psB = pb[2 * (m % 2) + 1]
                for k in range(16):
                    yh = ypTh[m % 2][:, k * 128:(k + 1) * 128]
                    yl = ypTl[m % 2][:, k * 128:(k + 1) * 128]
                    oh = owTh[:, k * 1024:(k + 1) * 1024]
                    ol = owTl[:, k * 1024:(k + 1) * 1024]
                    te.matmul(psA[:], yh, oh[:, 0:512], start=(k == 0), stop=False)
                    te.matmul(psB[:], yh, oh[:, 512:1024], start=(k == 0), stop=False)
                    te.matmul(psA[:], yh, ol[:, 0:512], start=False, stop=False)
                    te.matmul(psB[:], yh, ol[:, 512:1024], start=False, stop=False)
                    te.matmul(psA[:], yl, oh[:, 0:512], start=False, stop=(k == 15))
                    inst = te.matmul(psB[:], yl, oh[:, 512:1024], start=False,
                                     stop=(k == 15))
                inst.then_inc(sP3.h, 1)
            sP3.v = 32

        @block.gpsimd
        def _(gp):
            wf(gp, fvA3)
            for m in range(32):
                gp.wait_ge(sE3.h, m + 1)
                dsS3.dma(gp, acc_h[m * 128:(m + 1) * 128, :], accb[m % 2][:])

        @block.vector
        def _(v):
            wf(v, fvA3)
            for m in range(32):
                v.wait_ge(sP3.h, m + 1)
                dsXA.wait_upto(v, 2 * m + 1)
                if m >= 2:
                    dsS3.wait(v, m - 2)
                psA = pb[2 * (m % 2)]
                psB = pb[2 * (m % 2) + 1]
                v.tensor_add(preb[:], xch[m % 2][:], xnch[m % 2][:])
                v.drain()
                v.tensor_add(preb[:], preb[:], obrep[:])
                v.drain()
                v.tensor_add(accb[m % 2][:, 0:512], psA[:], preb[:, 0:512])
                v.drain()
                v.tensor_add(accb[m % 2][:, 512:1024], psB[:],
                             preb[:, 512:1024]).then_inc(sE3.h, 1)
                v.drain()
            sE3.v = 32

        fv_endA3 = fence_here(*dsS3.finals(), (s_ard, 112), (s_arc, 3))
        a3ctx.close()

        fctx = gctx.enter_context(ExitStack())
        idbf = mkbuf(fctx, "idbf", [128, 128], BF16)
        qdT = mkbuf(fctx, "qdT", [128, 22 * 8 * 128], BF16)
        f1ctx = fctx.enter_context(ExitStack())
        qwgT = mkbuf(f1ctx, "qwgT", [128, 8 * 22 * 128], BF16)
        qwuT = mkbuf(f1ctx, "qwuT", [128, 8 * 22 * 128], BF16)
        fprctx = f1ctx.enter_context(ExitStack())
        pr_qg = prep_weight(fprctx, "pqg", W["w_gate"], 22, D, [qwgT], fv_endA3,
                            mode="quant", layout="k", sw_dst=swg,
                            sw_mul_ap=sxrep[:, 0:1], ident=idbf,
                            extra_dma=[(idbf[:], identb_h[:])])
        pr_qu = prep_weight(fprctx, "pqu", W["w_up"], 22, D, [qwuT], fv_endA3,
                            mode="quant", layout="k", sw_dst=swu,
                            sw_mul_ap=sxrep[:, 0:1], ident=idbf)
        fvF1 = fence_here((pr_qg.sE, pr_qg.n), (pr_qu.sE, pr_qu.n))
        fprctx.close()

        xnf = [mkbuf(f1ctx, f"f1x{i}", [128, D]) for i in range(2)]
        q16b = mkbuf(f1ctx, "f1q16", [128, D], F16)
        qxb = [mkbuf(f1ctx, f"f1qx{i}", [128, D], BF16) for i in range(4)]
        qxT = mkbuf(f1ctx, "f1qxT", [128, 8 * TT], BF16)
        grb = [mkbuf(f1ctx, f"f1gr{i}", [128, TT]) for i in range(2)]
        slb = [mkbuf(f1ctx, f"f1sl{i}", [128, TT]) for i in range(2)]
        hb = [mkbuf(f1ctx, f"f1h{i}", [128, TT]) for i in range(2)]
        fst = mkbuf(f1ctx, "f1st", [128, 1])

        dsXF = DStream("f1x")
        sQ = mksem("f1q")
        sPF = mksem("f1p")
        sEF = mksem("f1e")
        sSL = mksem("f1sl")
        sHE = mksem("f1he")
        dsHS = DStream("f1hs")

        def f1_quant(v, t):
            if t > 0:
                v.wait_ge(sPF.h, 30 * (t - 1) + 8)
            for st in range(4):
                g = 4 * t + st
                dsXF.wait(v, g)
                v.tensor_scalar(q16b[:], xnf[g % 2][:], invsx[:, 0:1], MAGIC,
                                ALU.mult, ALU.add)
                v.drain()
                v.tensor_scalar(q16b[:], q16b[:], MAGIC - 128.0, MAGIC + 127.0,
                                ALU.max, ALU.min)
                v.drain()
                v.tensor_scalar(qxb[st][:], q16b[:], -MAGIC, None,
                                ALU.add).then_inc(sQ.h, 1)
                v.drain()

        def f1_tr(te, t):
            te.wait_ge(sQ.h, 4 * (t + 1))
            for dk in range(8):
                b = 8 * t + dk
                if b >= 2:
                    te.wait_ge(sEF.h, b - 1)
                bank = pb[6 + (b % 2)]
                for st in range(4):
                    inst = te.matmul(bank[:, st * 128:(st + 1) * 128],
                                     qxb[st][:, dk * 128:(dk + 1) * 128],
                                     idbf[:], start=(st == 0), stop=(st == 3))
                inst.then_inc(sPF.h, 1)

        def f1_trev(sc, t):
            for dk in range(8):
                b = 8 * t + dk
                sc.wait_ge(sPF.h, 30 * t + dk + 1)
                bank = pb[6 + (b % 2)]
                sc.copy(qxT[:, dk * 512:(dk + 1) * 512], bank[:]).then_inc(sEF.h, 1)
                sc.drain()

        def f1_mm(te, t):
            te.wait_ge(sEF.h, 8 * (t + 1))
            for i in range(22):
                gidx = 22 * t + i
                if gidx >= 2:
                    te.wait_ge(sHE.h, 2 * gidx - 2)
                psG = pb[2 * (i % 2)]
                psU = pb[2 * (i % 2) + 1]
                for wT, ps in ((qwgT, psG), (qwuT, psU)):
                    for k in range(8):
                        col = (k * 22 + i) * 128
                        inst = te.matmul(ps[:], wT[:, col:col + 128],
                                         qxT[:, k * 512:(k + 1) * 512],
                                         start=(k == 0), stop=(k == 7))
                inst.then_inc(sPF.h, 1)

        def f1_sil(sc, t):
            for i in range(22):
                gidx = 22 * t + i
                sc.wait_ge(sHE.h, 2 * gidx + 1)
                if gidx >= 2:
                    sc.wait_ge(sHE.h, 2 * gidx - 2)
                sc.activation(slb[i % 2][:], grb[i % 2][:], AF.Silu).then_inc(sSL.h, 1)
                sc.drain()

        def f1_hep(v, t):
            for i in range(22):
                gidx = 22 * t + i
                v.wait_ge(sPF.h, 30 * t + 8 + i + 1)
                if gidx >= 2:
                    v.wait_ge(sSL.h, gidx - 1)
                    dsHS.wait(v, gidx - 2)
                psG = pb[2 * (i % 2)]
                psU = pb[2 * (i % 2) + 1]
                v.tensor_scalar(grb[i % 2][:], psG[:], swg[:, i:i + 1], None,
                                ALU.mult).then_inc(sHE.h, 1)
                v.drain()
                v.wait_ge(sSL.h, gidx + 1)
                v.scalar_tensor_tensor(hb[i % 2][:], psU[:], swu[:, i:i + 1],
                                       slb[i % 2][:], ALU.mult, ALU.mult)
                v.drain()
                v.reduce_max(fst[:, 0:1], hb[i % 2][:], axis=AX.X,
                             apply_absolute_value=True)
                v.drain()
                v.tensor_tensor(mx_h[:], mx_h[:], fst[:, 0:1],
                                ALU.max).then_inc(sHE.h, 1)
                v.drain()

        @block.sync
        def _(sync):
            wf(sync, fvF1)
            for g in range(NST):
                if g >= 2:
                    sync.wait_ge(sQ.h, g - 1)
                dsXF.dma(sync, xnf[g % 2][:], xn_h[g * 128:(g + 1) * 128, :])

        @block.gpsimd
        def _(gp):
            wf(gp, fvF1)
            for t in range(NTT):
                for i in range(22):
                    gidx = 22 * t + i
                    gp.wait_ge(sHE.h, 2 * gidx + 2)
                    dsHS.dma(gp, hp_h[i * 128:(i + 1) * 128, t * TT:(t + 1) * TT],
                             hb[i % 2][:])

        @block.vector
        def _(v):
            wf(v, fvF1)
            f1_quant(v, 0)

        for t in range(NTT):
            @block.tensor
            def _(te, t=t):
                if t == 0:
                    wf(te, fvF1)
                f1_tr(te, t)
                f1_mm(te, t)

            @block.scalar
            def _(sc, t=t):
                if t == 0:
                    wf(sc, fvF1)
                    f1_trev(sc, 0)
                f1_sil(sc, t)
                if t + 1 < NTT:
                    f1_trev(sc, t + 1)

            @block.vector
            def _(v, t=t):
                if t + 1 < NTT:
                    f1_quant(v, t + 1)
                f1_hep(v, t)

        sQ.v = 4 * NTT
        sPF.v = 30 * NTT
        sEF.v = 8 * NTT
        sSL.v = 22 * NTT
        sHE.v = 2 * 22 * NTT

        fv_dp = fence_here(*dsHS.finals(), (sHE, sHE.v))
        f1ctx.close()

        swdrep = mkbuf(fctx, "swdrep", [128, D])
        dpctx = fctx.enter_context(ExitStack())
        pr_qd = prep_weight(dpctx, "pqd", W["w_down"], 8, IF, [qdT], fv_dp,
                            mode="quant", layout="k", sw_dst=swd8, ident=idbf)

        s_a2d = mksem("ar2d")
        s_a2c = mksem("ar2c")
        s_cc2 = mksem("cc2")

        @block.gpsimd
        def _(gp):
            gp.wait_ge(fence.h, fv_dp)
            gp.dma_start(out=scr_h[256:384][:, None], in_=mx_h[:],
                         allow_slow_non_contiguous=True).then_inc(s_a2d.h, 16)
            gp.wait_ge(s_a2d.h, 16)
            gp.dma_start(out=row128[:], in_=scr_h[256:384][None, :]).then_inc(s_a2d.h, 16)

        @block.vector
        def _(v):
            v.wait_ge(s_a2d.h, 32)
            v.reduce_max(sc4[0:1, 0:1], row128[:], axis=AX.X)
            v.drain()
            v.tensor_copy(row8[:], sc4[0:1, 0:1].broadcast_to([1, 8])).then_inc(s_a2c.h, 1)
            v.drain()

        @block.gpsimd
        def _(gp):
            gp.wait_ge(s_a2c.h, 1)
            gp.dma_start(out=cc_i[1][:], in_=row8[0:1, :]).then_inc(s_a2d.h, 16)
            gp.wait_ge(s_a2d.h, 48)
            gp.collective_compute("AllReduce", ALU.max,
                                  replica_groups=[list(range(NCORES))],
                                  ins=[cc_i[1][:]], outs=[cc_o[1][:]]).then_inc(s_cc2.h, 1)
            gp.wait_ge(s_cc2.h, 1)
            gp.dma_start(out=scal[0:1, 8:9],
                         in_=cc_o[1][0:1][None, :]).then_inc(s_a2d.h, 16)

        @block.scalar
        def _(sc):
            sc.wait_ge(s_a2d.h, 64)
            sc.activation(scal[0:1, 9:10], scal[0:1, 1:2],
                          AF.Sigmoid).then_inc(s_a2c.h, 1)
            sc.drain()

        @block.vector
        def _(v):
            v.wait_ge(s_a2c.h, 2)
            v.tensor_tensor(scal[0:1, 10:11], scal[0:1, 9:10], scal[0:1, 8:9], ALU.mult)
            v.drain()
            v.tensor_scalar(scal[0:1, 10:11], scal[0:1, 10:11], QI, None, ALU.mult)
            v.drain()
            v.reciprocal(scal[0:1, 11:12], scal[0:1, 10:11]).then_inc(s_a2c.h, 1)
            v.drain()

        @block.gpsimd
        def _(gp):
            gp.wait_ge(s_a2c.h, 3)
            gp.dma_start(out=scr_h[210:212][None, :],
                         in_=scal[0:1, 10:12]).then_inc(s_a2d.h, 16)
            gp.wait_ge(s_a2d.h, 80)
            gp.dma_start(out=sdrep[:], in_=scr_h[210:211][None, :]
                         .broadcast_to([128, 1])).then_inc(s_a2d.h, 16)
            gp.dma_start(out=invsd[:], in_=scr_h[211:212][None, :]
                         .broadcast_to([128, 1])).then_inc(s_a2d.h, 16)
            gp.wait_ge(s_a2d.h, 112)
            gp.wait_ge(pr_qd.sE.h, pr_qd.n)
            gp.dma_start(out=scr_h[1024:2048].rearrange("(c p) -> p c", p=128),
                         in_=swd8[:],
                         allow_slow_non_contiguous=True).then_inc(s_a2d.h, 16)
            gp.wait_ge(s_a2d.h, 128)
            gp.dma_start(out=swdrep[:], in_=scr_h[1024:2048][None, :]
                         .broadcast_to([128, D])).then_inc(s_a2d.h, 16)

        @block.vector
        def _(v):
            v.wait_ge(s_a2d.h, 144)
            v.tensor_scalar(swdrep[:], swdrep[:], sdrep[:, 0:1], None,
                            ALU.mult).then_inc(s_a2c.h, 1)
            v.drain()

        fvF2 = fence_here((s_a2d, 144), (s_a2c, 4))
        dpctx.close()

        hsl = [mkbuf(fctx, f"f2h{i}", [128, IF]) for i in range(2)]
        hq16 = mkbuf(fctx, "f2q16", [128, IF], F16)
        qhb = [mkbuf(fctx, f"f2qh{i}", [128, IF], BF16) for i in range(2)]
        accf = [mkbuf(fctx, f"f2a{i}", [128, D]) for i in range(2)]
        outb = [mkbuf(fctx, f"f2o{i}", [128, D]) for i in range(2)]

        dsHF = DStream("f2h")
        dsAF = DStream("f2a")
        sQF = mksem("f2q")
        sPD = mksem("f2p")
        sED = mksem("f2e")
        dsOUT = DStream("f2out")

        @block.sync
        def _(sync):
            wf(sync, fvF2)
            for m in range(32):
                if m >= 2:
                    sync.wait_ge(sQF.h, m - 1)
                dsHF.dma(sync,
                         hsl[m % 2][:].rearrange("p (k n) -> p k n", k=22),
                         hp_h.rearrange("(k p) n -> p k n", p=128)
                         [:, :, m * 128:(m + 1) * 128])

        @block.scalar
        def _(sc):
            wf(sc, fvF2)
            for m in range(32):
                if m >= 2:
                    sc.wait_ge(sED.h, m - 1)
                dsAF.dma(sc, accf[m % 2][:], acc_h[m * 128:(m + 1) * 128, :])

        def f2_ep(v, mm):
            v.wait_ge(sPD.h, mm + 1)
            dsAF.wait(v, mm)
            if mm >= 2:
                dsOUT.wait(v, mm - 2)
            psA = pb[2 * (mm % 2)]
            psB = pb[2 * (mm % 2) + 1]
            v.tensor_tensor(outb[mm % 2][:, 0:512], psA[:], swdrep[:, 0:512], ALU.mult)
            v.drain()
            v.tensor_tensor(outb[mm % 2][:, 512:1024], psB[:], swdrep[:, 512:1024],
                            ALU.mult)
            v.drain()
            v.tensor_add(outb[mm % 2][:], outb[mm % 2][:],
                         accf[mm % 2][:]).then_inc(sED.h, 1)
            v.drain()

        @block.gpsimd
        def _(gp):
            wf(gp, fvF2)
            for m in range(32):
                gp.wait_ge(sED.h, m + 1)
                dsOUT.dma(gp, out_h[m * 128:(m + 1) * 128, :], outb[m % 2][:])

        @block.vector
        def _(v):
            wf(v, fvF2)
            for m in range(32):
                dsHF.wait(v, m)
                if m >= 2:
                    v.wait_ge(sPD.h, m - 1)
                v.tensor_scalar(hq16[:], hsl[m % 2][:], invsd[:, 0:1], MAGIC,
                                ALU.mult, ALU.add)
                v.drain()
                v.tensor_scalar(hq16[:], hq16[:], MAGIC - 128.0, MAGIC + 127.0,
                                ALU.max, ALU.min)
                v.drain()
                v.tensor_scalar(qhb[m % 2][:], hq16[:], -MAGIC, None,
                                ALU.add).then_inc(sQF.h, 1)
                v.drain()
                if m >= 1:
                    f2_ep(v, m - 1)
            f2_ep(v, 31)
            sQF.v = 32
            sED.v = 32

        @block.tensor
        def _(te):
            wf(te, fvF2)
            for m in range(32):
                te.wait_ge(sQF.h, m + 1)
                if m >= 2:
                    te.wait_ge(sED.h, m - 1)
                psA = pb[2 * (m % 2)]
                psB = pb[2 * (m % 2) + 1]
                for k in range(22):
                    qh = qhb[m % 2][:, k * 128:(k + 1) * 128]
                    te.matmul(psA[:], qh, qdT[:, (k * 8) * 128:(k * 8) * 128 + 512],
                              start=(k == 0), stop=(k == 21))
                    inst = te.matmul(psB[:], qh,
                                     qdT[:, (k * 8 + 4) * 128:(k * 8 + 4) * 128 + 512],
                                     start=(k == 0), stop=(k == 21))
                inst.then_inc(sPD.h, 1)
            sPD.v = 32

        @block.sync
        def _(sync):
            for sm, v in dsOUT.finals():
                sync.wait_ge(sm.h, v)

        fctx.close()
    return nc


_NC_CACHE = {}


def kernel(L0_attn_x, L0_attn_norm_w, L0_attn_in_w, L0_attn_in_b, L0_attn_gate_w,
           L0_attn_gate_b, L0_attn_b_w, L0_attn_b_b, L0_attn_c_w, L0_attn_c_b,
           L0_attn_d_w, L0_attn_d_b, L0_attn_out_w, L0_attn_out_b, L0_attn_a_log,
           L0_ffn_clip_x, L0_ffn_clip_down, L0_ffn_w_gate, L0_ffn_w_up,
           L0_ffn_w_down):
    if "nc" not in _NC_CACHE:
        _NC_CACHE["nc"] = build_nc()
    nc = _NC_CACHE["nc"]
    x = np.ascontiguousarray(np.asarray(L0_attn_x, dtype=np.float32))
    common = {
        "identb": np.eye(128, dtype=ml_dtypes.bfloat16),
        "identf": np.eye(128, dtype=np.float32),
        "norm_w": np.ascontiguousarray(L0_attn_norm_w, dtype=np.float32),
        "in_w": np.ascontiguousarray(L0_attn_in_w, dtype=np.float32),
        "in_b": np.ascontiguousarray(L0_attn_in_b, dtype=np.float32),
        "gate_w": np.ascontiguousarray(L0_attn_gate_w, dtype=np.float32),
        "gate_b": np.ascontiguousarray(L0_attn_gate_b, dtype=np.float32),
        "b_w": np.ascontiguousarray(L0_attn_b_w, dtype=np.float32),
        "b_b": np.ascontiguousarray(L0_attn_b_b, dtype=np.float32),
        "c_w": np.ascontiguousarray(L0_attn_c_w, dtype=np.float32),
        "c_b": np.ascontiguousarray(L0_attn_c_b, dtype=np.float32),
        "d_w": np.ascontiguousarray(L0_attn_d_w, dtype=np.float32),
        "d_b": np.ascontiguousarray(L0_attn_d_b, dtype=np.float32),
        "out_w": np.ascontiguousarray(L0_attn_out_w, dtype=np.float32),
        "out_b": np.ascontiguousarray(L0_attn_out_b, dtype=np.float32),
        "a_log": np.ascontiguousarray(L0_attn_a_log, dtype=np.float32),
        "clip_x": np.ascontiguousarray(L0_ffn_clip_x, dtype=np.float32),
        "clip_down": np.ascontiguousarray(L0_ffn_clip_down, dtype=np.float32),
        "w_gate": np.ascontiguousarray(L0_ffn_w_gate, dtype=np.float32),
        "w_up": np.ascontiguousarray(L0_ffn_w_up, dtype=np.float32),
        "w_down": np.ascontiguousarray(L0_ffn_w_down, dtype=np.float32),
    }
    in_maps = []
    for c in range(NCORES):
        m = dict(common)
        m["x"] = np.ascontiguousarray(x[2 * c:2 * c + 2].reshape(T, D))
        in_maps.append(m)
    res = run_bass_kernel_spmd(nc, in_maps, core_ids=list(range(NCORES)))
    out = np.concatenate(
        [res.results[c]["out"].reshape(2, 2048, D) for c in range(NCORES)], axis=0)
    return np.ascontiguousarray(out.astype(np.float32))


# version 08
# speedup vs baseline: 1.1065x; 1.0152x over previous
import numpy as np
import ml_dtypes
import concourse.bass as bass
import concourse.mybir as mybir
from concourse.bass_utils import run_bass_kernel_spmd
from contextlib import ExitStack
from types import SimpleNamespace

F32 = mybir.dt.float32
F16 = mybir.dt.float16
BF16 = mybir.dt.bfloat16
AF = mybir.ActivationFunctionType
ALU = mybir.AluOpType
AX = mybir.AxisListType

NCORES = 8
D = 1024
I2 = 2048
IF = 2816
T = 4096
TT = 512
NTT = 8
NST = 32
MAGIC = 1536.0
R1024 = 1.0 / 1024.0
QI = 1.0 / 127.0


def build_nc():
    nc = bass.Bass("TRN2", target_bir_lowering=False, debug=False)

    x_h = nc.dram_tensor("x", [T, D], F32, kind="ExternalInput").ap()
    identb_h = nc.dram_tensor("identb", [128, 128], BF16, kind="ExternalInput").ap()
    identf_h = nc.dram_tensor("identf", [128, 128], F32, kind="ExternalInput").ap()
    W = {}
    for nm, shape in [
        ("norm_w", [D]), ("in_w", [I2, D]), ("in_b", [I2]),
        ("gate_w", [I2, D]), ("gate_b", [I2]),
        ("b_w", [I2, I2]), ("b_b", [I2]), ("c_w", [I2, I2]), ("c_b", [I2]),
        ("d_w", [I2, I2]), ("d_b", [I2]),
        ("out_w", [D, I2]), ("out_b", [D]), ("a_log", [I2]),
        ("clip_x", [1]), ("clip_down", [1]),
        ("w_gate", [IF, D]), ("w_up", [IF, D]), ("w_down", [D, IF]),
    ]:
        W[nm] = nc.dram_tensor(nm, shape, F32, kind="ExternalInput").ap()
    out_h = nc.dram_tensor("out", [T, D], F32, kind="ExternalOutput").ap()

    xn_h = nc.dram_tensor("xn_i", [T, D], F32, kind="Internal").ap()
    uhi_h = nc.dram_tensor("uhi_i", [I2, T], BF16, kind="Internal").ap()
    ulo_h = nc.dram_tensor("ulo_i", [I2, T], BF16, kind="Internal").ap()
    c_hb = nc.dram_tensor("c_i", [I2, T], F32, kind="Internal").ap()
    d_hb = nc.dram_tensor("d_i", [I2, T], F32, kind="Internal").ap()
    yph_h = nc.dram_tensor("yph_i", [I2, T], BF16, kind="Internal").ap()
    ypl_h = nc.dram_tensor("ypl_i", [I2, T], BF16, kind="Internal").ap()
    hp_h = nc.dram_tensor("hp_i", [IF, T], F32, kind="Internal").ap()
    acc_h = nc.dram_tensor("acc_i", [T, D], F32, kind="Internal").ap()
    scr_h = nc.dram_tensor("scr_i", [4096], F32, kind="Internal").ap()
    cc_i = [nc.dram_tensor(f"cci{k}", [8], F32, kind="Internal").ap() for k in range(2)]
    cc_o = [nc.dram_tensor(f"cco{k}", [8], F32, kind="Internal").ap() for k in range(2)]

    gctx = ExitStack()
    with gctx:
        def mkbuf(ctx, name, shape, dt=F32):
            return ctx.enter_context(nc.sbuf_tensor(name, shape, dt))

        _semno = [0]

        def mksem(tag):
            _semno[0] += 1
            return SimpleNamespace(
                h=gctx.enter_context(nc.semaphore(f"s{_semno[0]}_{tag}")), v=0)

        _shared = {}

        def shsem(tag):
            if tag not in _shared:
                _shared[tag] = mksem(tag)
            return _shared[tag]

        class DStream:

            def __init__(self, tag):
                self.s = [mksem(tag + "0"), mksem(tag + "1")]
                self.chk = []

            def dma(self, eng, out, in_, **kw):
                sm = self.s[len(self.chk) % 2]
                if sm.v >= 16:
                    eng.wait_ge(sm.h, sm.v)
                eng.dma_start(out=out, in_=in_, **kw).then_inc(sm.h, 16)
                sm.v += 16
                self.chk.append((sm, sm.v))
                return len(self.chk) - 1

            def wait(self, eng, idx):
                sm, v = self.chk[idx]
                eng.wait_ge(sm.h, v)

            def wait_upto(self, eng, idx):
                assert idx < len(self.chk), (idx, len(self.chk))
                best = {}
                for sm, v in self.chk[:idx + 1]:
                    best[id(sm)] = (sm, v)
                for sm, v in best.values():
                    eng.wait_ge(sm.h, v)

            def finals(self):
                best = {}
                for sm, v in self.chk:
                    best[id(sm)] = (sm, v)
                return list(best.values())

        identf = mkbuf(gctx, "identf_s", [128, 128], F32)
        bias_bcd = mkbuf(gctx, "bias_bcd", [128, 48])
        a_sb = mkbuf(gctx, "a_sb", [128, 16])
        _smalls = mkbuf(gctx, "smalls", [128, 80])
        swg = _smalls[:, 0:22]
        swu = _smalls[:, 22:44]
        swd8 = _smalls[:, 44:52]
        state = _smalls[:, 52:68]
        invsx = _smalls[:, 68:69]
        invsd = _smalls[:, 69:70]
        sxrep = _smalls[:, 70:71]
        sdrep = _smalls[:, 71:72]
        mx_xn = _smalls[:, 72:73]
        mx_h = _smalls[:, 73:74]
        _rows = mkbuf(gctx, "rowsml", [1, 160])
        scal = _rows[:, 0:16]
        row128 = _rows[:, 16:144]
        row8 = _rows[:, 144:152]
        sc4 = _rows[:, 152:156]
        a1ctx = gctx.enter_context(ExitStack())
        bias_ig = mkbuf(a1ctx, "bias_ig", [128, 32])
        identb = mkbuf(a1ctx, "identb_s", [128, 128], BF16)

        pb = [gctx.enter_context(nc.psum_tensor(f"pb{i}", [128, 512], F32))
              for i in range(8)]
        block = gctx.enter_context(nc.Block())
        fence = mksem("fence")

        def fence_here(*wait_list):
            fence.v += 1
            fv = fence.v

            @block.gpsimd
            def _(g):
                for s, v in wait_list:
                    g.wait_ge(s.h, v)
                g.sem_inc(fence.h, 1)
            return fv

        def wf(eng, fv):
            eng.wait_ge(fence.h, fv)

        ds_const = DStream("const")
        s_p0a = mksem("p0a")
        s_p0v = mksem("p0v")

        @block.sync
        def _(sync):
            for dst, src in [
                (identb[:], identb_h[:]),
                (identf[:], identf_h[:]),
                (bias_ig[:, 0:16], W["in_b"].rearrange("(c p) -> p c", p=128)),
                (bias_ig[:, 16:32], W["gate_b"].rearrange("(c p) -> p c", p=128)),
                (bias_bcd[:, 0:16], W["b_b"].rearrange("(c p) -> p c", p=128)),
                (bias_bcd[:, 16:32], W["c_b"].rearrange("(c p) -> p c", p=128)),
                (bias_bcd[:, 32:48], W["d_b"].rearrange("(c p) -> p c", p=128)),
                (a_sb[:], W["a_log"].rearrange("(c p) -> p c", p=128)),
                (scal[0:1, 0:1], W["clip_x"][None, :]),
                (scal[0:1, 1:2], W["clip_down"][None, :]),
            ]:
                ds_const.dma(sync, dst, src, allow_slow_non_contiguous=True)

        @block.scalar
        def _(sc):
            for sm, v in ds_const.finals():
                sc.wait_ge(sm.h, v)
            sc.activation(a_sb[:], a_sb[:], AF.Sigmoid, scale=-1.0).then_inc(s_p0a.h, 1)
        s_p0a.v = 1

        @block.vector
        def _(v):
            v.memset(mx_xn[:], 0.0)
            v.drain()
            v.memset(mx_h[:], 0.0).then_inc(s_p0v.h, 1)
        s_p0v.v = 1
        fv0 = fence_here(*ds_const.finals(), (s_p0a, 1), (s_p0v, 1))

        def prep_weight(ctx, tag, src_ap, n_oc, Kdim, dsts, fv, mode="hilo",
                        layout="oc", fold_rep=None, row_off=0, sw_dst=None,
                        sw_mul_ap=None, extra_dma=(), scratch=None,
                        load_engine="sync", pe_guard=None, ident=None):
            nk = Kdim // 128
            grp = "B" if tag in ("pga", "pqu") else "A"
            if "prL" + grp not in _shared:
                _shared["prL" + grp] = DStream("prL" + grp)
            dsL = _shared["prL" + grp]
            sS = shsem("prS" + grp)
            sP = shsem("prP" + grp)
            sE = shsem("prE" + grp)
            bS, bP, bE = sS.v, sP.v, sE.v
            pb_base = 6 if grp == "A" else 4
            ld_idx = {}
            nsrc = 2 if mode == "hilo" else 1
            if scratch is None:
                wsl = [mkbuf(ctx, f"{tag}_w{i}", [128, Kdim]) for i in range(2)]
                bsl = [[mkbuf(ctx, f"{tag}_b{j}{i}", [128, Kdim], BF16)
                        for i in range(2)] for j in range(nsrc)]
                q16 = (mkbuf(ctx, f"{tag}_q16", [128, Kdim], F16)
                       if mode == "quant" else None)
                rst = mkbuf(ctx, f"{tag}_rst", [128, 2]) if mode == "quant" else None
            else:
                wsl, bsl, q16, rst = scratch
            kbs = [(k0, min(4, nk - k0)) for k0 in range(0, nk, 4)]
            nbb = len(kbs) * nsrc
            n_batches = n_oc * nbb

            def emit_loads(eng):
                if fv is not None:
                    wf(eng, fv)
                for dst, src in extra_dma:
                    ld_idx.setdefault("x", []).append(dsL.dma(eng, dst, src))
                for s in range(n_oc):
                    if s >= 2:
                        eng.wait_ge(sS.h, bS + s - 1)
                    ld_idx[s] = dsL.dma(
                        eng, wsl[s % 2][:],
                        src_ap[row_off + s * 128: row_off + (s + 1) * 128, :])

            if load_engine == "sync":
                @block.sync
                def _(sync):
                    emit_loads(sync)
            else:
                @block.scalar
                def _(sc):
                    emit_loads(sc)

            @block.vector
            def _(v):
                if fv is not None:
                    wf(v, fv)
                for s in range(n_oc):
                    dsL.wait(v, ld_idx[s])
                    if s >= 2:
                        v.wait_ge(sP.h, bP + (s - 1) * nbb)
                    elif bP > 0:
                        v.wait_ge(sP.h, bP)
                    if mode == "hilo":
                        if fold_rep is not None:
                            for xi in ld_idx.get("x", []):
                                dsL.wait(v, xi)
                            v.tensor_tensor(wsl[s % 2][:], wsl[s % 2][:], fold_rep,
                                            ALU.mult)
                            v.drain()
                        v.tensor_copy(bsl[0][s % 2][:], wsl[s % 2][:])
                        v.drain()
                        v.tensor_sub(bsl[1][s % 2][:], wsl[s % 2][:],
                                     bsl[0][s % 2][:]).then_inc(sS.h, 1)
                        v.drain()
                    else:
                        v.reduce_max(rst[:, 0:1], wsl[s % 2][:], axis=AX.X,
                                     apply_absolute_value=True)
                        v.drain()
                        v.reciprocal(rst[:, 1:2], rst[:, 0:1])
                        v.drain()
                        v.tensor_scalar(rst[:, 1:2], rst[:, 1:2], 127.0, None, ALU.mult)
                        v.drain()
                        v.tensor_scalar(q16[:], wsl[s % 2][:], rst[:, 1:2], MAGIC,
                                        ALU.mult, ALU.add)
                        v.drain()
                        v.tensor_scalar(bsl[0][s % 2][:], q16[:], -MAGIC, None, ALU.add)
                        v.drain()
                        if sw_mul_ap is not None:
                            v.tensor_scalar(sw_dst[:, s:s + 1], rst[:, 0:1],
                                            sw_mul_ap, QI, ALU.mult,
                                            ALU.mult).then_inc(sS.h, 1)
                        else:
                            v.tensor_scalar(sw_dst[:, s:s + 1], rst[:, 0:1],
                                            QI, None, ALU.mult).then_inc(sS.h, 1)
                        v.drain()
                sS.v = bS + n_oc

            @block.tensor
            def _(te):
                if fv is not None:
                    wf(te, fv)
                if pe_guard is not None:
                    te.wait_ge(pe_guard[0].h, pe_guard[1])
                for xi in ld_idx.get("x", []):
                    dsL.wait(te, xi)
                b = 0
                for s in range(n_oc):
                    te.wait_ge(sS.h, bS + s + 1)
                    for j in range(nsrc):
                        for (k0, kn) in kbs:
                            if b >= 2:
                                te.wait_ge(sE.h, bE + b - 1)
                            bank = pb[pb_base + (b % 2)]
                            idt = ident if ident is not None else identb
                            for c in range(kn):
                                inst = te.matmul(
                                    bank[:, c * 128:(c + 1) * 128],
                                    bsl[j][s % 2][:, (k0 + c) * 128:(k0 + c + 1) * 128],
                                    idt[:], start=(c == 0), stop=(c == kn - 1))
                            inst.then_inc(sP.h, 1)
                            b += 1
                sP.v = bP + n_batches

            @block.scalar
            def _(scx):
                if fv is not None:
                    wf(scx, fv)
                b = 0
                for s in range(n_oc):
                    for j in range(nsrc):
                        for (k0, kn) in kbs:
                            scx.wait_ge(sP.h, bP + b + 1)
                            bank = pb[pb_base + (b % 2)]
                            dst = dsts[j]
                            if layout == "oc":
                                col0 = (s * nk + k0) * 128
                                scx.copy(dst[:, col0:col0 + kn * 128],
                                         bank[:, 0:kn * 128]).then_inc(sE.h, 1)
                            else:
                                dv = dst.rearrange("p (k s2 c) -> p k s2 c",
                                                   k=nk, s2=n_oc)
                                scx.copy(dv[:, k0:k0 + kn, s, :],
                                         bank[:, 0:kn * 128].rearrange(
                                             "p (a c) -> p a c", a=kn)).then_inc(sE.h, 1)
                            scx.drain()
                            b += 1
                sE.v = bE + n_batches
            return SimpleNamespace(sP=sP, sE=sE, n=bE + n_batches)

        inTh = mkbuf(a1ctx, "inTh", [128, 16 * 8 * 128], BF16)
        inTl = mkbuf(a1ctx, "inTl", [128, 16 * 8 * 128], BF16)
        gaTh = mkbuf(a1ctx, "gaTh", [128, 16 * 8 * 128], BF16)
        gaTl = mkbuf(a1ctx, "gaTl", [128, 16 * 8 * 128], BF16)

        prctx = a1ctx.enter_context(ExitStack())
        nwrep = mkbuf(prctx, "nwrep", [128, D])
        pr_in = prep_weight(prctx, "pin", W["in_w"], 16, D, [inTh, inTl], fv0,
                            fold_rep=nwrep[:],
                            extra_dma=[(nwrep[:],
                                        W["norm_w"][None, :].broadcast_to([128, D]))])
        pr_ga = prep_weight(prctx, "pga", W["gate_w"], 16, D, [gaTh, gaTl], fv0,
                            fold_rep=nwrep[:])
        fvA1 = fence_here((pr_in.sE, pr_in.n), (pr_ga.sE, pr_ga.n))
        prctx.close()

        xb = [mkbuf(a1ctx, f"a1x{i}", [128, D]) for i in range(2)]
        xcb = [mkbuf(a1ctx, f"a1xc{i}", [128, D]) for i in range(2)]
        x2h = [mkbuf(a1ctx, f"a1h{i}", [128, D], BF16) for i in range(4)]
        x2l = [mkbuf(a1ctx, f"a1l{i}", [128, D], BF16) for i in range(4)]
        xTh = [mkbuf(a1ctx, f"a1xTh{i}", [128, 8 * 512], BF16) for i in range(2)]
        xTl = [mkbuf(a1ctx, f"a1xTl{i}", [128, 8 * 512], BF16) for i in range(2)]
        st1 = mkbuf(a1ctx, "a1st1", [128, 4])
        st2 = mkbuf(a1ctx, "a1st2", [128, 2])
        sq2 = mkbuf(a1ctx, "a1sq2", [128, 2])
        rsy = mkbuf(a1ctx, "a1rsy", [128, 2])
        rtmp = mkbuf(a1ctx, "a1rt", [128, 2])
        ubf = [mkbuf(a1ctx, f"a1u{i}", [128, TT]) for i in range(2)]
        gbf = [mkbuf(a1ctx, f"a1g{i}", [128, TT]) for i in range(2)]
        uhb = [mkbuf(a1ctx, f"a1uh{i}", [128, TT], BF16) for i in range(2)]
        ulb = [mkbuf(a1ctx, f"a1ul{i}", [128, TT], BF16) for i in range(2)]

        dsX = DStream("a1x")
        sDV = mksem("a1dv")
        sAC = mksem("a1ac")
        sPE = mksem("a1pe")
        sPT = mksem("a1pt")
        sEV = mksem("a1ev")
        sSG = mksem("a1sg")
        sUE = mksem("a1ue")
        dsXS = DStream("a1xs")
        dsUS = DStream("a1us")

        def a1_sub_st(v, t, st):
            if t > 0 and st == 0:
                v.wait_ge(sPT.h, 16 * (t - 1) + 16)
            g = 4 * t + st
            dsX.wait(v, g)
            if g >= 2:
                dsXS.wait(v, g - 2)
            v.reduce_sum(st1[:, 0:1], xb[g % 2][:], axis=AX.X)
            v.drain()
            v.tensor_scalar(st1[:, 1:2], st1[:, 0:1], R1024, None, ALU.mult)
            v.drain()
            v.tensor_scalar(xcb[g % 2][:], xb[g % 2][:], st1[:, 1:2], None,
                            ALU.subtract).then_inc(sDV.h, 1)
            v.drain()
            v.wait_ge(sAC.h, 2 * g + 1)
            v.tensor_scalar(st2[:, 0:1], st1[:, 2:3], R1024, 1e-5, ALU.mult,
                            ALU.add)
            v.drain()
            v.tensor_scalar(st1[:, 3:4], st1[:, 2:3], R1024, None, ALU.mult)
            v.drain()
            v.reciprocal(rtmp[:, 0:1], st2[:, 0:1])
            v.drain()
            v.tensor_tensor(st2[:, 1:2], st1[:, 3:4], rtmp[:, 0:1], ALU.mult)
            v.drain()
            v.tensor_scalar(st2[:, 1:2], st2[:, 1:2], 1e-6, None,
                            ALU.add).then_inc(sDV.h, 1)
            v.drain()
            v.wait_ge(sAC.h, 2 * g + 2)
            v.reciprocal(rsy[:], sq2[:])
            v.drain()
            v.tensor_tensor(rtmp[:], rsy[:], rsy[:], ALU.mult)
            v.drain()
            v.tensor_tensor(rtmp[:], rtmp[:], st2[:], ALU.mult)
            v.drain()
            v.tensor_scalar(rtmp[:], rtmp[:], -0.5, 1.5, ALU.mult, ALU.add)
            v.drain()
            v.tensor_tensor(rsy[:], rsy[:], rtmp[:], ALU.mult)
            v.drain()
            v.tensor_scalar(xcb[g % 2][:], xcb[g % 2][:], rsy[:, 0:1], None,
                            ALU.mult)
            v.drain()
            v.reduce_max(st1[:, 0:1], xcb[g % 2][:], axis=AX.X,
                         apply_absolute_value=True)
            v.drain()
            v.tensor_tensor(mx_xn[:], mx_xn[:], st1[:, 0:1], ALU.max)
            v.drain()
            v.tensor_scalar(x2h[st][:], xcb[g % 2][:], rsy[:, 1:2], None, ALU.mult)
            v.drain()
            v.scalar_tensor_tensor(x2l[st][:], xcb[g % 2][:], rsy[:, 1:2],
                                   x2h[st][:], ALU.mult,
                                   ALU.subtract).then_inc(sDV.h, 1)
            v.drain()

        def a1_acts_st(sc, t, st):
            g = 4 * t + st
            sc.wait_ge(sDV.h, 3 * g + 1)
            sc.activation(xb[g % 2][:], xcb[g % 2][:], AF.Square,
                          accum_out=st1[:, 2:3]).then_inc(sAC.h, 1)
            sc.drain()
            sc.wait_ge(sDV.h, 3 * g + 2)
            sc.activation(sq2[:], st2[:], AF.Sqrt).then_inc(sAC.h, 1)
            sc.drain()

        def a1_tr(te, t, bs):
            for bl in bs:
                dk, j = bl // 2, bl % 2
                src = x2h if j == 0 else x2l
                b = 16 * t + bl
                if bl == 0:
                    te.wait_ge(sDV.h, 3 * (4 * t + 3) + 3)
                if b >= 2:
                    te.wait_ge(sEV.h, b - 1)
                bank = pb[6 + (b % 2)]
                for st in range(4):
                    inst = te.matmul(bank[:, st * 128:(st + 1) * 128],
                                     src[st][:, dk * 128:(dk + 1) * 128],
                                     identb[:], start=(st == 0), stop=(st == 3))
                inst.then_inc(sPT.h, 1)

        def a1_trev(sc, t, bs):
            for bl in bs:
                dk, j = bl // 2, bl % 2
                dst = xTh[t % 2] if j == 0 else xTl[t % 2]
                b = 16 * t + bl
                sc.wait_ge(sPT.h, b + 1)
                bank = pb[6 + (b % 2)]
                sc.copy(dst[:, dk * 512:(dk + 1) * 512], bank[:]).then_inc(sEV.h, 1)
                sc.drain()

        def a1_mm(te, t):
            te.wait_ge(sEV.h, 16 * (t + 1))
            for i in range(16):
                gidx = 16 * t + i
                if gidx >= 2:
                    te.wait_ge(sUE.h, gidx - 1)
                psI = pb[2 * (i % 2)]
                psG = pb[2 * (i % 2) + 1]
                for wh, wl, ps in ((inTh, inTl, psI), (gaTh, gaTl, psG)):
                    for k in range(8):
                        col = (i * 8 + k) * 128
                        te.matmul(ps[:], wh[:, col:col + 128],
                                  xTh[t % 2][:, k * 512:(k + 1) * 512],
                                  start=(k == 0), stop=False)
                        te.matmul(ps[:], wh[:, col:col + 128],
                                  xTl[t % 2][:, k * 512:(k + 1) * 512],
                                  start=False, stop=False)
                        inst = te.matmul(ps[:], wl[:, col:col + 128],
                                         xTh[t % 2][:, k * 512:(k + 1) * 512],
                                         start=False, stop=(k == 7))
                inst.then_inc(sPE.h, 1)
                if t + 1 < NTT and i >= 8:
                    a1_tr(te, t + 1, [2 * (i - 8), 2 * (i - 8) + 1])

        def a1_sig(sc, t):
            for i in range(16):
                gidx = 16 * t + i
                sc.wait_ge(sPE.h, 16 * t + i + 1)
                if gidx >= 2:
                    sc.wait_ge(sUE.h, gidx - 1)
                psG = pb[2 * (i % 2) + 1]
                sc.activation(gbf[i % 2][:], psG[:], AF.Sigmoid,
                              bias=bias_ig[:, 16 + i:17 + i]).then_inc(sSG.h, 1)
                sc.drain()
                if t + 1 < NTT and i in (1, 3, 5, 7):
                    a1_acts_st(sc, t + 1, (i - 1) // 2)
                if t + 1 < NTT and i >= 8:
                    a1_trev(sc, t + 1, [2 * (i - 8), 2 * (i - 8) + 1])

        def a1_uep(v, t):
            for i in range(16):
                gidx = 16 * t + i
                v.wait_ge(sSG.h, gidx + 1)
                if gidx >= 2:
                    dsUS.wait_upto(v, 2 * (gidx - 2) + 1)
                psI = pb[2 * (i % 2)]
                v.scalar_tensor_tensor(ubf[i % 2][:], psI[:], bias_ig[:, i:i + 1],
                                       gbf[i % 2][:], ALU.add, ALU.mult)
                v.drain()
                v.tensor_copy(uhb[i % 2][:], ubf[i % 2][:])
                v.drain()
                v.tensor_sub(ulb[i % 2][:], ubf[i % 2][:],
                             uhb[i % 2][:]).then_inc(sUE.h, 1)
                v.drain()
                if t + 1 < NTT and i in (1, 3, 5, 7):
                    a1_sub_st(v, t + 1, (i - 1) // 2)

        @block.sync
        def _(sync):
            wf(sync, fvA1)
            for g in range(NST):
                if g >= 2:
                    sync.wait_ge(sAC.h, 2 * (g - 2) + 1)
                dsX.dma(sync, xb[g % 2][:], x_h[g * 128:(g + 1) * 128, :])

        def a1_gp_xn_st(gp, t, st):
            g = 4 * t + st
            gp.wait_ge(sDV.h, 3 * g + 3)
            dsXS.dma(gp, xn_h[g * 128:(g + 1) * 128, :], xcb[g % 2][:])

        def a1_gp_u(gp, t):
            for i in range(16):
                gidx = 16 * t + i
                gp.wait_ge(sUE.h, gidx + 1)
                dsUS.dma(gp, uhi_h[i * 128:(i + 1) * 128, t * TT:(t + 1) * TT],
                         uhb[i % 2][:])
                dsUS.dma(gp, ulo_h[i * 128:(i + 1) * 128, t * TT:(t + 1) * TT],
                         ulb[i % 2][:])
                if t + 1 < NTT and i in (1, 3, 5, 7):
                    a1_gp_xn_st(gp, t + 1, (i - 1) // 2)

        @block.gpsimd
        def _(gp):
            for st in range(4):
                a1_gp_xn_st(gp, 0, st)
            for t in range(NTT):
                a1_gp_u(gp, t)

        @block.vector
        def _(v):
            wf(v, fvA1)
            for st in range(4):
                a1_sub_st(v, 0, st)

        @block.scalar
        def _(sc):
            wf(sc, fvA1)
            for st in range(4):
                a1_acts_st(sc, 0, st)

        for t in range(NTT):
            @block.tensor
            def _(te, t=t):
                if t == 0:
                    wf(te, fvA1)
                    a1_tr(te, 0, list(range(16)))
                a1_mm(te, t)

            @block.scalar
            def _(sc, t=t):
                if t == 0:
                    a1_trev(sc, 0, list(range(16)))
                a1_sig(sc, t)

            @block.vector
            def _(v, t=t):
                a1_uep(v, t)

        sDV.v = 3 * NST
        sAC.v = 2 * NST
        sPE.v = 16 * NTT
        sPT.v = 16 * NTT
        sEV.v = 16 * NTT
        sSG.v = 16 * NTT
        sUE.v = 16 * NTT

        fvAR1 = fence_here(*dsUS.finals(), *dsXS.finals(), (sUE, sUE.v))
        a1ctx.close()

        s_ard = mksem("ard")
        s_arc = mksem("arc")
        s_cc1 = mksem("cc1")

        @block.gpsimd
        def _(gp):
            gp.wait_ge(fence.h, fvAR1)
            gp.dma_start(out=scr_h[0:128][:, None], in_=mx_xn[:],
                         allow_slow_non_contiguous=True).then_inc(s_ard.h, 16)
            gp.wait_ge(s_ard.h, 16)
            gp.dma_start(out=row128[:], in_=scr_h[0:128][None, :]).then_inc(s_ard.h, 16)

        @block.vector
        def _(v):
            v.wait_ge(s_ard.h, 32)
            v.reduce_max(sc4[0:1, 0:1], row128[:], axis=AX.X)
            v.drain()
            v.tensor_copy(row8[:], sc4[0:1, 0:1].broadcast_to([1, 8])).then_inc(s_arc.h, 1)
            v.drain()

        @block.gpsimd
        def _(gp):
            gp.wait_ge(s_arc.h, 1)
            gp.dma_start(out=cc_i[0][:], in_=row8[0:1, :]).then_inc(s_ard.h, 16)
            gp.wait_ge(s_ard.h, 48)
            gp.collective_compute("AllReduce", ALU.max,
                                  replica_groups=[list(range(NCORES))],
                                  ins=[cc_i[0][:]], outs=[cc_o[0][:]]).then_inc(s_cc1.h, 1)
            gp.wait_ge(s_cc1.h, 1)
            gp.dma_start(out=scal[0:1, 2:3],
                         in_=cc_o[0][0:1][None, :]).then_inc(s_ard.h, 16)

        @block.scalar
        def _(sc):
            sc.wait_ge(s_ard.h, 64)
            sc.activation(scal[0:1, 4:5], scal[0:1, 0:1],
                          AF.Sigmoid).then_inc(s_arc.h, 1)
            sc.drain()

        @block.vector
        def _(v):
            v.wait_ge(s_arc.h, 2)
            v.tensor_tensor(scal[0:1, 5:6], scal[0:1, 4:5], scal[0:1, 2:3], ALU.mult)
            v.drain()
            v.tensor_scalar(scal[0:1, 5:6], scal[0:1, 5:6], QI, None, ALU.mult)
            v.drain()
            v.reciprocal(scal[0:1, 6:7], scal[0:1, 5:6]).then_inc(s_arc.h, 1)
            v.drain()

        @block.gpsimd
        def _(gp):
            gp.wait_ge(s_arc.h, 3)
            gp.dma_start(out=scr_h[200:202][None, :],
                         in_=scal[0:1, 5:7]).then_inc(s_ard.h, 16)
            gp.wait_ge(s_ard.h, 80)
            gp.dma_start(out=sxrep[:], in_=scr_h[200:201][None, :]
                         .broadcast_to([128, 1])).then_inc(s_ard.h, 16)
            gp.dma_start(out=invsx[:], in_=scr_h[201:202][None, :]
                         .broadcast_to([128, 1])).then_inc(s_ard.h, 16)
        s_ard.v = 112
        s_arc.v = 3

        a2ctx = gctx.enter_context(ExitStack())
        wThs = [mkbuf(a2ctx, f"wTh{i}", [128, 16 * 8 * 128], BF16) for i in range(2)]
        wTls = [mkbuf(a2ctx, f"wTl{i}", [128, 16 * 8 * 128], BF16) for i in range(2)]
        uTh = [mkbuf(a2ctx, f"uTh{i}", [128, 16 * TT], BF16) for i in range(2)]
        uTl = [mkbuf(a2ctx, f"uTl{i}", [128, 16 * TT], BF16) for i in range(2)]
        obuf = [mkbuf(a2ctx, "a2ob", [128, TT])] * 2
        cbuf = [mkbuf(a2ctx, f"a2c{i}", [128, TT]) for i in range(2)]
        dbuf = [mkbuf(a2ctx, f"a2d{i}", [128, TT]) for i in range(2)]
        ypbh = [mkbuf(a2ctx, "a2yh0", [128, TT], BF16)]
        ypbl = [mkbuf(a2ctx, "a2yl0", [128, TT], BF16)]
        a2_wsl = mkbuf(a2ctx, "a2pw", [128, 512])

        dsU = DStream("a2u")
        dsCD = DStream("a2cd")
        dsST = DStream("a2st")
        dsPL = DStream("a2pl")
        sPP = mksem("a2pp")
        sEvH = mksem("a2eh")
        sPEv = mksem("a2pe")
        sMP = mksem("a2mp")
        sME = mksem("a2me")
        sCDu = mksem("a2cdu")

        A2SPECS = [("c", 0), ("c", 1), ("d", 0), ("d", 1), ("b", 0), ("b", 1)]
        prep_ld_idx = [dict() for _ in range(6)]

        def a2_prep_load1(eng, k, h):
            kind, half = A2SPECS[k]
            wsrc = {"c": W["c_w"], "d": W["d_w"], "b": W["b_w"]}[kind]
            bP = 32 * k
            si, q = h // 4, h % 4
            if h >= 1:
                eng.wait_ge(sPP.h, bP + h)
            prep_ld_idx[k][h] = dsPL.dma(
                eng, a2_wsl[:],
                wsrc[half * 1024 + si * 128: half * 1024 + (si + 1) * 128,
                     q * 512:(q + 1) * 512])

        def a2_prep_hi1(sc, k, h):
            bP = 32 * k
            si, q = h // 4, h % 4
            sc.wait_ge(sPP.h, bP + h + 1)
            bank = pb[6 + (h % 2)]
            col0 = (si * 16 + q * 4) * 128
            sc.copy(wThs[k % 2][:, col0:col0 + 512],
                    bank[:]).then_inc(sEvH.h, 1)
            sc.drain()

        def a2_prep_lo1(v, k, h):
            bE = 32 * k
            si, q = h // 4, h % 4
            v.wait_ge(sEvH.h, bE + h + 1)
            bank = pb[6 + (h % 2)]
            col0 = (si * 16 + q * 4) * 128
            v.tensor_sub(wTls[k % 2][:, col0:col0 + 512], bank[:],
                         wThs[k % 2][:, col0:col0 + 512]).then_inc(sPEv.h, 1)
            v.drain()

        def a2_prep_actwork(sc, k, first=False):
            if first:
                wf(sc, fvAR1)
            a2_prep_load1(sc, k, 0)
            a2_prep_load1(sc, k, 1)
            for h in range(32):
                a2_prep_hi1(sc, k, h)
                if h + 2 < 32:
                    a2_prep_load1(sc, k, h + 2)

        def a2_prep_los(v, k, hs, first=False):
            if first:
                wf(v, fvAR1)
            for h in hs:
                a2_prep_lo1(v, k, h)

        def a2_prep_trs(te, k, hs):
            bE = 32 * k
            for h in hs:
                dsPL.wait(te, prep_ld_idx[k][h])
                if bE + h >= 2:
                    te.wait_ge(sPEv.h, bE + h - 1)
                bank = pb[6 + (h % 2)]
                for c in range(4):
                    inst = te.matmul(bank[:, c * 128:(c + 1) * 128],
                                     a2_wsl[:, c * 128:(c + 1) * 128],
                                     identf[:], start=(c == 0), stop=(c == 3))
                inst.then_inc(sPP.h, 1)



        def a2_main(k):
            kind, half = A2SPECS[k]
            whbm = {"c": c_hb, "d": d_hb}.get(kind)
            bias_off = {"b": 0, "c": 16, "d": 32}[kind]
            mp0 = 64 * k
            me0 = 64 * k
            st_per = 2 if kind == "b" else 1
            st0 = len(dsST.chk)
            bUi = len(dsU.chk)
            bCDi = len(dsCD.chk)
            wTh = wThs[k % 2]
            wTl = wTls[k % 2]

            @block.sync
            def _(sync):
                if k == 0:
                    wf(sync, fvAR1)
                for t in range(NTT):
                    if t >= 2:
                        sync.wait_ge(sMP.h, mp0 + 8 * (t - 1))
                    elif mp0 - 8 * (1 - t) > 0:
                        sync.wait_ge(sMP.h, mp0 - 8 * (1 - t))
                    dsU.dma(sync,
                            uTh[t % 2][:].rearrange("p (kk n) -> p kk n", kk=16),
                            uhi_h.rearrange("(kk p) n -> p kk n", p=128)
                            [:, :, t * TT:(t + 1) * TT])
                    dsU.dma(sync,
                            uTl[t % 2][:].rearrange("p (kk n) -> p kk n", kk=16),
                            ulo_h.rearrange("(kk p) n -> p kk n", p=128)
                            [:, :, t * TT:(t + 1) * TT])
                    if kind == "b":
                        cdu0 = 128 if half == 1 else 0
                        for j in range(8):
                            idx = 8 * t + j
                            if idx >= 2:
                                sync.wait_ge(sCDu.h, cdu0 + 2 * (idx - 1))
                            elif cdu0 > 0:
                                sync.wait_ge(sCDu.h, cdu0)
                            dsCD.dma(sync, cbuf[idx % 2][:],
                                     c_hb[half * 1024 + j * 128:
                                          half * 1024 + (j + 1) * 128,
                                          t * TT:(t + 1) * TT])
                            dsCD.dma(sync, dbuf[idx % 2][:],
                                     d_hb[half * 1024 + j * 128:
                                          half * 1024 + (j + 1) * 128,
                                          t * TT:(t + 1) * TT])

            @block.tensor
            def _(te):
                te.wait_ge(sPEv.h, 32 * (k + 1))
                for t in range(NTT):
                    dsU.wait_upto(te, bUi + 2 * t + 1)
                    for j in range(8):
                        idx = 8 * t + j
                        if idx >= 3:
                            te.wait_ge(sME.h, me0 + idx - 2)
                        elif me0 > 0:
                            te.wait_ge(sME.h, me0)
                        ps = pb[idx % 3]
                        for kk in range(16):
                            col = (j * 16 + kk) * 128
                            te.matmul(ps[:], wTh[:, col:col + 128],
                                      uTh[t % 2][:, kk * TT:(kk + 1) * TT],
                                      start=(kk == 0), stop=False)
                            te.matmul(ps[:], wTh[:, col:col + 128],
                                      uTl[t % 2][:, kk * TT:(kk + 1) * TT],
                                      start=False, stop=False)
                            inst = te.matmul(ps[:], wTl[:, col:col + 128],
                                             uTh[t % 2][:, kk * TT:(kk + 1) * TT],
                                             start=False, stop=(kk == 15))
                        inst.then_inc(sMP.h, 1)
                        if k + 1 < 6 and idx % 2 == 1:
                            a2_prep_trs(te, k + 1, [idx // 2])

            @block.gpsimd
            def _(gp):
                if k == 0:
                    wf(gp, fvAR1)
                for t in range(NTT):
                    for j in range(8):
                        idx = 8 * t + j
                        gp.wait_ge(sME.h, me0 + idx + 1)
                        row = half * 1024 + j * 128
                        if kind != "b":
                            dsST.dma(gp, whbm[row:row + 128, t * TT:(t + 1) * TT],
                                     obuf[idx % 2][:])
                        else:
                            dsST.dma(gp, yph_h[row:row + 128, t * TT:(t + 1) * TT],
                                     ypbh[0][:])
                            dsST.dma(gp, ypl_h[row:row + 128, t * TT:(t + 1) * TT],
                                     ypbl[0][:])

            @block.vector
            def _(v):
                if k == 0:
                    wf(v, fvAR1)
                for t in range(NTT):
                    for j in range(8):
                        idx = 8 * t + j
                        v.wait_ge(sMP.h, mp0 + idx + 1)
                        if kind != "b" and idx >= 1:
                            dsST.wait_upto(v, st0 + st_per * idx - 1)
                        elif st0 > 0 and idx < 1:
                            dsST.wait_upto(v, st0 - 1)
                        ps = pb[idx % 3]
                        bcol = bias_off + half * 8 + j
                        inst = v.tensor_scalar(obuf[idx % 2][:], ps[:],
                                               bias_bcd[:, bcol:bcol + 1], None,
                                               ALU.add)
                        if kind != "b":
                            inst.then_inc(sME.h, 1)
                        v.drain()
                        if kind != "b" and k + 1 < 6 and idx % 2 == 1:
                            a2_prep_los(v, k + 1, [idx // 2])
                        if kind == "b":
                            dsCD.wait_upto(v, bCDi + 2 * idx + 1)
                            hsv = pb[3]
                            init = (0.0 if (t % 4 == 0)
                                    else state[:, half * 8 + j: half * 8 + j + 1])
                            v.tensor_tensor_scan(
                                hsv[:],
                                a_sb[:, half * 8 + j: half * 8 + j + 1]
                                .broadcast_to([128, TT]),
                                obuf[idx % 2][:], init, ALU.mult, ALU.add)
                            v.drain()
                            v.tensor_copy(state[:, half * 8 + j: half * 8 + j + 1],
                                          hsv[:, TT - 1: TT])
                            v.drain()
                            v.tensor_tensor(hsv[:], hsv[:], cbuf[idx % 2][:],
                                            ALU.mult).then_inc(sCDu.h, 1)
                            v.drain()
                            v.tensor_tensor(hsv[:], hsv[:], dbuf[idx % 2][:],
                                            ALU.add).then_inc(sCDu.h, 1)
                            v.drain()
                            if idx >= 1:
                                dsST.wait_upto(v, st0 + 2 * idx - 1)
                            v.tensor_copy(ypbh[0][:], hsv[:])
                            v.drain()
                            v.tensor_sub(ypbl[0][:], hsv[:],
                                         ypbh[0][:]).then_inc(sME.h, 1)
                            v.drain()
                            if k + 1 < 6 and idx % 2 == 1:
                                a2_prep_los(v, k + 1, [idx // 2])

        @block.scalar
        def _(sc):
            a2_prep_actwork(sc, 0, first=True)

        @block.vector
        def _(v):
            a2_prep_los(v, 0, range(32), first=True)

        @block.tensor
        def _(te):
            wf(te, fvAR1)
            a2_prep_trs(te, 0, range(32))

        for k in range(6):
            if k + 1 < 6:
                @block.scalar
                def _(sc, k=k):
                    a2_prep_actwork(sc, k + 1)
            a2_main(k)

        fv = fence_here((sMP, 64 * 6), (sME, 64 * 6), (sPEv, 32 * 6),
                        *dsST.finals())
        a2ctx.close()

        a3ctx = gctx.enter_context(ExitStack())
        obrep = mkbuf(a3ctx, "obrep", [128, D])
        idb3 = mkbuf(a3ctx, "idb3", [128, 128], BF16)
        owTh = mkbuf(a3ctx, "owTh", [128, 16 * 8 * 128], BF16)
        owTl = mkbuf(a3ctx, "owTl", [128, 16 * 8 * 128], BF16)
        a3prctx = a3ctx.enter_context(ExitStack())
        pr_ow = prep_weight(a3prctx, "pow", W["out_w"], 8, I2, [owTh, owTl], fv,
                            layout="k", ident=idb3,
                            extra_dma=[(obrep[:],
                                        W["out_b"][None, :].broadcast_to([128, D])),
                                       (idb3[:], identb_h[:])])
        fvA3 = fence_here((pr_ow.sE, pr_ow.n))
        a3prctx.close()

        ypTh = [mkbuf(a3ctx, f"a3yh{i}", [128, 16 * 128], BF16) for i in range(2)]
        ypTl = [mkbuf(a3ctx, f"a3yl{i}", [128, 16 * 128], BF16) for i in range(2)]
        xch = [mkbuf(a3ctx, f"a3x{i}", [128, D]) for i in range(2)]
        xnch = [mkbuf(a3ctx, f"a3n{i}", [128, D]) for i in range(2)]
        preb = mkbuf(a3ctx, "a3pre", [128, D])
        accb = [mkbuf(a3ctx, f"a3a{i}", [128, D]) for i in range(2)]

        dsY = DStream("a3y")
        dsXA = DStream("a3xa")
        sP3 = mksem("a3p")
        sE3 = mksem("a3e")
        dsS3 = DStream("a3s")

        @block.sync
        def _(sync):
            wf(sync, fvA3)
            for m in range(32):
                if m >= 2:
                    sync.wait_ge(sP3.h, m - 1)
                dsY.dma(sync,
                        ypTh[m % 2][:].rearrange("p (k n) -> p k n", k=16),
                        yph_h.rearrange("(k p) n -> p k n", p=128)
                        [:, :, m * 128:(m + 1) * 128])
                dsY.dma(sync,
                        ypTl[m % 2][:].rearrange("p (k n) -> p k n", k=16),
                        ypl_h.rearrange("(k p) n -> p k n", p=128)
                        [:, :, m * 128:(m + 1) * 128])

        @block.scalar
        def _(sc):
            wf(sc, fvA3)
            for m in range(32):
                if m >= 2:
                    sc.wait_ge(sE3.h, m - 1)
                dsXA.dma(sc, xch[m % 2][:], x_h[m * 128:(m + 1) * 128, :])
                dsXA.dma(sc, xnch[m % 2][:], xn_h[m * 128:(m + 1) * 128, :])

        @block.tensor
        def _(te):
            wf(te, fvA3)
            for m in range(32):
                dsY.wait_upto(te, 2 * m + 1)
                if m >= 2:
                    te.wait_ge(sE3.h, m - 1)
                psA = pb[2 * (m % 2)]
                psB = pb[2 * (m % 2) + 1]
                for k in range(16):
                    yh = ypTh[m % 2][:, k * 128:(k + 1) * 128]
                    yl = ypTl[m % 2][:, k * 128:(k + 1) * 128]
                    oh = owTh[:, k * 1024:(k + 1) * 1024]
                    ol = owTl[:, k * 1024:(k + 1) * 1024]
                    te.matmul(psA[:], yh, oh[:, 0:512], start=(k == 0), stop=False)
                    te.matmul(psB[:], yh, oh[:, 512:1024], start=(k == 0), stop=False)
                    te.matmul(psA[:], yh, ol[:, 0:512], start=False, stop=False)
                    te.matmul(psB[:], yh, ol[:, 512:1024], start=False, stop=False)
                    te.matmul(psA[:], yl, oh[:, 0:512], start=False, stop=(k == 15))
                    inst = te.matmul(psB[:], yl, oh[:, 512:1024], start=False,
                                     stop=(k == 15))
                inst.then_inc(sP3.h, 1)
            sP3.v = 32

        @block.gpsimd
        def _(gp):
            wf(gp, fvA3)
            for m in range(32):
                gp.wait_ge(sE3.h, m + 1)
                dsS3.dma(gp, acc_h[m * 128:(m + 1) * 128, :], accb[m % 2][:])

        @block.vector
        def _(v):
            wf(v, fvA3)
            for m in range(32):
                v.wait_ge(sP3.h, m + 1)
                dsXA.wait_upto(v, 2 * m + 1)
                if m >= 2:
                    dsS3.wait(v, m - 2)
                psA = pb[2 * (m % 2)]
                psB = pb[2 * (m % 2) + 1]
                v.tensor_add(preb[:], xch[m % 2][:], xnch[m % 2][:])
                v.drain()
                v.tensor_add(preb[:], preb[:], obrep[:])
                v.drain()
                v.tensor_add(accb[m % 2][:, 0:512], psA[:], preb[:, 0:512])
                v.drain()
                v.tensor_add(accb[m % 2][:, 512:1024], psB[:],
                             preb[:, 512:1024]).then_inc(sE3.h, 1)
                v.drain()
            sE3.v = 32

        fv_endA3 = fence_here(*dsS3.finals(), (s_ard, 112), (s_arc, 3))
        a3ctx.close()

        fctx = gctx.enter_context(ExitStack())
        idbf = mkbuf(fctx, "idbf", [128, 128], BF16)
        qdT = mkbuf(fctx, "qdT", [128, 22 * 8 * 128], BF16)
        f1ctx = fctx.enter_context(ExitStack())
        qwgT = mkbuf(f1ctx, "qwgT", [128, 8 * 22 * 128], BF16)
        qwuT = mkbuf(f1ctx, "qwuT", [128, 8 * 22 * 128], BF16)
        fprctx = f1ctx.enter_context(ExitStack())
        pr_qg = prep_weight(fprctx, "pqg", W["w_gate"], 22, D, [qwgT], fv_endA3,
                            mode="quant", layout="k", sw_dst=swg,
                            sw_mul_ap=sxrep[:, 0:1], ident=idbf,
                            extra_dma=[(idbf[:], identb_h[:])])
        pr_qu = prep_weight(fprctx, "pqu", W["w_up"], 22, D, [qwuT], fv_endA3,
                            mode="quant", layout="k", sw_dst=swu,
                            sw_mul_ap=sxrep[:, 0:1], ident=idbf)
        fvF1 = fence_here((pr_qg.sE, pr_qg.n), (pr_qu.sE, pr_qu.n))
        fprctx.close()

        xnf = [mkbuf(f1ctx, f"f1x{i}", [128, D]) for i in range(2)]
        q16b = mkbuf(f1ctx, "f1q16", [128, D], F16)
        qxb = [mkbuf(f1ctx, f"f1qx{i}", [128, D], BF16) for i in range(4)]
        qxT = mkbuf(f1ctx, "f1qxT", [128, 8 * TT], BF16)
        grb = [mkbuf(f1ctx, f"f1gr{i}", [128, TT]) for i in range(2)]
        slb = [mkbuf(f1ctx, f"f1sl{i}", [128, TT]) for i in range(2)]
        hb = [mkbuf(f1ctx, f"f1h{i}", [128, TT]) for i in range(2)]
        fst = mkbuf(f1ctx, "f1st", [128, 1])

        dsXF = DStream("f1x")
        sQ = mksem("f1q")
        sPF = mksem("f1p")
        sEF = mksem("f1e")
        sSL = mksem("f1sl")
        sHE = mksem("f1he")
        dsHS = DStream("f1hs")

        def f1_quant(v, t):
            if t > 0:
                v.wait_ge(sPF.h, 30 * (t - 1) + 8)
            for st in range(4):
                g = 4 * t + st
                dsXF.wait(v, g)
                v.tensor_scalar(q16b[:], xnf[g % 2][:], invsx[:, 0:1], MAGIC,
                                ALU.mult, ALU.add)
                v.drain()
                v.tensor_scalar(q16b[:], q16b[:], MAGIC - 128.0, MAGIC + 127.0,
                                ALU.max, ALU.min)
                v.drain()
                v.tensor_scalar(qxb[st][:], q16b[:], -MAGIC, None,
                                ALU.add).then_inc(sQ.h, 1)
                v.drain()

        def f1_tr(te, t):
            te.wait_ge(sQ.h, 4 * (t + 1))
            for dk in range(8):
                b = 8 * t + dk
                if b >= 2:
                    te.wait_ge(sEF.h, b - 1)
                bank = pb[6 + (b % 2)]
                for st in range(4):
                    inst = te.matmul(bank[:, st * 128:(st + 1) * 128],
                                     qxb[st][:, dk * 128:(dk + 1) * 128],
                                     idbf[:], start=(st == 0), stop=(st == 3))
                inst.then_inc(sPF.h, 1)

        def f1_trev(sc, t):
            for dk in range(8):
                b = 8 * t + dk
                sc.wait_ge(sPF.h, 30 * t + dk + 1)
                bank = pb[6 + (b % 2)]
                sc.copy(qxT[:, dk * 512:(dk + 1) * 512], bank[:]).then_inc(sEF.h, 1)
                sc.drain()

        def f1_mm(te, t):
            te.wait_ge(sEF.h, 8 * (t + 1))
            for i in range(22):
                gidx = 22 * t + i
                if gidx >= 2:
                    te.wait_ge(sHE.h, 2 * gidx - 2)
                psG = pb[2 * (i % 2)]
                psU = pb[2 * (i % 2) + 1]
                for wT, ps in ((qwgT, psG), (qwuT, psU)):
                    for k in range(8):
                        col = (k * 22 + i) * 128
                        inst = te.matmul(ps[:], wT[:, col:col + 128],
                                         qxT[:, k * 512:(k + 1) * 512],
                                         start=(k == 0), stop=(k == 7))
                inst.then_inc(sPF.h, 1)

        def f1_sil(sc, t):
            for i in range(22):
                gidx = 22 * t + i
                sc.wait_ge(sHE.h, 2 * gidx + 1)
                if gidx >= 2:
                    sc.wait_ge(sHE.h, 2 * gidx - 2)
                sc.activation(slb[i % 2][:], grb[i % 2][:], AF.Silu).then_inc(sSL.h, 1)
                sc.drain()

        def f1_hep(v, t):
            for i in range(22):
                gidx = 22 * t + i
                v.wait_ge(sPF.h, 30 * t + 8 + i + 1)
                if gidx >= 2:
                    v.wait_ge(sSL.h, gidx - 1)
                    dsHS.wait(v, gidx - 2)
                psG = pb[2 * (i % 2)]
                psU = pb[2 * (i % 2) + 1]
                v.tensor_scalar(grb[i % 2][:], psG[:], swg[:, i:i + 1], None,
                                ALU.mult).then_inc(sHE.h, 1)
                v.drain()
                v.wait_ge(sSL.h, gidx + 1)
                v.scalar_tensor_tensor(hb[i % 2][:], psU[:], swu[:, i:i + 1],
                                       slb[i % 2][:], ALU.mult, ALU.mult)
                v.drain()
                v.reduce_max(fst[:, 0:1], hb[i % 2][:], axis=AX.X,
                             apply_absolute_value=True)
                v.drain()
                v.tensor_tensor(mx_h[:], mx_h[:], fst[:, 0:1],
                                ALU.max).then_inc(sHE.h, 1)
                v.drain()

        @block.sync
        def _(sync):
            wf(sync, fvF1)
            for g in range(NST):
                if g >= 2:
                    sync.wait_ge(sQ.h, g - 1)
                dsXF.dma(sync, xnf[g % 2][:], xn_h[g * 128:(g + 1) * 128, :])

        @block.gpsimd
        def _(gp):
            wf(gp, fvF1)
            for t in range(NTT):
                for i in range(22):
                    gidx = 22 * t + i
                    gp.wait_ge(sHE.h, 2 * gidx + 2)
                    dsHS.dma(gp, hp_h[i * 128:(i + 1) * 128, t * TT:(t + 1) * TT],
                             hb[i % 2][:])

        @block.vector
        def _(v):
            wf(v, fvF1)
            f1_quant(v, 0)

        for t in range(NTT):
            @block.tensor
            def _(te, t=t):
                if t == 0:
                    wf(te, fvF1)
                f1_tr(te, t)
                f1_mm(te, t)

            @block.scalar
            def _(sc, t=t):
                if t == 0:
                    wf(sc, fvF1)
                    f1_trev(sc, 0)
                f1_sil(sc, t)
                if t + 1 < NTT:
                    f1_trev(sc, t + 1)

            @block.vector
            def _(v, t=t):
                if t + 1 < NTT:
                    f1_quant(v, t + 1)
                f1_hep(v, t)

        sQ.v = 4 * NTT
        sPF.v = 30 * NTT
        sEF.v = 8 * NTT
        sSL.v = 22 * NTT
        sHE.v = 2 * 22 * NTT

        fv_dp = fence_here(*dsHS.finals(), (sHE, sHE.v))
        f1ctx.close()

        swdrep = mkbuf(fctx, "swdrep", [128, D])
        dpctx = fctx.enter_context(ExitStack())
        pr_qd = prep_weight(dpctx, "pqd", W["w_down"], 8, IF, [qdT], fv_dp,
                            mode="quant", layout="k", sw_dst=swd8, ident=idbf)

        s_a2d = mksem("ar2d")
        s_a2c = mksem("ar2c")
        s_cc2 = mksem("cc2")

        @block.gpsimd
        def _(gp):
            gp.wait_ge(fence.h, fv_dp)
            gp.dma_start(out=scr_h[256:384][:, None], in_=mx_h[:],
                         allow_slow_non_contiguous=True).then_inc(s_a2d.h, 16)
            gp.wait_ge(s_a2d.h, 16)
            gp.dma_start(out=row128[:], in_=scr_h[256:384][None, :]).then_inc(s_a2d.h, 16)

        @block.vector
        def _(v):
            v.wait_ge(s_a2d.h, 32)
            v.reduce_max(sc4[0:1, 0:1], row128[:], axis=AX.X)
            v.drain()
            v.tensor_copy(row8[:], sc4[0:1, 0:1].broadcast_to([1, 8])).then_inc(s_a2c.h, 1)
            v.drain()

        @block.gpsimd
        def _(gp):
            gp.wait_ge(s_a2c.h, 1)
            gp.dma_start(out=cc_i[1][:], in_=row8[0:1, :]).then_inc(s_a2d.h, 16)
            gp.wait_ge(s_a2d.h, 48)
            gp.collective_compute("AllReduce", ALU.max,
                                  replica_groups=[list(range(NCORES))],
                                  ins=[cc_i[1][:]], outs=[cc_o[1][:]]).then_inc(s_cc2.h, 1)
            gp.wait_ge(s_cc2.h, 1)
            gp.dma_start(out=scal[0:1, 8:9],
                         in_=cc_o[1][0:1][None, :]).then_inc(s_a2d.h, 16)

        @block.scalar
        def _(sc):
            sc.wait_ge(s_a2d.h, 64)
            sc.activation(scal[0:1, 9:10], scal[0:1, 1:2],
                          AF.Sigmoid).then_inc(s_a2c.h, 1)
            sc.drain()

        @block.vector
        def _(v):
            v.wait_ge(s_a2c.h, 2)
            v.tensor_tensor(scal[0:1, 10:11], scal[0:1, 9:10], scal[0:1, 8:9], ALU.mult)
            v.drain()
            v.tensor_scalar(scal[0:1, 10:11], scal[0:1, 10:11], QI, None, ALU.mult)
            v.drain()
            v.reciprocal(scal[0:1, 11:12], scal[0:1, 10:11]).then_inc(s_a2c.h, 1)
            v.drain()

        @block.gpsimd
        def _(gp):
            gp.wait_ge(s_a2c.h, 3)
            gp.dma_start(out=scr_h[210:212][None, :],
                         in_=scal[0:1, 10:12]).then_inc(s_a2d.h, 16)
            gp.wait_ge(s_a2d.h, 80)
            gp.dma_start(out=sdrep[:], in_=scr_h[210:211][None, :]
                         .broadcast_to([128, 1])).then_inc(s_a2d.h, 16)
            gp.dma_start(out=invsd[:], in_=scr_h[211:212][None, :]
                         .broadcast_to([128, 1])).then_inc(s_a2d.h, 16)
            gp.wait_ge(s_a2d.h, 112)
            gp.wait_ge(pr_qd.sE.h, pr_qd.n)
            gp.dma_start(out=scr_h[1024:2048].rearrange("(c p) -> p c", p=128),
                         in_=swd8[:],
                         allow_slow_non_contiguous=True).then_inc(s_a2d.h, 16)
            gp.wait_ge(s_a2d.h, 128)
            gp.dma_start(out=swdrep[:], in_=scr_h[1024:2048][None, :]
                         .broadcast_to([128, D])).then_inc(s_a2d.h, 16)

        @block.vector
        def _(v):
            v.wait_ge(s_a2d.h, 144)
            v.tensor_scalar(swdrep[:], swdrep[:], sdrep[:, 0:1], None,
                            ALU.mult).then_inc(s_a2c.h, 1)
            v.drain()

        fvF2 = fence_here((s_a2d, 144), (s_a2c, 4))
        dpctx.close()

        hsl = [mkbuf(fctx, f"f2h{i}", [128, IF]) for i in range(2)]
        hq16 = mkbuf(fctx, "f2q16", [128, IF], F16)
        qhb = [mkbuf(fctx, f"f2qh{i}", [128, IF], BF16) for i in range(2)]
        accf = [mkbuf(fctx, f"f2a{i}", [128, D]) for i in range(2)]
        outb = [mkbuf(fctx, f"f2o{i}", [128, D]) for i in range(2)]

        dsHF = DStream("f2h")
        dsAF = DStream("f2a")
        sQF = mksem("f2q")
        sPD = mksem("f2p")
        sED = mksem("f2e")
        dsOUT = DStream("f2out")

        @block.sync
        def _(sync):
            wf(sync, fvF2)
            for m in range(32):
                if m >= 2:
                    sync.wait_ge(sQF.h, m - 1)
                dsHF.dma(sync,
                         hsl[m % 2][:].rearrange("p (k n) -> p k n", k=22),
                         hp_h.rearrange("(k p) n -> p k n", p=128)
                         [:, :, m * 128:(m + 1) * 128])

        @block.scalar
        def _(sc):
            wf(sc, fvF2)
            for m in range(32):
                if m >= 2:
                    sc.wait_ge(sED.h, m - 1)
                dsAF.dma(sc, accf[m % 2][:], acc_h[m * 128:(m + 1) * 128, :])

        def f2_ep(v, mm):
            v.wait_ge(sPD.h, mm + 1)
            dsAF.wait(v, mm)
            if mm >= 2:
                dsOUT.wait(v, mm - 2)
            psA = pb[2 * (mm % 2)]
            psB = pb[2 * (mm % 2) + 1]
            v.tensor_tensor(outb[mm % 2][:, 0:512], psA[:], swdrep[:, 0:512], ALU.mult)
            v.drain()
            v.tensor_tensor(outb[mm % 2][:, 512:1024], psB[:], swdrep[:, 512:1024],
                            ALU.mult)
            v.drain()
            v.tensor_add(outb[mm % 2][:], outb[mm % 2][:],
                         accf[mm % 2][:]).then_inc(sED.h, 1)
            v.drain()

        @block.gpsimd
        def _(gp):
            wf(gp, fvF2)
            for m in range(32):
                gp.wait_ge(sED.h, m + 1)
                dsOUT.dma(gp, out_h[m * 128:(m + 1) * 128, :], outb[m % 2][:])

        @block.vector
        def _(v):
            wf(v, fvF2)
            for m in range(32):
                dsHF.wait(v, m)
                if m >= 2:
                    v.wait_ge(sPD.h, m - 1)
                v.tensor_scalar(hq16[:], hsl[m % 2][:], invsd[:, 0:1], MAGIC,
                                ALU.mult, ALU.add)
                v.drain()
                v.tensor_scalar(hq16[:], hq16[:], MAGIC - 128.0, MAGIC + 127.0,
                                ALU.max, ALU.min)
                v.drain()
                v.tensor_scalar(qhb[m % 2][:], hq16[:], -MAGIC, None,
                                ALU.add).then_inc(sQF.h, 1)
                v.drain()
                if m >= 1:
                    f2_ep(v, m - 1)
            f2_ep(v, 31)
            sQF.v = 32
            sED.v = 32

        @block.tensor
        def _(te):
            wf(te, fvF2)
            for m in range(32):
                te.wait_ge(sQF.h, m + 1)
                if m >= 2:
                    te.wait_ge(sED.h, m - 1)
                psA = pb[2 * (m % 2)]
                psB = pb[2 * (m % 2) + 1]
                for k in range(22):
                    qh = qhb[m % 2][:, k * 128:(k + 1) * 128]
                    te.matmul(psA[:], qh, qdT[:, (k * 8) * 128:(k * 8) * 128 + 512],
                              start=(k == 0), stop=(k == 21))
                    inst = te.matmul(psB[:], qh,
                                     qdT[:, (k * 8 + 4) * 128:(k * 8 + 4) * 128 + 512],
                                     start=(k == 0), stop=(k == 21))
                inst.then_inc(sPD.h, 1)
            sPD.v = 32

        @block.sync
        def _(sync):
            for sm, v in dsOUT.finals():
                sync.wait_ge(sm.h, v)

        fctx.close()
    return nc


_NC_CACHE = {}


def kernel(L0_attn_x, L0_attn_norm_w, L0_attn_in_w, L0_attn_in_b, L0_attn_gate_w,
           L0_attn_gate_b, L0_attn_b_w, L0_attn_b_b, L0_attn_c_w, L0_attn_c_b,
           L0_attn_d_w, L0_attn_d_b, L0_attn_out_w, L0_attn_out_b, L0_attn_a_log,
           L0_ffn_clip_x, L0_ffn_clip_down, L0_ffn_w_gate, L0_ffn_w_up,
           L0_ffn_w_down):
    if "nc" not in _NC_CACHE:
        _NC_CACHE["nc"] = build_nc()
    nc = _NC_CACHE["nc"]
    x = np.ascontiguousarray(np.asarray(L0_attn_x, dtype=np.float32))
    common = {
        "identb": np.eye(128, dtype=ml_dtypes.bfloat16),
        "identf": np.eye(128, dtype=np.float32),
        "norm_w": np.ascontiguousarray(L0_attn_norm_w, dtype=np.float32),
        "in_w": np.ascontiguousarray(L0_attn_in_w, dtype=np.float32),
        "in_b": np.ascontiguousarray(L0_attn_in_b, dtype=np.float32),
        "gate_w": np.ascontiguousarray(L0_attn_gate_w, dtype=np.float32),
        "gate_b": np.ascontiguousarray(L0_attn_gate_b, dtype=np.float32),
        "b_w": np.ascontiguousarray(L0_attn_b_w, dtype=np.float32),
        "b_b": np.ascontiguousarray(L0_attn_b_b, dtype=np.float32),
        "c_w": np.ascontiguousarray(L0_attn_c_w, dtype=np.float32),
        "c_b": np.ascontiguousarray(L0_attn_c_b, dtype=np.float32),
        "d_w": np.ascontiguousarray(L0_attn_d_w, dtype=np.float32),
        "d_b": np.ascontiguousarray(L0_attn_d_b, dtype=np.float32),
        "out_w": np.ascontiguousarray(L0_attn_out_w, dtype=np.float32),
        "out_b": np.ascontiguousarray(L0_attn_out_b, dtype=np.float32),
        "a_log": np.ascontiguousarray(L0_attn_a_log, dtype=np.float32),
        "clip_x": np.ascontiguousarray(L0_ffn_clip_x, dtype=np.float32),
        "clip_down": np.ascontiguousarray(L0_ffn_clip_down, dtype=np.float32),
        "w_gate": np.ascontiguousarray(L0_ffn_w_gate, dtype=np.float32),
        "w_up": np.ascontiguousarray(L0_ffn_w_up, dtype=np.float32),
        "w_down": np.ascontiguousarray(L0_ffn_w_down, dtype=np.float32),
    }
    in_maps = []
    for c in range(NCORES):
        m = dict(common)
        m["x"] = np.ascontiguousarray(x[2 * c:2 * c + 2].reshape(T, D))
        in_maps.append(m)
    res = run_bass_kernel_spmd(nc, in_maps, core_ids=list(range(NCORES)))
    out = np.concatenate(
        [res.results[c]["out"].reshape(2, 2048, D) for c in range(NCORES)], axis=0)
    return np.ascontiguousarray(out.astype(np.float32))


# version 09
# speedup vs baseline: 1.1122x; 1.0051x over previous
import numpy as np
import ml_dtypes
import concourse.bass as bass
import concourse.mybir as mybir
from concourse.bass_utils import run_bass_kernel_spmd
from contextlib import ExitStack
from types import SimpleNamespace

F32 = mybir.dt.float32
F16 = mybir.dt.float16
BF16 = mybir.dt.bfloat16
AF = mybir.ActivationFunctionType
ALU = mybir.AluOpType
AX = mybir.AxisListType

NCORES = 8
D = 1024
I2 = 2048
IF = 2816
T = 4096
TT = 512
NTT = 8
NST = 32
MAGIC = 1536.0
R1024 = 1.0 / 1024.0
QI = 1.0 / 127.0


def build_nc():
    nc = bass.Bass("TRN2", target_bir_lowering=False, debug=False)

    x_h = nc.dram_tensor("x", [T, D], F32, kind="ExternalInput").ap()
    identb_h = nc.dram_tensor("identb", [128, 128], BF16, kind="ExternalInput").ap()
    identf_h = nc.dram_tensor("identf", [128, 128], F32, kind="ExternalInput").ap()
    W = {}
    for nm, shape in [
        ("norm_w", [D]), ("in_w", [I2, D]), ("in_b", [I2]),
        ("gate_w", [I2, D]), ("gate_b", [I2]),
        ("b_w", [I2, I2]), ("b_b", [I2]), ("c_w", [I2, I2]), ("c_b", [I2]),
        ("d_w", [I2, I2]), ("d_b", [I2]),
        ("out_w", [D, I2]), ("out_b", [D]), ("a_log", [I2]),
        ("clip_x", [1]), ("clip_down", [1]),
        ("w_gate", [IF, D]), ("w_up", [IF, D]), ("w_down", [D, IF]),
    ]:
        W[nm] = nc.dram_tensor(nm, shape, F32, kind="ExternalInput").ap()
    out_h = nc.dram_tensor("out", [T, D], F32, kind="ExternalOutput").ap()

    xn_h = nc.dram_tensor("xn_i", [T, D], F32, kind="Internal").ap()
    uhi_h = nc.dram_tensor("uhi_i", [I2, T], BF16, kind="Internal").ap()
    ulo_h = nc.dram_tensor("ulo_i", [I2, T], BF16, kind="Internal").ap()
    c_hb = nc.dram_tensor("c_i", [I2, T], F32, kind="Internal").ap()
    d_hb = nc.dram_tensor("d_i", [I2, T], F32, kind="Internal").ap()
    yph_h = nc.dram_tensor("yph_i", [I2, T], BF16, kind="Internal").ap()
    ypl_h = nc.dram_tensor("ypl_i", [I2, T], BF16, kind="Internal").ap()
    hp_h = nc.dram_tensor("hp_i", [IF, T], F32, kind="Internal").ap()
    acc_h = nc.dram_tensor("acc_i", [T, D], F32, kind="Internal").ap()
    scr_h = nc.dram_tensor("scr_i", [4096], F32, kind="Internal").ap()
    cc_i = [nc.dram_tensor(f"cci{k}", [8], F32, kind="Internal").ap() for k in range(2)]
    cc_o = [nc.dram_tensor(f"cco{k}", [8], F32, kind="Internal").ap() for k in range(2)]

    gctx = ExitStack()
    with gctx:
        def mkbuf(ctx, name, shape, dt=F32):
            return ctx.enter_context(nc.sbuf_tensor(name, shape, dt))

        _semno = [0]

        def mksem(tag):
            _semno[0] += 1
            return SimpleNamespace(
                h=gctx.enter_context(nc.semaphore(f"s{_semno[0]}_{tag}")), v=0)

        _shared = {}

        def shsem(tag):
            if tag not in _shared:
                _shared[tag] = mksem(tag)
            return _shared[tag]

        class DStream:

            def __init__(self, tag):
                self.s = [mksem(tag + "0"), mksem(tag + "1")]
                self.chk = []

            def dma(self, eng, out, in_, **kw):
                sm = self.s[len(self.chk) % 2]
                if sm.v >= 16:
                    eng.wait_ge(sm.h, sm.v)
                eng.dma_start(out=out, in_=in_, **kw).then_inc(sm.h, 16)
                sm.v += 16
                self.chk.append((sm, sm.v))
                return len(self.chk) - 1

            def wait(self, eng, idx):
                sm, v = self.chk[idx]
                eng.wait_ge(sm.h, v)

            def wait_upto(self, eng, idx):
                assert idx < len(self.chk), (idx, len(self.chk))
                best = {}
                for sm, v in self.chk[:idx + 1]:
                    best[id(sm)] = (sm, v)
                for sm, v in best.values():
                    eng.wait_ge(sm.h, v)

            def finals(self):
                best = {}
                for sm, v in self.chk:
                    best[id(sm)] = (sm, v)
                return list(best.values())

        identf = mkbuf(gctx, "identf_s", [128, 128], F32)
        bias_bcd = mkbuf(gctx, "bias_bcd", [128, 48])
        a_sb = mkbuf(gctx, "a_sb", [128, 16])
        _smalls = mkbuf(gctx, "smalls", [128, 80])
        swg = _smalls[:, 0:22]
        swu = _smalls[:, 22:44]
        swd8 = _smalls[:, 44:52]
        state = _smalls[:, 52:68]
        invsx = _smalls[:, 68:69]
        invsd = _smalls[:, 69:70]
        sxrep = _smalls[:, 70:71]
        sdrep = _smalls[:, 71:72]
        mx_xn = _smalls[:, 72:73]
        mx_h = _smalls[:, 73:74]
        _rows = mkbuf(gctx, "rowsml", [1, 160])
        scal = _rows[:, 0:16]
        row128 = _rows[:, 16:144]
        row8 = _rows[:, 144:152]
        sc4 = _rows[:, 152:156]
        a1ctx = gctx.enter_context(ExitStack())
        bias_ig = mkbuf(a1ctx, "bias_ig", [128, 32])
        identb = mkbuf(a1ctx, "identb_s", [128, 128], BF16)

        pb = [gctx.enter_context(nc.psum_tensor(f"pb{i}", [128, 512], F32))
              for i in range(8)]
        block = gctx.enter_context(nc.Block())
        fence = mksem("fence")

        def fence_here(*wait_list):
            fence.v += 1
            fv = fence.v

            @block.gpsimd
            def _(g):
                for s, v in wait_list:
                    g.wait_ge(s.h, v)
                g.sem_inc(fence.h, 1)
            return fv

        def wf(eng, fv):
            eng.wait_ge(fence.h, fv)

        ds_const = DStream("const")
        s_p0a = mksem("p0a")
        s_p0v = mksem("p0v")

        @block.sync
        def _(sync):
            for dst, src in [
                (identb[:], identb_h[:]),
                (identf[:], identf_h[:]),
                (bias_ig[:, 0:16], W["in_b"].rearrange("(c p) -> p c", p=128)),
                (bias_ig[:, 16:32], W["gate_b"].rearrange("(c p) -> p c", p=128)),
                (bias_bcd[:, 0:16], W["b_b"].rearrange("(c p) -> p c", p=128)),
                (bias_bcd[:, 16:32], W["c_b"].rearrange("(c p) -> p c", p=128)),
                (bias_bcd[:, 32:48], W["d_b"].rearrange("(c p) -> p c", p=128)),
                (a_sb[:], W["a_log"].rearrange("(c p) -> p c", p=128)),
                (scal[0:1, 0:1], W["clip_x"][None, :]),
                (scal[0:1, 1:2], W["clip_down"][None, :]),
            ]:
                ds_const.dma(sync, dst, src, allow_slow_non_contiguous=True)

        @block.scalar
        def _(sc):
            for sm, v in ds_const.finals():
                sc.wait_ge(sm.h, v)
            sc.activation(a_sb[:], a_sb[:], AF.Sigmoid, scale=-1.0).then_inc(s_p0a.h, 1)
        s_p0a.v = 1

        @block.vector
        def _(v):
            v.memset(mx_xn[:], 0.0)
            v.drain()
            v.memset(mx_h[:], 0.0).then_inc(s_p0v.h, 1)
        s_p0v.v = 1
        fv0 = fence_here(*ds_const.finals(), (s_p0a, 1), (s_p0v, 1))

        def prep_weight(ctx, tag, src_ap, n_oc, Kdim, dsts, fv, mode="hilo",
                        layout="oc", fold_rep=None, row_off=0, sw_dst=None,
                        sw_mul_ap=None, extra_dma=(), scratch=None,
                        load_engine="sync", pe_guard=None, ident=None):
            nk = Kdim // 128
            grp = "B" if tag in ("pga", "pqu") else "A"
            if "prL" + grp not in _shared:
                _shared["prL" + grp] = DStream("prL" + grp)
            dsL = _shared["prL" + grp]
            sS = shsem("prS" + grp)
            sP = shsem("prP" + grp)
            sE = shsem("prE" + grp)
            bS, bP, bE = sS.v, sP.v, sE.v
            pb_base = 6 if grp == "A" else 4
            ld_idx = {}
            nsrc = 2 if mode == "hilo" else 1
            if scratch is None:
                wsl = [mkbuf(ctx, f"{tag}_w{i}", [128, Kdim]) for i in range(2)]
                bsl = [[mkbuf(ctx, f"{tag}_b{j}{i}", [128, Kdim], BF16)
                        for i in range(2)] for j in range(nsrc)]
                q16 = (mkbuf(ctx, f"{tag}_q16", [128, Kdim], F16)
                       if mode == "quant" else None)
                rst = mkbuf(ctx, f"{tag}_rst", [128, 2]) if mode == "quant" else None
            else:
                wsl, bsl, q16, rst = scratch
            kbs = [(k0, min(4, nk - k0)) for k0 in range(0, nk, 4)]
            nbb = len(kbs) * nsrc
            n_batches = n_oc * nbb

            def emit_loads(eng):
                if fv is not None:
                    wf(eng, fv)
                for dst, src in extra_dma:
                    ld_idx.setdefault("x", []).append(dsL.dma(eng, dst, src))
                for s in range(n_oc):
                    if s >= 2:
                        eng.wait_ge(sS.h, bS + s - 1)
                    ld_idx[s] = dsL.dma(
                        eng, wsl[s % 2][:],
                        src_ap[row_off + s * 128: row_off + (s + 1) * 128, :])

            if load_engine == "sync":
                @block.sync
                def _(sync):
                    emit_loads(sync)
            else:
                @block.scalar
                def _(sc):
                    emit_loads(sc)

            @block.vector
            def _(v):
                if fv is not None:
                    wf(v, fv)
                for s in range(n_oc):
                    dsL.wait(v, ld_idx[s])
                    if s >= 2:
                        v.wait_ge(sP.h, bP + (s - 1) * nbb)
                    elif bP > 0:
                        v.wait_ge(sP.h, bP)
                    if mode == "hilo":
                        if fold_rep is not None:
                            for xi in ld_idx.get("x", []):
                                dsL.wait(v, xi)
                            v.tensor_tensor(wsl[s % 2][:], wsl[s % 2][:], fold_rep,
                                            ALU.mult)
                            v.drain()
                        v.tensor_copy(bsl[0][s % 2][:], wsl[s % 2][:])
                        v.drain()
                        v.tensor_sub(bsl[1][s % 2][:], wsl[s % 2][:],
                                     bsl[0][s % 2][:]).then_inc(sS.h, 1)
                        v.drain()
                    else:
                        v.reduce_max(rst[:, 0:1], wsl[s % 2][:], axis=AX.X,
                                     apply_absolute_value=True)
                        v.drain()
                        v.reciprocal(rst[:, 1:2], rst[:, 0:1])
                        v.drain()
                        v.tensor_scalar(rst[:, 1:2], rst[:, 1:2], 127.0, None, ALU.mult)
                        v.drain()
                        v.tensor_scalar(q16[:], wsl[s % 2][:], rst[:, 1:2], MAGIC,
                                        ALU.mult, ALU.add)
                        v.drain()
                        v.tensor_scalar(bsl[0][s % 2][:], q16[:], -MAGIC, None, ALU.add)
                        v.drain()
                        if sw_mul_ap is not None:
                            v.tensor_scalar(sw_dst[:, s:s + 1], rst[:, 0:1],
                                            sw_mul_ap, QI, ALU.mult,
                                            ALU.mult).then_inc(sS.h, 1)
                        else:
                            v.tensor_scalar(sw_dst[:, s:s + 1], rst[:, 0:1],
                                            QI, None, ALU.mult).then_inc(sS.h, 1)
                        v.drain()
                sS.v = bS + n_oc

            @block.tensor
            def _(te):
                if fv is not None:
                    wf(te, fv)
                if pe_guard is not None:
                    te.wait_ge(pe_guard[0].h, pe_guard[1])
                for xi in ld_idx.get("x", []):
                    dsL.wait(te, xi)
                b = 0
                for s in range(n_oc):
                    te.wait_ge(sS.h, bS + s + 1)
                    for j in range(nsrc):
                        for (k0, kn) in kbs:
                            if b >= 2:
                                te.wait_ge(sE.h, bE + b - 1)
                            bank = pb[pb_base + (b % 2)]
                            idt = ident if ident is not None else identb
                            for c in range(kn):
                                inst = te.matmul(
                                    bank[:, c * 128:(c + 1) * 128],
                                    bsl[j][s % 2][:, (k0 + c) * 128:(k0 + c + 1) * 128],
                                    idt[:], start=(c == 0), stop=(c == kn - 1))
                            inst.then_inc(sP.h, 1)
                            b += 1
                sP.v = bP + n_batches

            @block.scalar
            def _(scx):
                if fv is not None:
                    wf(scx, fv)
                b = 0
                for s in range(n_oc):
                    for j in range(nsrc):
                        for (k0, kn) in kbs:
                            scx.wait_ge(sP.h, bP + b + 1)
                            bank = pb[pb_base + (b % 2)]
                            dst = dsts[j]
                            if layout == "oc":
                                col0 = (s * nk + k0) * 128
                                scx.copy(dst[:, col0:col0 + kn * 128],
                                         bank[:, 0:kn * 128]).then_inc(sE.h, 1)
                            else:
                                dv = dst.rearrange("p (k s2 c) -> p k s2 c",
                                                   k=nk, s2=n_oc)
                                scx.copy(dv[:, k0:k0 + kn, s, :],
                                         bank[:, 0:kn * 128].rearrange(
                                             "p (a c) -> p a c", a=kn)).then_inc(sE.h, 1)
                            scx.drain()
                            b += 1
                sE.v = bE + n_batches
            return SimpleNamespace(sP=sP, sE=sE, n=bE + n_batches)

        inTh = mkbuf(a1ctx, "inTh", [128, 16 * 8 * 128], BF16)
        inTl = mkbuf(a1ctx, "inTl", [128, 16 * 8 * 128], BF16)
        gaTh = mkbuf(a1ctx, "gaTh", [128, 16 * 8 * 128], BF16)
        gaTl = mkbuf(a1ctx, "gaTl", [128, 16 * 8 * 128], BF16)

        prctx = a1ctx.enter_context(ExitStack())
        nwrep = mkbuf(prctx, "nwrep", [128, D])
        pr_in = prep_weight(prctx, "pin", W["in_w"], 16, D, [inTh, inTl], fv0,
                            fold_rep=nwrep[:],
                            extra_dma=[(nwrep[:],
                                        W["norm_w"][None, :].broadcast_to([128, D]))])
        pr_ga = prep_weight(prctx, "pga", W["gate_w"], 16, D, [gaTh, gaTl], fv0,
                            fold_rep=nwrep[:])
        fvA1 = fence_here((pr_in.sE, pr_in.n), (pr_ga.sE, pr_ga.n))
        prctx.close()

        xb = [mkbuf(a1ctx, f"a1x{i}", [128, D]) for i in range(2)]
        xcb = [mkbuf(a1ctx, f"a1xc{i}", [128, D]) for i in range(2)]
        x2h = [mkbuf(a1ctx, f"a1h{i}", [128, D], BF16) for i in range(4)]
        x2l = [mkbuf(a1ctx, f"a1l{i}", [128, D], BF16) for i in range(4)]
        xTh = [mkbuf(a1ctx, f"a1xTh{i}", [128, 8 * 512], BF16) for i in range(2)]
        xTl = [mkbuf(a1ctx, f"a1xTl{i}", [128, 8 * 512], BF16) for i in range(2)]
        st1 = mkbuf(a1ctx, "a1st1", [128, 4])
        st2 = mkbuf(a1ctx, "a1st2", [128, 2])
        sq2 = mkbuf(a1ctx, "a1sq2", [128, 2])
        rsy = mkbuf(a1ctx, "a1rsy", [128, 2])
        rtmp = mkbuf(a1ctx, "a1rt", [128, 2])
        ubf = [mkbuf(a1ctx, f"a1u{i}", [128, TT]) for i in range(2)]
        gbf = [mkbuf(a1ctx, f"a1g{i}", [128, TT]) for i in range(2)]
        uhb = [mkbuf(a1ctx, f"a1uh{i}", [128, TT], BF16) for i in range(2)]
        ulb = [mkbuf(a1ctx, f"a1ul{i}", [128, TT], BF16) for i in range(2)]

        dsX = DStream("a1x")
        sDV = mksem("a1dv")
        sAC = mksem("a1ac")
        sPE = mksem("a1pe")
        sPT = mksem("a1pt")
        sEV = mksem("a1ev")
        sSG = mksem("a1sg")
        sUE = mksem("a1ue")
        dsXS = DStream("a1xs")
        dsUS = DStream("a1us")

        def a1_sub_st(v, t, st):
            if t > 0 and st == 0:
                v.wait_ge(sPT.h, 16 * (t - 1) + 16)
            g = 4 * t + st
            dsX.wait(v, g)
            if g >= 2:
                dsXS.wait(v, g - 2)
            v.reduce_sum(st1[:, 0:1], xb[g % 2][:], axis=AX.X)
            v.drain()
            v.tensor_scalar(st1[:, 1:2], st1[:, 0:1], R1024, None, ALU.mult)
            v.drain()
            v.tensor_scalar(xcb[g % 2][:], xb[g % 2][:], st1[:, 1:2], None,
                            ALU.subtract).then_inc(sDV.h, 1)
            v.drain()
            v.wait_ge(sAC.h, 2 * g + 1)
            v.tensor_scalar(st2[:, 0:1], st1[:, 2:3], R1024, 1e-5, ALU.mult,
                            ALU.add)
            v.drain()
            v.tensor_scalar(st1[:, 3:4], st1[:, 2:3], R1024, None, ALU.mult)
            v.drain()
            v.reciprocal(rtmp[:, 0:1], st2[:, 0:1])
            v.drain()
            v.tensor_tensor(st2[:, 1:2], st1[:, 3:4], rtmp[:, 0:1], ALU.mult)
            v.drain()
            v.tensor_scalar(st2[:, 1:2], st2[:, 1:2], 1e-6, None,
                            ALU.add).then_inc(sDV.h, 1)
            v.drain()
            v.wait_ge(sAC.h, 2 * g + 2)
            v.reciprocal(rsy[:], sq2[:])
            v.drain()
            v.tensor_tensor(rtmp[:], rsy[:], rsy[:], ALU.mult)
            v.drain()
            v.tensor_tensor(rtmp[:], rtmp[:], st2[:], ALU.mult)
            v.drain()
            v.tensor_scalar(rtmp[:], rtmp[:], -0.5, 1.5, ALU.mult, ALU.add)
            v.drain()
            v.tensor_tensor(rsy[:], rsy[:], rtmp[:], ALU.mult)
            v.drain()
            v.tensor_scalar(xcb[g % 2][:], xcb[g % 2][:], rsy[:, 0:1], None,
                            ALU.mult)
            v.drain()
            v.reduce_max(st1[:, 0:1], xcb[g % 2][:], axis=AX.X,
                         apply_absolute_value=True)
            v.drain()
            v.tensor_tensor(mx_xn[:], mx_xn[:], st1[:, 0:1], ALU.max)
            v.drain()
            v.tensor_scalar(x2h[st][:], xcb[g % 2][:], rsy[:, 1:2], None, ALU.mult)
            v.drain()
            v.scalar_tensor_tensor(x2l[st][:], xcb[g % 2][:], rsy[:, 1:2],
                                   x2h[st][:], ALU.mult,
                                   ALU.subtract).then_inc(sDV.h, 1)
            v.drain()

        def a1_acts_st(sc, t, st):
            g = 4 * t + st
            sc.wait_ge(sDV.h, 3 * g + 1)
            sc.activation(xb[g % 2][:], xcb[g % 2][:], AF.Square,
                          accum_out=st1[:, 2:3]).then_inc(sAC.h, 1)
            sc.drain()
            sc.wait_ge(sDV.h, 3 * g + 2)
            sc.activation(sq2[:], st2[:], AF.Sqrt).then_inc(sAC.h, 1)
            sc.drain()

        def a1_tr(te, t, bs):
            for bl in bs:
                dk, j = bl // 2, bl % 2
                src = x2h if j == 0 else x2l
                b = 16 * t + bl
                if bl == 0:
                    te.wait_ge(sDV.h, 3 * (4 * t + 3) + 3)
                if b >= 2:
                    te.wait_ge(sEV.h, b - 1)
                bank = pb[6 + (b % 2)]
                for st in range(4):
                    inst = te.matmul(bank[:, st * 128:(st + 1) * 128],
                                     src[st][:, dk * 128:(dk + 1) * 128],
                                     identb[:], start=(st == 0), stop=(st == 3))
                inst.then_inc(sPT.h, 1)

        def a1_trev(sc, t, bs):
            for bl in bs:
                dk, j = bl // 2, bl % 2
                dst = xTh[t % 2] if j == 0 else xTl[t % 2]
                b = 16 * t + bl
                sc.wait_ge(sPT.h, b + 1)
                bank = pb[6 + (b % 2)]
                sc.copy(dst[:, dk * 512:(dk + 1) * 512], bank[:]).then_inc(sEV.h, 1)
                sc.drain()

        def a1_mm(te, t):
            te.wait_ge(sEV.h, 16 * (t + 1))
            for i in range(16):
                gidx = 16 * t + i
                if gidx >= 2:
                    te.wait_ge(sUE.h, gidx - 1)
                psI = pb[2 * (i % 2)]
                psG = pb[2 * (i % 2) + 1]
                for wh, wl, ps in ((inTh, inTl, psI), (gaTh, gaTl, psG)):
                    for k in range(8):
                        col = (i * 8 + k) * 128
                        te.matmul(ps[:], wh[:, col:col + 128],
                                  xTh[t % 2][:, k * 512:(k + 1) * 512],
                                  start=(k == 0), stop=False)
                        te.matmul(ps[:], wh[:, col:col + 128],
                                  xTl[t % 2][:, k * 512:(k + 1) * 512],
                                  start=False, stop=False)
                        inst = te.matmul(ps[:], wl[:, col:col + 128],
                                         xTh[t % 2][:, k * 512:(k + 1) * 512],
                                         start=False, stop=(k == 7))
                inst.then_inc(sPE.h, 1)
                if t + 1 < NTT and i >= 8:
                    a1_tr(te, t + 1, [2 * (i - 8), 2 * (i - 8) + 1])

        def a1_sig(sc, t):
            for i in range(16):
                gidx = 16 * t + i
                sc.wait_ge(sPE.h, 16 * t + i + 1)
                if gidx >= 2:
                    sc.wait_ge(sUE.h, gidx - 1)
                psG = pb[2 * (i % 2) + 1]
                sc.activation(gbf[i % 2][:], psG[:], AF.Sigmoid,
                              bias=bias_ig[:, 16 + i:17 + i]).then_inc(sSG.h, 1)
                sc.drain()
                if t + 1 < NTT and i in (1, 3, 5, 7):
                    a1_acts_st(sc, t + 1, (i - 1) // 2)
                if t + 1 < NTT and i >= 8:
                    a1_trev(sc, t + 1, [2 * (i - 8), 2 * (i - 8) + 1])

        def a1_uep(v, t):
            for i in range(16):
                gidx = 16 * t + i
                v.wait_ge(sSG.h, gidx + 1)
                if gidx >= 2:
                    dsUS.wait_upto(v, 2 * (gidx - 2) + 1)
                psI = pb[2 * (i % 2)]
                v.scalar_tensor_tensor(ubf[i % 2][:], psI[:], bias_ig[:, i:i + 1],
                                       gbf[i % 2][:], ALU.add, ALU.mult)
                v.drain()
                v.tensor_copy(uhb[i % 2][:], ubf[i % 2][:])
                v.drain()
                v.tensor_sub(ulb[i % 2][:], ubf[i % 2][:],
                             uhb[i % 2][:]).then_inc(sUE.h, 1)
                v.drain()
                if t + 1 < NTT and i in (1, 3, 5, 7):
                    a1_sub_st(v, t + 1, (i - 1) // 2)

        @block.sync
        def _(sync):
            wf(sync, fvA1)
            for g in range(NST):
                if g >= 2:
                    sync.wait_ge(sAC.h, 2 * (g - 2) + 1)
                dsX.dma(sync, xb[g % 2][:], x_h[g * 128:(g + 1) * 128, :])

        def a1_gp_xn_st(gp, t, st):
            g = 4 * t + st
            gp.wait_ge(sDV.h, 3 * g + 3)
            dsXS.dma(gp, xn_h[g * 128:(g + 1) * 128, :], xcb[g % 2][:])

        def a1_gp_u(gp, t):
            for i in range(16):
                gidx = 16 * t + i
                gp.wait_ge(sUE.h, gidx + 1)
                dsUS.dma(gp, uhi_h[i * 128:(i + 1) * 128, t * TT:(t + 1) * TT],
                         uhb[i % 2][:])
                dsUS.dma(gp, ulo_h[i * 128:(i + 1) * 128, t * TT:(t + 1) * TT],
                         ulb[i % 2][:])
                if t + 1 < NTT and i in (1, 3, 5, 7):
                    a1_gp_xn_st(gp, t + 1, (i - 1) // 2)

        @block.gpsimd
        def _(gp):
            for st in range(4):
                a1_gp_xn_st(gp, 0, st)
            for t in range(NTT):
                a1_gp_u(gp, t)

        @block.vector
        def _(v):
            wf(v, fvA1)
            for st in range(4):
                a1_sub_st(v, 0, st)

        @block.scalar
        def _(sc):
            wf(sc, fvA1)
            for st in range(4):
                a1_acts_st(sc, 0, st)

        for t in range(NTT):
            @block.tensor
            def _(te, t=t):
                if t == 0:
                    wf(te, fvA1)
                    a1_tr(te, 0, list(range(16)))
                a1_mm(te, t)

            @block.scalar
            def _(sc, t=t):
                if t == 0:
                    a1_trev(sc, 0, list(range(16)))
                a1_sig(sc, t)

            @block.vector
            def _(v, t=t):
                a1_uep(v, t)

        sDV.v = 3 * NST
        sAC.v = 2 * NST
        sPE.v = 16 * NTT
        sPT.v = 16 * NTT
        sEV.v = 16 * NTT
        sSG.v = 16 * NTT
        sUE.v = 16 * NTT

        fvAR1 = fence_here(*dsUS.finals(), *dsXS.finals(), (sUE, sUE.v))
        a1ctx.close()

        s_ard = mksem("ard")
        s_arc = mksem("arc")
        s_cc1 = mksem("cc1")

        @block.gpsimd
        def _(gp):
            gp.wait_ge(fence.h, fvAR1)
            gp.dma_start(out=scr_h[0:128][:, None], in_=mx_xn[:],
                         allow_slow_non_contiguous=True).then_inc(s_ard.h, 16)
            gp.wait_ge(s_ard.h, 16)
            gp.dma_start(out=row128[:], in_=scr_h[0:128][None, :]).then_inc(s_ard.h, 16)

        @block.vector
        def _(v):
            v.wait_ge(s_ard.h, 32)
            v.reduce_max(sc4[0:1, 0:1], row128[:], axis=AX.X)
            v.drain()
            v.tensor_copy(row8[:], sc4[0:1, 0:1].broadcast_to([1, 8])).then_inc(s_arc.h, 1)
            v.drain()

        @block.gpsimd
        def _(gp):
            gp.wait_ge(s_arc.h, 1)
            gp.dma_start(out=cc_i[0][:], in_=row8[0:1, :]).then_inc(s_ard.h, 16)
            gp.wait_ge(s_ard.h, 48)
            gp.collective_compute("AllReduce", ALU.max,
                                  replica_groups=[list(range(NCORES))],
                                  ins=[cc_i[0][:]], outs=[cc_o[0][:]]).then_inc(s_cc1.h, 1)
            gp.wait_ge(s_cc1.h, 1)
            gp.dma_start(out=scal[0:1, 2:3],
                         in_=cc_o[0][0:1][None, :]).then_inc(s_ard.h, 16)

        @block.scalar
        def _(sc):
            sc.wait_ge(s_ard.h, 64)
            sc.activation(scal[0:1, 4:5], scal[0:1, 0:1],
                          AF.Sigmoid).then_inc(s_arc.h, 1)
            sc.drain()

        @block.vector
        def _(v):
            v.wait_ge(s_arc.h, 2)
            v.tensor_tensor(scal[0:1, 5:6], scal[0:1, 4:5], scal[0:1, 2:3], ALU.mult)
            v.drain()
            v.tensor_scalar(scal[0:1, 5:6], scal[0:1, 5:6], QI, None, ALU.mult)
            v.drain()
            v.reciprocal(scal[0:1, 6:7], scal[0:1, 5:6]).then_inc(s_arc.h, 1)
            v.drain()

        @block.gpsimd
        def _(gp):
            gp.wait_ge(s_arc.h, 3)
            gp.dma_start(out=scr_h[200:202][None, :],
                         in_=scal[0:1, 5:7]).then_inc(s_ard.h, 16)
            gp.wait_ge(s_ard.h, 80)
            gp.dma_start(out=sxrep[:], in_=scr_h[200:201][None, :]
                         .broadcast_to([128, 1])).then_inc(s_ard.h, 16)
            gp.dma_start(out=invsx[:], in_=scr_h[201:202][None, :]
                         .broadcast_to([128, 1])).then_inc(s_ard.h, 16)
        s_ard.v = 112
        s_arc.v = 3

        a2ctx = gctx.enter_context(ExitStack())
        wThs = [mkbuf(a2ctx, f"wTh{i}", [128, 16 * 8 * 128], BF16) for i in range(2)]
        wTls = [mkbuf(a2ctx, f"wTl{i}", [128, 16 * 8 * 128], BF16) for i in range(2)]
        uTh = [mkbuf(a2ctx, f"uTh{i}", [128, 16 * TT], BF16) for i in range(2)]
        uTl = [mkbuf(a2ctx, f"uTl{i}", [128, 16 * TT], BF16) for i in range(2)]
        obuf = [mkbuf(a2ctx, "a2ob", [128, TT])] * 2
        cbuf = [mkbuf(a2ctx, f"a2c{i}", [128, TT]) for i in range(2)]
        dbuf = [mkbuf(a2ctx, f"a2d{i}", [128, TT]) for i in range(2)]
        ypbh = [mkbuf(a2ctx, "a2yh0", [128, TT], BF16)]
        ypbl = [mkbuf(a2ctx, "a2yl0", [128, TT], BF16)]
        a2_wsl = mkbuf(a2ctx, "a2pw", [128, 512])

        dsU = DStream("a2u")
        dsCD = DStream("a2cd")
        dsST = DStream("a2st")
        dsPL = DStream("a2pl")
        dsPG = DStream("a2pg")
        sPP = mksem("a2pp")
        sEvH = mksem("a2eh")
        sPEv = mksem("a2pe")
        sMP = mksem("a2mp")
        sME = mksem("a2me")
        sCDu = mksem("a2cdu")

        A2SPECS = [("c", 0), ("c", 1), ("d", 0), ("d", 1), ("b", 0), ("b", 1)]
        prep_ld_idx = [dict() for _ in range(6)]

        def a2_prep_load1(eng, k, h, ds=None, scr=None, step=1):
            kind, half = A2SPECS[k]
            wsrc = {"c": W["c_w"], "d": W["d_w"], "b": W["b_w"]}[kind]
            bP = 32 * k
            si, q = h // 4, h % 4
            if ds is None:
                ds, scr = dsPL, a2_wsl
            if h >= step:
                eng.wait_ge(sPP.h, bP + h - step + 1)
            prep_ld_idx[k][h] = (ds, ds.dma(
                eng, scr[:],
                wsrc[half * 1024 + si * 128: half * 1024 + (si + 1) * 128,
                     q * 512:(q + 1) * 512]), scr)

        def a2_prep_hi1(sc, k, h):
            bP = 32 * k
            si, q = h // 4, h % 4
            sc.wait_ge(sPP.h, bP + h + 1)
            bank = pb[6 + (h % 2)]
            col0 = (si * 16 + q * 4) * 128
            sc.copy(wThs[k % 2][:, col0:col0 + 512],
                    bank[:]).then_inc(sEvH.h, 1)
            sc.drain()

        def a2_prep_lo1(v, k, h):
            bE = 32 * k
            si, q = h // 4, h % 4
            v.wait_ge(sEvH.h, bE + h + 1)
            bank = pb[6 + (h % 2)]
            col0 = (si * 16 + q * 4) * 128
            v.tensor_sub(wTls[k % 2][:, col0:col0 + 512], bank[:],
                         wThs[k % 2][:, col0:col0 + 512]).then_inc(sPEv.h, 1)
            v.drain()

        def a2_prep_actwork(sc, k, first=False):
            if first:
                wf(sc, fvAR1)
            if k == 0:
                a2_prep_load1(sc, k, 0, dsPL, a2_wsl, step=2)
                for h in range(32):
                    a2_prep_hi1(sc, k, h)
                    if h + 2 < 32 and (h + 2) % 2 == 0:
                        a2_prep_load1(sc, k, h + 2, dsPL, a2_wsl, step=2)
            else:
                a2_prep_load1(sc, k, 0)
                a2_prep_load1(sc, k, 1)
                for h in range(32):
                    a2_prep_hi1(sc, k, h)
                    if h + 2 < 32:
                        a2_prep_load1(sc, k, h + 2)

        def a2_prep_los(v, k, hs, first=False):
            if first:
                wf(v, fvAR1)
            for h in hs:
                a2_prep_lo1(v, k, h)

        def a2_prep_trs(te, k, hs):
            bE = 32 * k
            for h in hs:
                ds, idx, scr = prep_ld_idx[k][h]
                ds.wait(te, idx)
                if bE + h >= 2:
                    te.wait_ge(sPEv.h, bE + h - 1)
                bank = pb[6 + (h % 2)]
                for c in range(4):
                    inst = te.matmul(bank[:, c * 128:(c + 1) * 128],
                                     scr[:, c * 128:(c + 1) * 128],
                                     identf[:], start=(c == 0), stop=(c == 3))
                inst.then_inc(sPP.h, 1)



        def a2_main(k):
            kind, half = A2SPECS[k]
            whbm = {"c": c_hb, "d": d_hb}.get(kind)
            bias_off = {"b": 0, "c": 16, "d": 32}[kind]
            mp0 = 64 * k
            me0 = 64 * k
            st_per = 2 if kind == "b" else 1
            st0 = len(dsST.chk)
            bUi = len(dsU.chk)
            bCDi = len(dsCD.chk)
            wTh = wThs[k % 2]
            wTl = wTls[k % 2]

            @block.sync
            def _(sync):
                if k == 0:
                    wf(sync, fvAR1)
                for t in range(NTT):
                    if t >= 2:
                        sync.wait_ge(sMP.h, mp0 + 8 * (t - 1))
                    elif mp0 - 8 * (1 - t) > 0:
                        sync.wait_ge(sMP.h, mp0 - 8 * (1 - t))
                    dsU.dma(sync,
                            uTh[t % 2][:].rearrange("p (kk n) -> p kk n", kk=16),
                            uhi_h.rearrange("(kk p) n -> p kk n", p=128)
                            [:, :, t * TT:(t + 1) * TT])
                    dsU.dma(sync,
                            uTl[t % 2][:].rearrange("p (kk n) -> p kk n", kk=16),
                            ulo_h.rearrange("(kk p) n -> p kk n", p=128)
                            [:, :, t * TT:(t + 1) * TT])
                    if kind == "b":
                        cdu0 = 128 if half == 1 else 0
                        for j in range(8):
                            idx = 8 * t + j
                            if idx >= 2:
                                sync.wait_ge(sCDu.h, cdu0 + 2 * (idx - 1))
                            elif cdu0 > 0:
                                sync.wait_ge(sCDu.h, cdu0)
                            dsCD.dma(sync, cbuf[idx % 2][:],
                                     c_hb[half * 1024 + j * 128:
                                          half * 1024 + (j + 1) * 128,
                                          t * TT:(t + 1) * TT])
                            dsCD.dma(sync, dbuf[idx % 2][:],
                                     d_hb[half * 1024 + j * 128:
                                          half * 1024 + (j + 1) * 128,
                                          t * TT:(t + 1) * TT])

            @block.tensor
            def _(te):
                te.wait_ge(sPEv.h, 32 * (k + 1))
                for t in range(NTT):
                    dsU.wait_upto(te, bUi + 2 * t + 1)
                    for j in range(8):
                        idx = 8 * t + j
                        if idx >= 3:
                            te.wait_ge(sME.h, me0 + idx - 2)
                        elif me0 > 0:
                            te.wait_ge(sME.h, me0)
                        ps = pb[idx % 3]
                        for kk in range(16):
                            col = (j * 16 + kk) * 128
                            te.matmul(ps[:], wTh[:, col:col + 128],
                                      uTh[t % 2][:, kk * TT:(kk + 1) * TT],
                                      start=(kk == 0), stop=False)
                            te.matmul(ps[:], wTh[:, col:col + 128],
                                      uTl[t % 2][:, kk * TT:(kk + 1) * TT],
                                      start=False, stop=False)
                            inst = te.matmul(ps[:], wTl[:, col:col + 128],
                                             uTh[t % 2][:, kk * TT:(kk + 1) * TT],
                                             start=False, stop=(kk == 15))
                        inst.then_inc(sMP.h, 1)
                        if k + 1 < 6 and idx % 2 == 1:
                            a2_prep_trs(te, k + 1, [idx // 2])

            @block.gpsimd
            def _(gp):
                if k == 0:
                    wf(gp, fvAR1)
                for t in range(NTT):
                    for j in range(8):
                        idx = 8 * t + j
                        gp.wait_ge(sME.h, me0 + idx + 1)
                        row = half * 1024 + j * 128
                        if kind != "b":
                            dsST.dma(gp, whbm[row:row + 128, t * TT:(t + 1) * TT],
                                     obuf[idx % 2][:])
                        else:
                            dsST.dma(gp, yph_h[row:row + 128, t * TT:(t + 1) * TT],
                                     ypbh[0][:])
                            dsST.dma(gp, ypl_h[row:row + 128, t * TT:(t + 1) * TT],
                                     ypbl[0][:])

            @block.vector
            def _(v):
                if k == 0:
                    wf(v, fvAR1)
                for t in range(NTT):
                    for j in range(8):
                        idx = 8 * t + j
                        v.wait_ge(sMP.h, mp0 + idx + 1)
                        if kind != "b" and idx >= 1:
                            dsST.wait_upto(v, st0 + st_per * idx - 1)
                        elif st0 > 0 and idx < 1:
                            dsST.wait_upto(v, st0 - 1)
                        ps = pb[idx % 3]
                        bcol = bias_off + half * 8 + j
                        inst = v.tensor_scalar(obuf[idx % 2][:], ps[:],
                                               bias_bcd[:, bcol:bcol + 1], None,
                                               ALU.add)
                        if kind != "b":
                            inst.then_inc(sME.h, 1)
                        v.drain()
                        if kind != "b" and k + 1 < 6 and idx % 2 == 1:
                            a2_prep_los(v, k + 1, [idx // 2])
                        if kind == "b":
                            dsCD.wait_upto(v, bCDi + 2 * idx + 1)
                            hsv = pb[3]
                            init = (0.0 if (t % 4 == 0)
                                    else state[:, half * 8 + j: half * 8 + j + 1])
                            v.tensor_tensor_scan(
                                hsv[:],
                                a_sb[:, half * 8 + j: half * 8 + j + 1]
                                .broadcast_to([128, TT]),
                                obuf[idx % 2][:], init, ALU.mult, ALU.add)
                            v.drain()
                            v.tensor_copy(state[:, half * 8 + j: half * 8 + j + 1],
                                          hsv[:, TT - 1: TT])
                            v.drain()
                            v.tensor_tensor(hsv[:], hsv[:], cbuf[idx % 2][:],
                                            ALU.mult).then_inc(sCDu.h, 1)
                            v.drain()
                            v.tensor_tensor(hsv[:], hsv[:], dbuf[idx % 2][:],
                                            ALU.add).then_inc(sCDu.h, 1)
                            v.drain()
                            if idx >= 1:
                                dsST.wait_upto(v, st0 + 2 * idx - 1)
                            v.tensor_copy(ypbh[0][:], hsv[:])
                            v.drain()
                            v.tensor_sub(ypbl[0][:], hsv[:],
                                         ypbh[0][:]).then_inc(sME.h, 1)
                            v.drain()
                            if k + 1 < 6 and idx % 2 == 1:
                                a2_prep_los(v, k + 1, [idx // 2])

        @block.gpsimd
        def _(gp):
            wf(gp, fvAR1)
            for h in range(1, 32, 2):
                a2_prep_load1(gp, 0, h, dsPG, cbuf[0], step=2)

        @block.scalar
        def _(sc):
            a2_prep_actwork(sc, 0, first=True)

        @block.vector
        def _(v):
            a2_prep_los(v, 0, range(32), first=True)

        @block.tensor
        def _(te):
            wf(te, fvAR1)
            a2_prep_trs(te, 0, range(32))

        for k in range(6):
            if k + 1 < 6:
                @block.scalar
                def _(sc, k=k):
                    a2_prep_actwork(sc, k + 1)
            a2_main(k)

        fv = fence_here((sMP, 64 * 6), (sME, 64 * 6), (sPEv, 32 * 6),
                        *dsST.finals())
        a2ctx.close()

        a3ctx = gctx.enter_context(ExitStack())
        obrep = mkbuf(a3ctx, "obrep", [128, D])
        idb3 = mkbuf(a3ctx, "idb3", [128, 128], BF16)
        owTh = mkbuf(a3ctx, "owTh", [128, 16 * 8 * 128], BF16)
        owTl = mkbuf(a3ctx, "owTl", [128, 16 * 8 * 128], BF16)
        a3prctx = a3ctx.enter_context(ExitStack())
        pr_ow = prep_weight(a3prctx, "pow", W["out_w"], 8, I2, [owTh, owTl], fv,
                            layout="k", ident=idb3,
                            extra_dma=[(obrep[:],
                                        W["out_b"][None, :].broadcast_to([128, D])),
                                       (idb3[:], identb_h[:])])
        fvA3 = fence_here((pr_ow.sE, pr_ow.n))
        a3prctx.close()

        ypTh = [mkbuf(a3ctx, f"a3yh{i}", [128, 16 * 128], BF16) for i in range(2)]
        ypTl = [mkbuf(a3ctx, f"a3yl{i}", [128, 16 * 128], BF16) for i in range(2)]
        xch = [mkbuf(a3ctx, f"a3x{i}", [128, D]) for i in range(2)]
        xnch = [mkbuf(a3ctx, f"a3n{i}", [128, D]) for i in range(2)]
        preb = mkbuf(a3ctx, "a3pre", [128, D])
        accb = [mkbuf(a3ctx, f"a3a{i}", [128, D]) for i in range(2)]

        dsY = DStream("a3y")
        dsXA = DStream("a3xa")
        sP3 = mksem("a3p")
        sE3 = mksem("a3e")
        dsS3 = DStream("a3s")

        @block.sync
        def _(sync):
            wf(sync, fvA3)
            for m in range(32):
                if m >= 2:
                    sync.wait_ge(sP3.h, m - 1)
                dsY.dma(sync,
                        ypTh[m % 2][:].rearrange("p (k n) -> p k n", k=16),
                        yph_h.rearrange("(k p) n -> p k n", p=128)
                        [:, :, m * 128:(m + 1) * 128])
                dsY.dma(sync,
                        ypTl[m % 2][:].rearrange("p (k n) -> p k n", k=16),
                        ypl_h.rearrange("(k p) n -> p k n", p=128)
                        [:, :, m * 128:(m + 1) * 128])

        @block.scalar
        def _(sc):
            wf(sc, fvA3)
            for m in range(32):
                if m >= 2:
                    sc.wait_ge(sE3.h, m - 1)
                dsXA.dma(sc, xch[m % 2][:], x_h[m * 128:(m + 1) * 128, :])
                dsXA.dma(sc, xnch[m % 2][:], xn_h[m * 128:(m + 1) * 128, :])

        @block.tensor
        def _(te):
            wf(te, fvA3)
            for m in range(32):
                dsY.wait_upto(te, 2 * m + 1)
                if m >= 2:
                    te.wait_ge(sE3.h, m - 1)
                psA = pb[2 * (m % 2)]
                psB = pb[2 * (m % 2) + 1]
                for k in range(16):
                    yh = ypTh[m % 2][:, k * 128:(k + 1) * 128]
                    yl = ypTl[m % 2][:, k * 128:(k + 1) * 128]
                    oh = owTh[:, k * 1024:(k + 1) * 1024]
                    ol = owTl[:, k * 1024:(k + 1) * 1024]
                    te.matmul(psA[:], yh, oh[:, 0:512], start=(k == 0), stop=False)
                    te.matmul(psB[:], yh, oh[:, 512:1024], start=(k == 0), stop=False)
                    te.matmul(psA[:], yh, ol[:, 0:512], start=False, stop=False)
                    te.matmul(psB[:], yh, ol[:, 512:1024], start=False, stop=False)
                    te.matmul(psA[:], yl, oh[:, 0:512], start=False, stop=(k == 15))
                    inst = te.matmul(psB[:], yl, oh[:, 512:1024], start=False,
                                     stop=(k == 15))
                inst.then_inc(sP3.h, 1)
            sP3.v = 32

        @block.gpsimd
        def _(gp):
            wf(gp, fvA3)
            for m in range(32):
                gp.wait_ge(sE3.h, m + 1)
                dsS3.dma(gp, acc_h[m * 128:(m + 1) * 128, :], accb[m % 2][:])

        @block.vector
        def _(v):
            wf(v, fvA3)
            for m in range(32):
                v.wait_ge(sP3.h, m + 1)
                dsXA.wait_upto(v, 2 * m + 1)
                if m >= 2:
                    dsS3.wait(v, m - 2)
                psA = pb[2 * (m % 2)]
                psB = pb[2 * (m % 2) + 1]
                v.tensor_add(preb[:], xch[m % 2][:], xnch[m % 2][:])
                v.drain()
                v.tensor_add(preb[:], preb[:], obrep[:])
                v.drain()
                v.tensor_add(accb[m % 2][:, 0:512], psA[:], preb[:, 0:512])
                v.drain()
                v.tensor_add(accb[m % 2][:, 512:1024], psB[:],
                             preb[:, 512:1024]).then_inc(sE3.h, 1)
                v.drain()
            sE3.v = 32

        fv_endA3 = fence_here(*dsS3.finals(), (s_ard, 112), (s_arc, 3))
        a3ctx.close()

        fctx = gctx.enter_context(ExitStack())
        idbf = mkbuf(fctx, "idbf", [128, 128], BF16)
        qdT = mkbuf(fctx, "qdT", [128, 22 * 8 * 128], BF16)
        f1ctx = fctx.enter_context(ExitStack())
        qwgT = mkbuf(f1ctx, "qwgT", [128, 8 * 22 * 128], BF16)
        qwuT = mkbuf(f1ctx, "qwuT", [128, 8 * 22 * 128], BF16)
        fprctx = f1ctx.enter_context(ExitStack())
        pr_qg = prep_weight(fprctx, "pqg", W["w_gate"], 22, D, [qwgT], fv_endA3,
                            mode="quant", layout="k", sw_dst=swg,
                            sw_mul_ap=sxrep[:, 0:1], ident=idbf,
                            extra_dma=[(idbf[:], identb_h[:])])
        pr_qu = prep_weight(fprctx, "pqu", W["w_up"], 22, D, [qwuT], fv_endA3,
                            mode="quant", layout="k", sw_dst=swu,
                            sw_mul_ap=sxrep[:, 0:1], ident=idbf)
        fvF1 = fence_here((pr_qg.sE, pr_qg.n), (pr_qu.sE, pr_qu.n))
        fprctx.close()

        xnf = [mkbuf(f1ctx, f"f1x{i}", [128, D]) for i in range(2)]
        q16b = mkbuf(f1ctx, "f1q16", [128, D], F16)
        qxb = [mkbuf(f1ctx, f"f1qx{i}", [128, D], BF16) for i in range(4)]
        qxT = mkbuf(f1ctx, "f1qxT", [128, 8 * TT], BF16)
        grb = [mkbuf(f1ctx, f"f1gr{i}", [128, TT]) for i in range(2)]
        slb = [mkbuf(f1ctx, f"f1sl{i}", [128, TT]) for i in range(2)]
        hb = [mkbuf(f1ctx, f"f1h{i}", [128, TT]) for i in range(2)]
        fst = mkbuf(f1ctx, "f1st", [128, 1])

        dsXF = DStream("f1x")
        sQ = mksem("f1q")
        sPF = mksem("f1p")
        sEF = mksem("f1e")
        sSL = mksem("f1sl")
        sHE = mksem("f1he")
        dsHS = DStream("f1hs")

        def f1_quant(v, t):
            if t > 0:
                v.wait_ge(sPF.h, 30 * (t - 1) + 8)
            for st in range(4):
                g = 4 * t + st
                dsXF.wait(v, g)
                v.tensor_scalar(q16b[:], xnf[g % 2][:], invsx[:, 0:1], MAGIC,
                                ALU.mult, ALU.add)
                v.drain()
                v.tensor_scalar(q16b[:], q16b[:], MAGIC - 128.0, MAGIC + 127.0,
                                ALU.max, ALU.min)
                v.drain()
                v.tensor_scalar(qxb[st][:], q16b[:], -MAGIC, None,
                                ALU.add).then_inc(sQ.h, 1)
                v.drain()

        def f1_tr(te, t):
            te.wait_ge(sQ.h, 4 * (t + 1))
            for dk in range(8):
                b = 8 * t + dk
                if b >= 2:
                    te.wait_ge(sEF.h, b - 1)
                bank = pb[6 + (b % 2)]
                for st in range(4):
                    inst = te.matmul(bank[:, st * 128:(st + 1) * 128],
                                     qxb[st][:, dk * 128:(dk + 1) * 128],
                                     idbf[:], start=(st == 0), stop=(st == 3))
                inst.then_inc(sPF.h, 1)

        def f1_trev(sc, t):
            for dk in range(8):
                b = 8 * t + dk
                sc.wait_ge(sPF.h, 30 * t + dk + 1)
                bank = pb[6 + (b % 2)]
                sc.copy(qxT[:, dk * 512:(dk + 1) * 512], bank[:]).then_inc(sEF.h, 1)
                sc.drain()

        def f1_mm(te, t):
            te.wait_ge(sEF.h, 8 * (t + 1))
            for i in range(22):
                gidx = 22 * t + i
                if gidx >= 2:
                    te.wait_ge(sHE.h, 2 * gidx - 2)
                psG = pb[2 * (i % 2)]
                psU = pb[2 * (i % 2) + 1]
                for wT, ps in ((qwgT, psG), (qwuT, psU)):
                    for k in range(8):
                        col = (k * 22 + i) * 128
                        inst = te.matmul(ps[:], wT[:, col:col + 128],
                                         qxT[:, k * 512:(k + 1) * 512],
                                         start=(k == 0), stop=(k == 7))
                inst.then_inc(sPF.h, 1)

        def f1_sil(sc, t):
            for i in range(22):
                gidx = 22 * t + i
                sc.wait_ge(sHE.h, 2 * gidx + 1)
                if gidx >= 2:
                    sc.wait_ge(sHE.h, 2 * gidx - 2)
                sc.activation(slb[i % 2][:], grb[i % 2][:], AF.Silu).then_inc(sSL.h, 1)
                sc.drain()

        def f1_hep(v, t):
            for i in range(22):
                gidx = 22 * t + i
                v.wait_ge(sPF.h, 30 * t + 8 + i + 1)
                if gidx >= 2:
                    v.wait_ge(sSL.h, gidx - 1)
                    dsHS.wait(v, gidx - 2)
                psG = pb[2 * (i % 2)]
                psU = pb[2 * (i % 2) + 1]
                v.tensor_scalar(grb[i % 2][:], psG[:], swg[:, i:i + 1], None,
                                ALU.mult).then_inc(sHE.h, 1)
                v.drain()
                v.wait_ge(sSL.h, gidx + 1)
                v.scalar_tensor_tensor(hb[i % 2][:], psU[:], swu[:, i:i + 1],
                                       slb[i % 2][:], ALU.mult, ALU.mult)
                v.drain()
                v.reduce_max(fst[:, 0:1], hb[i % 2][:], axis=AX.X,
                             apply_absolute_value=True)
                v.drain()
                v.tensor_tensor(mx_h[:], mx_h[:], fst[:, 0:1],
                                ALU.max).then_inc(sHE.h, 1)
                v.drain()

        @block.sync
        def _(sync):
            wf(sync, fvF1)
            for g in range(NST):
                if g >= 2:
                    sync.wait_ge(sQ.h, g - 1)
                dsXF.dma(sync, xnf[g % 2][:], xn_h[g * 128:(g + 1) * 128, :])

        @block.gpsimd
        def _(gp):
            wf(gp, fvF1)
            for t in range(NTT):
                for i in range(22):
                    gidx = 22 * t + i
                    gp.wait_ge(sHE.h, 2 * gidx + 2)
                    dsHS.dma(gp, hp_h[i * 128:(i + 1) * 128, t * TT:(t + 1) * TT],
                             hb[i % 2][:])

        @block.vector
        def _(v):
            wf(v, fvF1)
            f1_quant(v, 0)

        for t in range(NTT):
            @block.tensor
            def _(te, t=t):
                if t == 0:
                    wf(te, fvF1)
                f1_tr(te, t)
                f1_mm(te, t)

            @block.scalar
            def _(sc, t=t):
                if t == 0:
                    wf(sc, fvF1)
                    f1_trev(sc, 0)
                f1_sil(sc, t)
                if t + 1 < NTT:
                    f1_trev(sc, t + 1)

            @block.vector
            def _(v, t=t):
                if t + 1 < NTT:
                    f1_quant(v, t + 1)
                f1_hep(v, t)

        sQ.v = 4 * NTT
        sPF.v = 30 * NTT
        sEF.v = 8 * NTT
        sSL.v = 22 * NTT
        sHE.v = 2 * 22 * NTT

        fv_dp = fence_here(*dsHS.finals(), (sHE, sHE.v))
        f1ctx.close()

        swdrep = mkbuf(fctx, "swdrep", [128, D])
        dpctx = fctx.enter_context(ExitStack())
        pr_qd = prep_weight(dpctx, "pqd", W["w_down"], 8, IF, [qdT], fv_dp,
                            mode="quant", layout="k", sw_dst=swd8, ident=idbf)

        s_a2d = mksem("ar2d")
        s_a2c = mksem("ar2c")
        s_cc2 = mksem("cc2")

        @block.gpsimd
        def _(gp):
            gp.wait_ge(fence.h, fv_dp)
            gp.dma_start(out=scr_h[256:384][:, None], in_=mx_h[:],
                         allow_slow_non_contiguous=True).then_inc(s_a2d.h, 16)
            gp.wait_ge(s_a2d.h, 16)
            gp.dma_start(out=row128[:], in_=scr_h[256:384][None, :]).then_inc(s_a2d.h, 16)

        @block.vector
        def _(v):
            v.wait_ge(s_a2d.h, 32)
            v.reduce_max(sc4[0:1, 0:1], row128[:], axis=AX.X)
            v.drain()
            v.tensor_copy(row8[:], sc4[0:1, 0:1].broadcast_to([1, 8])).then_inc(s_a2c.h, 1)
            v.drain()

        @block.gpsimd
        def _(gp):
            gp.wait_ge(s_a2c.h, 1)
            gp.dma_start(out=cc_i[1][:], in_=row8[0:1, :]).then_inc(s_a2d.h, 16)
            gp.wait_ge(s_a2d.h, 48)
            gp.collective_compute("AllReduce", ALU.max,
                                  replica_groups=[list(range(NCORES))],
                                  ins=[cc_i[1][:]], outs=[cc_o[1][:]]).then_inc(s_cc2.h, 1)
            gp.wait_ge(s_cc2.h, 1)
            gp.dma_start(out=scal[0:1, 8:9],
                         in_=cc_o[1][0:1][None, :]).then_inc(s_a2d.h, 16)

        @block.scalar
        def _(sc):
            sc.wait_ge(s_a2d.h, 64)
            sc.activation(scal[0:1, 9:10], scal[0:1, 1:2],
                          AF.Sigmoid).then_inc(s_a2c.h, 1)
            sc.drain()

        @block.vector
        def _(v):
            v.wait_ge(s_a2c.h, 2)
            v.tensor_tensor(scal[0:1, 10:11], scal[0:1, 9:10], scal[0:1, 8:9], ALU.mult)
            v.drain()
            v.tensor_scalar(scal[0:1, 10:11], scal[0:1, 10:11], QI, None, ALU.mult)
            v.drain()
            v.reciprocal(scal[0:1, 11:12], scal[0:1, 10:11]).then_inc(s_a2c.h, 1)
            v.drain()

        @block.gpsimd
        def _(gp):
            gp.wait_ge(s_a2c.h, 3)
            gp.dma_start(out=scr_h[210:212][None, :],
                         in_=scal[0:1, 10:12]).then_inc(s_a2d.h, 16)
            gp.wait_ge(s_a2d.h, 80)
            gp.dma_start(out=sdrep[:], in_=scr_h[210:211][None, :]
                         .broadcast_to([128, 1])).then_inc(s_a2d.h, 16)
            gp.dma_start(out=invsd[:], in_=scr_h[211:212][None, :]
                         .broadcast_to([128, 1])).then_inc(s_a2d.h, 16)
            gp.wait_ge(s_a2d.h, 112)
            gp.wait_ge(pr_qd.sE.h, pr_qd.n)
            gp.dma_start(out=scr_h[1024:2048].rearrange("(c p) -> p c", p=128),
                         in_=swd8[:],
                         allow_slow_non_contiguous=True).then_inc(s_a2d.h, 16)
            gp.wait_ge(s_a2d.h, 128)
            gp.dma_start(out=swdrep[:], in_=scr_h[1024:2048][None, :]
                         .broadcast_to([128, D])).then_inc(s_a2d.h, 16)

        @block.vector
        def _(v):
            v.wait_ge(s_a2d.h, 144)
            v.tensor_scalar(swdrep[:], swdrep[:], sdrep[:, 0:1], None,
                            ALU.mult).then_inc(s_a2c.h, 1)
            v.drain()

        fvF2 = fence_here((s_a2d, 144), (s_a2c, 4))
        dpctx.close()

        hsl = [mkbuf(fctx, f"f2h{i}", [128, IF]) for i in range(2)]
        hq16 = mkbuf(fctx, "f2q16", [128, IF], F16)
        qhb = [mkbuf(fctx, f"f2qh{i}", [128, IF], BF16) for i in range(2)]
        accf = [mkbuf(fctx, f"f2a{i}", [128, D]) for i in range(2)]
        outb = [mkbuf(fctx, f"f2o{i}", [128, D]) for i in range(2)]

        dsHF = DStream("f2h")
        dsAF = DStream("f2a")
        sQF = mksem("f2q")
        sPD = mksem("f2p")
        sED = mksem("f2e")
        dsOUT = DStream("f2out")

        @block.sync
        def _(sync):
            wf(sync, fvF2)
            for m in range(32):
                if m >= 2:
                    sync.wait_ge(sQF.h, m - 1)
                dsHF.dma(sync,
                         hsl[m % 2][:].rearrange("p (k n) -> p k n", k=22),
                         hp_h.rearrange("(k p) n -> p k n", p=128)
                         [:, :, m * 128:(m + 1) * 128])

        @block.scalar
        def _(sc):
            wf(sc, fvF2)
            for m in range(32):
                if m >= 2:
                    sc.wait_ge(sED.h, m - 1)
                dsAF.dma(sc, accf[m % 2][:], acc_h[m * 128:(m + 1) * 128, :])

        def f2_ep(v, mm):
            v.wait_ge(sPD.h, mm + 1)
            dsAF.wait(v, mm)
            if mm >= 2:
                dsOUT.wait(v, mm - 2)
            psA = pb[2 * (mm % 2)]
            psB = pb[2 * (mm % 2) + 1]
            v.tensor_tensor(outb[mm % 2][:, 0:512], psA[:], swdrep[:, 0:512], ALU.mult)
            v.drain()
            v.tensor_tensor(outb[mm % 2][:, 512:1024], psB[:], swdrep[:, 512:1024],
                            ALU.mult)
            v.drain()
            v.tensor_add(outb[mm % 2][:], outb[mm % 2][:],
                         accf[mm % 2][:]).then_inc(sED.h, 1)
            v.drain()

        @block.gpsimd
        def _(gp):
            wf(gp, fvF2)
            for m in range(32):
                gp.wait_ge(sED.h, m + 1)
                dsOUT.dma(gp, out_h[m * 128:(m + 1) * 128, :], outb[m % 2][:])

        @block.vector
        def _(v):
            wf(v, fvF2)
            for m in range(32):
                dsHF.wait(v, m)
                if m >= 2:
                    v.wait_ge(sPD.h, m - 1)
                v.tensor_scalar(hq16[:], hsl[m % 2][:], invsd[:, 0:1], MAGIC,
                                ALU.mult, ALU.add)
                v.drain()
                v.tensor_scalar(hq16[:], hq16[:], MAGIC - 128.0, MAGIC + 127.0,
                                ALU.max, ALU.min)
                v.drain()
                v.tensor_scalar(qhb[m % 2][:], hq16[:], -MAGIC, None,
                                ALU.add).then_inc(sQF.h, 1)
                v.drain()
                if m >= 1:
                    f2_ep(v, m - 1)
            f2_ep(v, 31)
            sQF.v = 32
            sED.v = 32

        @block.tensor
        def _(te):
            wf(te, fvF2)
            for m in range(32):
                te.wait_ge(sQF.h, m + 1)
                if m >= 2:
                    te.wait_ge(sED.h, m - 1)
                psA = pb[2 * (m % 2)]
                psB = pb[2 * (m % 2) + 1]
                for k in range(22):
                    qh = qhb[m % 2][:, k * 128:(k + 1) * 128]
                    te.matmul(psA[:], qh, qdT[:, (k * 8) * 128:(k * 8) * 128 + 512],
                              start=(k == 0), stop=(k == 21))
                    inst = te.matmul(psB[:], qh,
                                     qdT[:, (k * 8 + 4) * 128:(k * 8 + 4) * 128 + 512],
                                     start=(k == 0), stop=(k == 21))
                inst.then_inc(sPD.h, 1)
            sPD.v = 32

        @block.sync
        def _(sync):
            for sm, v in dsOUT.finals():
                sync.wait_ge(sm.h, v)

        fctx.close()
    return nc


_NC_CACHE = {}


def kernel(L0_attn_x, L0_attn_norm_w, L0_attn_in_w, L0_attn_in_b, L0_attn_gate_w,
           L0_attn_gate_b, L0_attn_b_w, L0_attn_b_b, L0_attn_c_w, L0_attn_c_b,
           L0_attn_d_w, L0_attn_d_b, L0_attn_out_w, L0_attn_out_b, L0_attn_a_log,
           L0_ffn_clip_x, L0_ffn_clip_down, L0_ffn_w_gate, L0_ffn_w_up,
           L0_ffn_w_down):
    if "nc" not in _NC_CACHE:
        _NC_CACHE["nc"] = build_nc()
    nc = _NC_CACHE["nc"]
    x = np.ascontiguousarray(np.asarray(L0_attn_x, dtype=np.float32))
    common = {
        "identb": np.eye(128, dtype=ml_dtypes.bfloat16),
        "identf": np.eye(128, dtype=np.float32),
        "norm_w": np.ascontiguousarray(L0_attn_norm_w, dtype=np.float32),
        "in_w": np.ascontiguousarray(L0_attn_in_w, dtype=np.float32),
        "in_b": np.ascontiguousarray(L0_attn_in_b, dtype=np.float32),
        "gate_w": np.ascontiguousarray(L0_attn_gate_w, dtype=np.float32),
        "gate_b": np.ascontiguousarray(L0_attn_gate_b, dtype=np.float32),
        "b_w": np.ascontiguousarray(L0_attn_b_w, dtype=np.float32),
        "b_b": np.ascontiguousarray(L0_attn_b_b, dtype=np.float32),
        "c_w": np.ascontiguousarray(L0_attn_c_w, dtype=np.float32),
        "c_b": np.ascontiguousarray(L0_attn_c_b, dtype=np.float32),
        "d_w": np.ascontiguousarray(L0_attn_d_w, dtype=np.float32),
        "d_b": np.ascontiguousarray(L0_attn_d_b, dtype=np.float32),
        "out_w": np.ascontiguousarray(L0_attn_out_w, dtype=np.float32),
        "out_b": np.ascontiguousarray(L0_attn_out_b, dtype=np.float32),
        "a_log": np.ascontiguousarray(L0_attn_a_log, dtype=np.float32),
        "clip_x": np.ascontiguousarray(L0_ffn_clip_x, dtype=np.float32),
        "clip_down": np.ascontiguousarray(L0_ffn_clip_down, dtype=np.float32),
        "w_gate": np.ascontiguousarray(L0_ffn_w_gate, dtype=np.float32),
        "w_up": np.ascontiguousarray(L0_ffn_w_up, dtype=np.float32),
        "w_down": np.ascontiguousarray(L0_ffn_w_down, dtype=np.float32),
    }
    in_maps = []
    for c in range(NCORES):
        m = dict(common)
        m["x"] = np.ascontiguousarray(x[2 * c:2 * c + 2].reshape(T, D))
        in_maps.append(m)
    res = run_bass_kernel_spmd(nc, in_maps, core_ids=list(range(NCORES)))
    out = np.concatenate(
        [res.results[c]["out"].reshape(2, 2048, D) for c in range(NCORES)], axis=0)
    return np.ascontiguousarray(out.astype(np.float32))


# version 10
# speedup vs baseline: 1.1124x; 1.0002x over previous
import numpy as np
import ml_dtypes
import concourse.bass as bass
import concourse.mybir as mybir
from concourse.bass_utils import run_bass_kernel_spmd
from contextlib import ExitStack
from types import SimpleNamespace

F32 = mybir.dt.float32
F16 = mybir.dt.float16
BF16 = mybir.dt.bfloat16
AF = mybir.ActivationFunctionType
ALU = mybir.AluOpType
AX = mybir.AxisListType

NCORES = 8
D = 1024
I2 = 2048
IF = 2816
T = 4096
TT = 512
NTT = 8
NST = 32
MAGIC = 1536.0
R1024 = 1.0 / 1024.0
QI = 1.0 / 127.0


def build_nc():
    nc = bass.Bass("TRN2", target_bir_lowering=False, debug=False)

    x_h = nc.dram_tensor("x", [T, D], F32, kind="ExternalInput").ap()
    identb_h = nc.dram_tensor("identb", [128, 128], BF16, kind="ExternalInput").ap()
    identf_h = nc.dram_tensor("identf", [128, 128], F32, kind="ExternalInput").ap()
    W = {}
    for nm, shape in [
        ("norm_w", [D]), ("in_w", [I2, D]), ("in_b", [I2]),
        ("gate_w", [I2, D]), ("gate_b", [I2]),
        ("b_w", [I2, I2]), ("b_b", [I2]), ("c_w", [I2, I2]), ("c_b", [I2]),
        ("d_w", [I2, I2]), ("d_b", [I2]),
        ("out_w", [D, I2]), ("out_b", [D]), ("a_log", [I2]),
        ("clip_x", [1]), ("clip_down", [1]),
        ("w_gate", [IF, D]), ("w_up", [IF, D]), ("w_down", [D, IF]),
    ]:
        W[nm] = nc.dram_tensor(nm, shape, F32, kind="ExternalInput").ap()
    out_h = nc.dram_tensor("out", [T, D], F32, kind="ExternalOutput").ap()

    xn_h = nc.dram_tensor("xn_i", [T, D], F32, kind="Internal").ap()
    uhi_h = nc.dram_tensor("uhi_i", [I2, T], BF16, kind="Internal").ap()
    ulo_h = nc.dram_tensor("ulo_i", [I2, T], BF16, kind="Internal").ap()
    c_hb = nc.dram_tensor("c_i", [I2, T], F32, kind="Internal").ap()
    d_hb = nc.dram_tensor("d_i", [I2, T], F32, kind="Internal").ap()
    yph_h = nc.dram_tensor("yph_i", [I2, T], BF16, kind="Internal").ap()
    ypl_h = nc.dram_tensor("ypl_i", [I2, T], BF16, kind="Internal").ap()
    hp_h = nc.dram_tensor("hp_i", [IF, T], F32, kind="Internal").ap()
    acc_h = nc.dram_tensor("acc_i", [T, D], F32, kind="Internal").ap()
    scr_h = nc.dram_tensor("scr_i", [4096], F32, kind="Internal").ap()
    cc_i = [nc.dram_tensor(f"cci{k}", [8], F32, kind="Internal").ap() for k in range(2)]
    cc_o = [nc.dram_tensor(f"cco{k}", [8], F32, kind="Internal").ap() for k in range(2)]

    gctx = ExitStack()
    with gctx:
        def mkbuf(ctx, name, shape, dt=F32):
            return ctx.enter_context(nc.sbuf_tensor(name, shape, dt))

        _semno = [0]

        def mksem(tag):
            _semno[0] += 1
            return SimpleNamespace(
                h=gctx.enter_context(nc.semaphore(f"s{_semno[0]}_{tag}")), v=0)

        _shared = {}

        def shsem(tag):
            if tag not in _shared:
                _shared[tag] = mksem(tag)
            return _shared[tag]

        class DStream:

            def __init__(self, tag):
                self.s = [mksem(tag + "0"), mksem(tag + "1")]
                self.chk = []

            def dma(self, eng, out, in_, **kw):
                sm = self.s[len(self.chk) % 2]
                if sm.v >= 16:
                    eng.wait_ge(sm.h, sm.v)
                eng.dma_start(out=out, in_=in_, **kw).then_inc(sm.h, 16)
                sm.v += 16
                self.chk.append((sm, sm.v))
                return len(self.chk) - 1

            def wait(self, eng, idx):
                sm, v = self.chk[idx]
                eng.wait_ge(sm.h, v)

            def wait_upto(self, eng, idx):
                assert idx < len(self.chk), (idx, len(self.chk))
                best = {}
                for sm, v in self.chk[:idx + 1]:
                    best[id(sm)] = (sm, v)
                for sm, v in best.values():
                    eng.wait_ge(sm.h, v)

            def finals(self):
                best = {}
                for sm, v in self.chk:
                    best[id(sm)] = (sm, v)
                return list(best.values())

        identf = mkbuf(gctx, "identf_s", [128, 128], F32)
        bias_bcd = mkbuf(gctx, "bias_bcd", [128, 48])
        a_sb = mkbuf(gctx, "a_sb", [128, 16])
        _smalls = mkbuf(gctx, "smalls", [128, 80])
        swg = _smalls[:, 0:22]
        swu = _smalls[:, 22:44]
        swd8 = _smalls[:, 44:52]
        state = _smalls[:, 52:68]
        invsx = _smalls[:, 68:69]
        invsd = _smalls[:, 69:70]
        sxrep = _smalls[:, 70:71]
        sdrep = _smalls[:, 71:72]
        mx_xn = _smalls[:, 72:73]
        mx_h = _smalls[:, 73:74]
        _rows = mkbuf(gctx, "rowsml", [1, 160])
        scal = _rows[:, 0:16]
        row128 = _rows[:, 16:144]
        row8 = _rows[:, 144:152]
        sc4 = _rows[:, 152:156]
        a1ctx = gctx.enter_context(ExitStack())
        bias_ig = mkbuf(a1ctx, "bias_ig", [128, 32])
        identb = mkbuf(a1ctx, "identb_s", [128, 128], BF16)

        pb = [gctx.enter_context(nc.psum_tensor(f"pb{i}", [128, 512], F32))
              for i in range(8)]
        block = gctx.enter_context(nc.Block())
        fence = mksem("fence")

        def fence_here(*wait_list):
            fence.v += 1
            fv = fence.v

            @block.gpsimd
            def _(g):
                for s, v in wait_list:
                    g.wait_ge(s.h, v)
                g.sem_inc(fence.h, 1)
            return fv

        def wf(eng, fv):
            eng.wait_ge(fence.h, fv)

        ds_const = DStream("const")
        s_p0a = mksem("p0a")
        s_p0v = mksem("p0v")

        @block.sync
        def _(sync):
            for dst, src in [
                (identb[:], identb_h[:]),
                (identf[:], identf_h[:]),
                (bias_ig[:, 0:16], W["in_b"].rearrange("(c p) -> p c", p=128)),
                (bias_ig[:, 16:32], W["gate_b"].rearrange("(c p) -> p c", p=128)),
                (bias_bcd[:, 0:16], W["b_b"].rearrange("(c p) -> p c", p=128)),
                (bias_bcd[:, 16:32], W["c_b"].rearrange("(c p) -> p c", p=128)),
                (bias_bcd[:, 32:48], W["d_b"].rearrange("(c p) -> p c", p=128)),
                (a_sb[:], W["a_log"].rearrange("(c p) -> p c", p=128)),
                (scal[0:1, 0:1], W["clip_x"][None, :]),
                (scal[0:1, 1:2], W["clip_down"][None, :]),
            ]:
                ds_const.dma(sync, dst, src, allow_slow_non_contiguous=True)

        @block.scalar
        def _(sc):
            for sm, v in ds_const.finals():
                sc.wait_ge(sm.h, v)
            sc.activation(a_sb[:], a_sb[:], AF.Sigmoid, scale=-1.0).then_inc(s_p0a.h, 1)
        s_p0a.v = 1

        @block.vector
        def _(v):
            v.memset(mx_xn[:], 0.0)
            v.drain()
            v.memset(mx_h[:], 0.0).then_inc(s_p0v.h, 1)
        s_p0v.v = 1
        fv0 = fence_here(*ds_const.finals(), (s_p0a, 1), (s_p0v, 1))

        def prep_weight(ctx, tag, src_ap, n_oc, Kdim, dsts, fv, mode="hilo",
                        layout="oc", fold_rep=None, row_off=0, sw_dst=None,
                        sw_mul_ap=None, extra_dma=(), scratch=None,
                        load_engine="sync", pe_guard=None, ident=None):
            nk = Kdim // 128
            grp = "B" if tag in ("pga", "pqu") else "A"
            lkey = "prLG" if load_engine == "gpsimd" else "prL" + grp
            if lkey not in _shared:
                _shared[lkey] = DStream(lkey)
            dsL = _shared[lkey]
            sS = shsem("prS" + grp)
            sP = shsem("prP" + grp)
            sE = shsem("prE" + grp)
            bS, bP, bE = sS.v, sP.v, sE.v
            pb_base = 6 if grp == "A" else 4
            ld_idx = {}
            nsrc = 2 if mode == "hilo" else 1
            if scratch is None:
                wsl = [mkbuf(ctx, f"{tag}_w{i}", [128, Kdim]) for i in range(2)]
                bsl = [[mkbuf(ctx, f"{tag}_b{j}{i}", [128, Kdim], BF16)
                        for i in range(2)] for j in range(nsrc)]
                q16 = (mkbuf(ctx, f"{tag}_q16", [128, Kdim], F16)
                       if mode == "quant" else None)
                rst = mkbuf(ctx, f"{tag}_rst", [128, 2]) if mode == "quant" else None
            else:
                wsl, bsl, q16, rst = scratch
            kbs = [(k0, min(4, nk - k0)) for k0 in range(0, nk, 4)]
            nbb = len(kbs) * nsrc
            n_batches = n_oc * nbb

            def emit_loads(eng):
                if fv is not None:
                    wf(eng, fv)
                for dst, src in extra_dma:
                    ld_idx.setdefault("x", []).append(dsL.dma(eng, dst, src))
                for s in range(n_oc):
                    if s >= 2:
                        eng.wait_ge(sS.h, bS + s - 1)
                    ld_idx[s] = dsL.dma(
                        eng, wsl[s % 2][:],
                        src_ap[row_off + s * 128: row_off + (s + 1) * 128, :])

            if load_engine == "sync":
                @block.sync
                def _(sync):
                    emit_loads(sync)
            elif load_engine == "gpsimd":
                @block.gpsimd
                def _(gp):
                    emit_loads(gp)
            else:
                @block.scalar
                def _(sc):
                    emit_loads(sc)

            @block.vector
            def _(v):
                if fv is not None:
                    wf(v, fv)
                for s in range(n_oc):
                    dsL.wait(v, ld_idx[s])
                    if s >= 2:
                        v.wait_ge(sP.h, bP + (s - 1) * nbb)
                    elif bP > 0:
                        v.wait_ge(sP.h, bP)
                    if mode == "hilo":
                        if fold_rep is not None:
                            for xi in ld_idx.get("x", []):
                                dsL.wait(v, xi)
                            v.tensor_tensor(wsl[s % 2][:], wsl[s % 2][:], fold_rep,
                                            ALU.mult)
                            v.drain()
                        v.tensor_copy(bsl[0][s % 2][:], wsl[s % 2][:])
                        v.drain()
                        v.tensor_sub(bsl[1][s % 2][:], wsl[s % 2][:],
                                     bsl[0][s % 2][:]).then_inc(sS.h, 1)
                        v.drain()
                    else:
                        v.reduce_max(rst[:, 0:1], wsl[s % 2][:], axis=AX.X,
                                     apply_absolute_value=True)
                        v.drain()
                        v.reciprocal(rst[:, 1:2], rst[:, 0:1])
                        v.drain()
                        v.tensor_scalar(rst[:, 1:2], rst[:, 1:2], 127.0, None, ALU.mult)
                        v.drain()
                        v.tensor_scalar(q16[:], wsl[s % 2][:], rst[:, 1:2], MAGIC,
                                        ALU.mult, ALU.add)
                        v.drain()
                        v.tensor_scalar(bsl[0][s % 2][:], q16[:], -MAGIC, None, ALU.add)
                        v.drain()
                        if sw_mul_ap is not None:
                            v.tensor_scalar(sw_dst[:, s:s + 1], rst[:, 0:1],
                                            sw_mul_ap, QI, ALU.mult,
                                            ALU.mult).then_inc(sS.h, 1)
                        else:
                            v.tensor_scalar(sw_dst[:, s:s + 1], rst[:, 0:1],
                                            QI, None, ALU.mult).then_inc(sS.h, 1)
                        v.drain()
                sS.v = bS + n_oc

            @block.tensor
            def _(te):
                if fv is not None:
                    wf(te, fv)
                if pe_guard is not None:
                    te.wait_ge(pe_guard[0].h, pe_guard[1])
                for xi in ld_idx.get("x", []):
                    dsL.wait(te, xi)
                b = 0
                for s in range(n_oc):
                    te.wait_ge(sS.h, bS + s + 1)
                    for j in range(nsrc):
                        for (k0, kn) in kbs:
                            if b >= 2:
                                te.wait_ge(sE.h, bE + b - 1)
                            bank = pb[pb_base + (b % 2)]
                            idt = ident if ident is not None else identb
                            for c in range(kn):
                                inst = te.matmul(
                                    bank[:, c * 128:(c + 1) * 128],
                                    bsl[j][s % 2][:, (k0 + c) * 128:(k0 + c + 1) * 128],
                                    idt[:], start=(c == 0), stop=(c == kn - 1))
                            inst.then_inc(sP.h, 1)
                            b += 1
                sP.v = bP + n_batches

            @block.scalar
            def _(scx):
                if fv is not None:
                    wf(scx, fv)
                b = 0
                for s in range(n_oc):
                    for j in range(nsrc):
                        for (k0, kn) in kbs:
                            scx.wait_ge(sP.h, bP + b + 1)
                            bank = pb[pb_base + (b % 2)]
                            dst = dsts[j]
                            if layout == "oc":
                                col0 = (s * nk + k0) * 128
                                scx.copy(dst[:, col0:col0 + kn * 128],
                                         bank[:, 0:kn * 128]).then_inc(sE.h, 1)
                            else:
                                dv = dst.rearrange("p (k s2 c) -> p k s2 c",
                                                   k=nk, s2=n_oc)
                                scx.copy(dv[:, k0:k0 + kn, s, :],
                                         bank[:, 0:kn * 128].rearrange(
                                             "p (a c) -> p a c", a=kn)).then_inc(sE.h, 1)
                            scx.drain()
                            b += 1
                sE.v = bE + n_batches
            return SimpleNamespace(sP=sP, sE=sE, n=bE + n_batches)

        inTh = mkbuf(a1ctx, "inTh", [128, 16 * 8 * 128], BF16)
        inTl = mkbuf(a1ctx, "inTl", [128, 16 * 8 * 128], BF16)
        gaTh = mkbuf(a1ctx, "gaTh", [128, 16 * 8 * 128], BF16)
        gaTl = mkbuf(a1ctx, "gaTl", [128, 16 * 8 * 128], BF16)

        prctx = a1ctx.enter_context(ExitStack())
        nwrep = mkbuf(prctx, "nwrep", [128, D])
        pr_in = prep_weight(prctx, "pin", W["in_w"], 16, D, [inTh, inTl], fv0,
                            fold_rep=nwrep[:],
                            extra_dma=[(nwrep[:],
                                        W["norm_w"][None, :].broadcast_to([128, D]))])
        pr_ga = prep_weight(prctx, "pga", W["gate_w"], 16, D, [gaTh, gaTl], fv0,
                            fold_rep=nwrep[:], load_engine="gpsimd")
        fvA1 = fence_here((pr_in.sE, pr_in.n), (pr_ga.sE, pr_ga.n))
        prctx.close()

        xb = [mkbuf(a1ctx, f"a1x{i}", [128, D]) for i in range(2)]
        xcb = [mkbuf(a1ctx, f"a1xc{i}", [128, D]) for i in range(2)]
        x2h = [mkbuf(a1ctx, f"a1h{i}", [128, D], BF16) for i in range(4)]
        x2l = [mkbuf(a1ctx, f"a1l{i}", [128, D], BF16) for i in range(4)]
        xTh = [mkbuf(a1ctx, f"a1xTh{i}", [128, 8 * 512], BF16) for i in range(2)]
        xTl = [mkbuf(a1ctx, f"a1xTl{i}", [128, 8 * 512], BF16) for i in range(2)]
        st1 = mkbuf(a1ctx, "a1st1", [128, 4])
        st2 = mkbuf(a1ctx, "a1st2", [128, 2])
        sq2 = mkbuf(a1ctx, "a1sq2", [128, 2])
        rsy = mkbuf(a1ctx, "a1rsy", [128, 2])
        rtmp = mkbuf(a1ctx, "a1rt", [128, 2])
        ubf = [mkbuf(a1ctx, f"a1u{i}", [128, TT]) for i in range(2)]
        gbf = [mkbuf(a1ctx, f"a1g{i}", [128, TT]) for i in range(2)]
        uhb = [mkbuf(a1ctx, f"a1uh{i}", [128, TT], BF16) for i in range(2)]
        ulb = [mkbuf(a1ctx, f"a1ul{i}", [128, TT], BF16) for i in range(2)]

        dsX = DStream("a1x")
        sDV = mksem("a1dv")
        sAC = mksem("a1ac")
        sPE = mksem("a1pe")
        sPT = mksem("a1pt")
        sEV = mksem("a1ev")
        sSG = mksem("a1sg")
        sUE = mksem("a1ue")
        dsXS = DStream("a1xs")
        dsUS = DStream("a1us")

        def a1_sub_st(v, t, st):
            if t > 0 and st == 0:
                v.wait_ge(sPT.h, 16 * (t - 1) + 16)
            g = 4 * t + st
            dsX.wait(v, g)
            if g >= 2:
                dsXS.wait(v, g - 2)
            v.reduce_sum(st1[:, 0:1], xb[g % 2][:], axis=AX.X)
            v.drain()
            v.tensor_scalar(st1[:, 1:2], st1[:, 0:1], R1024, None, ALU.mult)
            v.drain()
            v.tensor_scalar(xcb[g % 2][:], xb[g % 2][:], st1[:, 1:2], None,
                            ALU.subtract).then_inc(sDV.h, 1)
            v.drain()
            v.wait_ge(sAC.h, 2 * g + 1)
            v.tensor_scalar(st2[:, 0:1], st1[:, 2:3], R1024, 1e-5, ALU.mult,
                            ALU.add)
            v.drain()
            v.tensor_scalar(st1[:, 3:4], st1[:, 2:3], R1024, None, ALU.mult)
            v.drain()
            v.reciprocal(rtmp[:, 0:1], st2[:, 0:1])
            v.drain()
            v.tensor_tensor(st2[:, 1:2], st1[:, 3:4], rtmp[:, 0:1], ALU.mult)
            v.drain()
            v.tensor_scalar(st2[:, 1:2], st2[:, 1:2], 1e-6, None,
                            ALU.add).then_inc(sDV.h, 1)
            v.drain()
            v.wait_ge(sAC.h, 2 * g + 2)
            v.reciprocal(rsy[:], sq2[:])
            v.drain()
            v.tensor_tensor(rtmp[:], rsy[:], rsy[:], ALU.mult)
            v.drain()
            v.tensor_tensor(rtmp[:], rtmp[:], st2[:], ALU.mult)
            v.drain()
            v.tensor_scalar(rtmp[:], rtmp[:], -0.5, 1.5, ALU.mult, ALU.add)
            v.drain()
            v.tensor_tensor(rsy[:], rsy[:], rtmp[:], ALU.mult)
            v.drain()
            v.tensor_scalar(xcb[g % 2][:], xcb[g % 2][:], rsy[:, 0:1], None,
                            ALU.mult)
            v.drain()
            v.reduce_max(st1[:, 0:1], xcb[g % 2][:], axis=AX.X,
                         apply_absolute_value=True)
            v.drain()
            v.tensor_tensor(mx_xn[:], mx_xn[:], st1[:, 0:1], ALU.max)
            v.drain()
            v.tensor_scalar(x2h[st][:], xcb[g % 2][:], rsy[:, 1:2], None, ALU.mult)
            v.drain()
            v.scalar_tensor_tensor(x2l[st][:], xcb[g % 2][:], rsy[:, 1:2],
                                   x2h[st][:], ALU.mult,
                                   ALU.subtract).then_inc(sDV.h, 1)
            v.drain()

        def a1_acts_st(sc, t, st):
            g = 4 * t + st
            sc.wait_ge(sDV.h, 3 * g + 1)
            sc.activation(xb[g % 2][:], xcb[g % 2][:], AF.Square,
                          accum_out=st1[:, 2:3]).then_inc(sAC.h, 1)
            sc.drain()
            sc.wait_ge(sDV.h, 3 * g + 2)
            sc.activation(sq2[:], st2[:], AF.Sqrt).then_inc(sAC.h, 1)
            sc.drain()

        def a1_tr(te, t, bs):
            for bl in bs:
                dk, j = bl // 2, bl % 2
                src = x2h if j == 0 else x2l
                b = 16 * t + bl
                if bl == 0:
                    te.wait_ge(sDV.h, 3 * (4 * t + 3) + 3)
                if b >= 2:
                    te.wait_ge(sEV.h, b - 1)
                bank = pb[6 + (b % 2)]
                for st in range(4):
                    inst = te.matmul(bank[:, st * 128:(st + 1) * 128],
                                     src[st][:, dk * 128:(dk + 1) * 128],
                                     identb[:], start=(st == 0), stop=(st == 3))
                inst.then_inc(sPT.h, 1)

        def a1_trev(sc, t, bs):
            for bl in bs:
                dk, j = bl // 2, bl % 2
                dst = xTh[t % 2] if j == 0 else xTl[t % 2]
                b = 16 * t + bl
                sc.wait_ge(sPT.h, b + 1)
                bank = pb[6 + (b % 2)]
                sc.copy(dst[:, dk * 512:(dk + 1) * 512], bank[:]).then_inc(sEV.h, 1)
                sc.drain()

        def a1_mm(te, t):
            te.wait_ge(sEV.h, 16 * (t + 1))
            for i in range(16):
                gidx = 16 * t + i
                if gidx >= 2:
                    te.wait_ge(sUE.h, gidx - 1)
                psI = pb[2 * (i % 2)]
                psG = pb[2 * (i % 2) + 1]
                for wh, wl, ps in ((inTh, inTl, psI), (gaTh, gaTl, psG)):
                    for k in range(8):
                        col = (i * 8 + k) * 128
                        te.matmul(ps[:], wh[:, col:col + 128],
                                  xTh[t % 2][:, k * 512:(k + 1) * 512],
                                  start=(k == 0), stop=False)
                        te.matmul(ps[:], wh[:, col:col + 128],
                                  xTl[t % 2][:, k * 512:(k + 1) * 512],
                                  start=False, stop=False)
                        inst = te.matmul(ps[:], wl[:, col:col + 128],
                                         xTh[t % 2][:, k * 512:(k + 1) * 512],
                                         start=False, stop=(k == 7))
                inst.then_inc(sPE.h, 1)
                if t + 1 < NTT and i >= 8:
                    a1_tr(te, t + 1, [2 * (i - 8), 2 * (i - 8) + 1])

        def a1_sig(sc, t):
            for i in range(16):
                gidx = 16 * t + i
                sc.wait_ge(sPE.h, 16 * t + i + 1)
                if gidx >= 2:
                    sc.wait_ge(sUE.h, gidx - 1)
                psG = pb[2 * (i % 2) + 1]
                sc.activation(gbf[i % 2][:], psG[:], AF.Sigmoid,
                              bias=bias_ig[:, 16 + i:17 + i]).then_inc(sSG.h, 1)
                sc.drain()
                if t + 1 < NTT and i in (1, 3, 5, 7):
                    a1_acts_st(sc, t + 1, (i - 1) // 2)
                if t + 1 < NTT and i >= 8:
                    a1_trev(sc, t + 1, [2 * (i - 8), 2 * (i - 8) + 1])

        def a1_uep(v, t):
            for i in range(16):
                gidx = 16 * t + i
                v.wait_ge(sSG.h, gidx + 1)
                if gidx >= 2:
                    dsUS.wait_upto(v, 2 * (gidx - 2) + 1)
                psI = pb[2 * (i % 2)]
                v.scalar_tensor_tensor(ubf[i % 2][:], psI[:], bias_ig[:, i:i + 1],
                                       gbf[i % 2][:], ALU.add, ALU.mult)
                v.drain()
                v.tensor_copy(uhb[i % 2][:], ubf[i % 2][:])
                v.drain()
                v.tensor_sub(ulb[i % 2][:], ubf[i % 2][:],
                             uhb[i % 2][:]).then_inc(sUE.h, 1)
                v.drain()
                if t + 1 < NTT and i in (1, 3, 5, 7):
                    a1_sub_st(v, t + 1, (i - 1) // 2)

        @block.sync
        def _(sync):
            wf(sync, fvA1)
            for g in range(NST):
                if g >= 2:
                    sync.wait_ge(sAC.h, 2 * (g - 2) + 1)
                dsX.dma(sync, xb[g % 2][:], x_h[g * 128:(g + 1) * 128, :])

        def a1_gp_xn_st(gp, t, st):
            g = 4 * t + st
            gp.wait_ge(sDV.h, 3 * g + 3)
            dsXS.dma(gp, xn_h[g * 128:(g + 1) * 128, :], xcb[g % 2][:])

        def a1_gp_u(gp, t):
            for i in range(16):
                gidx = 16 * t + i
                gp.wait_ge(sUE.h, gidx + 1)
                dsUS.dma(gp, uhi_h[i * 128:(i + 1) * 128, t * TT:(t + 1) * TT],
                         uhb[i % 2][:])
                dsUS.dma(gp, ulo_h[i * 128:(i + 1) * 128, t * TT:(t + 1) * TT],
                         ulb[i % 2][:])
                if t + 1 < NTT and i in (1, 3, 5, 7):
                    a1_gp_xn_st(gp, t + 1, (i - 1) // 2)

        @block.gpsimd
        def _(gp):
            for st in range(4):
                a1_gp_xn_st(gp, 0, st)
            for t in range(NTT):
                a1_gp_u(gp, t)

        @block.vector
        def _(v):
            wf(v, fvA1)
            for st in range(4):
                a1_sub_st(v, 0, st)

        @block.scalar
        def _(sc):
            wf(sc, fvA1)
            for st in range(4):
                a1_acts_st(sc, 0, st)

        for t in range(NTT):
            @block.tensor
            def _(te, t=t):
                if t == 0:
                    wf(te, fvA1)
                    a1_tr(te, 0, list(range(16)))
                a1_mm(te, t)

            @block.scalar
            def _(sc, t=t):
                if t == 0:
                    a1_trev(sc, 0, list(range(16)))
                a1_sig(sc, t)

            @block.vector
            def _(v, t=t):
                a1_uep(v, t)

        sDV.v = 3 * NST
        sAC.v = 2 * NST
        sPE.v = 16 * NTT
        sPT.v = 16 * NTT
        sEV.v = 16 * NTT
        sSG.v = 16 * NTT
        sUE.v = 16 * NTT

        fvAR1 = fence_here(*dsUS.finals(), *dsXS.finals(), (sUE, sUE.v))
        a1ctx.close()

        s_ard = mksem("ard")
        s_arc = mksem("arc")
        s_cc1 = mksem("cc1")

        @block.gpsimd
        def _(gp):
            gp.wait_ge(fence.h, fvAR1)
            gp.dma_start(out=scr_h[0:128][:, None], in_=mx_xn[:],
                         allow_slow_non_contiguous=True).then_inc(s_ard.h, 16)
            gp.wait_ge(s_ard.h, 16)
            gp.dma_start(out=row128[:], in_=scr_h[0:128][None, :]).then_inc(s_ard.h, 16)

        @block.vector
        def _(v):
            v.wait_ge(s_ard.h, 32)
            v.reduce_max(sc4[0:1, 0:1], row128[:], axis=AX.X)
            v.drain()
            v.tensor_copy(row8[:], sc4[0:1, 0:1].broadcast_to([1, 8])).then_inc(s_arc.h, 1)
            v.drain()

        @block.gpsimd
        def _(gp):
            gp.wait_ge(s_arc.h, 1)
            gp.dma_start(out=cc_i[0][:], in_=row8[0:1, :]).then_inc(s_ard.h, 16)
            gp.wait_ge(s_ard.h, 48)
            gp.collective_compute("AllReduce", ALU.max,
                                  replica_groups=[list(range(NCORES))],
                                  ins=[cc_i[0][:]], outs=[cc_o[0][:]]).then_inc(s_cc1.h, 1)
            gp.wait_ge(s_cc1.h, 1)
            gp.dma_start(out=scal[0:1, 2:3],
                         in_=cc_o[0][0:1][None, :]).then_inc(s_ard.h, 16)

        @block.scalar
        def _(sc):
            sc.wait_ge(s_ard.h, 64)
            sc.activation(scal[0:1, 4:5], scal[0:1, 0:1],
                          AF.Sigmoid).then_inc(s_arc.h, 1)
            sc.drain()

        @block.vector
        def _(v):
            v.wait_ge(s_arc.h, 2)
            v.tensor_tensor(scal[0:1, 5:6], scal[0:1, 4:5], scal[0:1, 2:3], ALU.mult)
            v.drain()
            v.tensor_scalar(scal[0:1, 5:6], scal[0:1, 5:6], QI, None, ALU.mult)
            v.drain()
            v.reciprocal(scal[0:1, 6:7], scal[0:1, 5:6]).then_inc(s_arc.h, 1)
            v.drain()

        @block.gpsimd
        def _(gp):
            gp.wait_ge(s_arc.h, 3)
            gp.dma_start(out=scr_h[200:202][None, :],
                         in_=scal[0:1, 5:7]).then_inc(s_ard.h, 16)
            gp.wait_ge(s_ard.h, 80)
            gp.dma_start(out=sxrep[:], in_=scr_h[200:201][None, :]
                         .broadcast_to([128, 1])).then_inc(s_ard.h, 16)
            gp.dma_start(out=invsx[:], in_=scr_h[201:202][None, :]
                         .broadcast_to([128, 1])).then_inc(s_ard.h, 16)
        s_ard.v = 112
        s_arc.v = 3

        a2ctx = gctx.enter_context(ExitStack())
        wThs = [mkbuf(a2ctx, f"wTh{i}", [128, 16 * 8 * 128], BF16) for i in range(2)]
        wTls = [mkbuf(a2ctx, f"wTl{i}", [128, 16 * 8 * 128], BF16) for i in range(2)]
        uTh = [mkbuf(a2ctx, f"uTh{i}", [128, 16 * TT], BF16) for i in range(2)]
        uTl = [mkbuf(a2ctx, f"uTl{i}", [128, 16 * TT], BF16) for i in range(2)]
        obuf = [mkbuf(a2ctx, "a2ob", [128, TT])] * 2
        cbuf = [mkbuf(a2ctx, f"a2c{i}", [128, TT]) for i in range(2)]
        dbuf = [mkbuf(a2ctx, f"a2d{i}", [128, TT]) for i in range(2)]
        ypbh = [mkbuf(a2ctx, "a2yh0", [128, TT], BF16)]
        ypbl = [mkbuf(a2ctx, "a2yl0", [128, TT], BF16)]
        a2_wsl = mkbuf(a2ctx, "a2pw", [128, 512])

        dsU = DStream("a2u")
        dsCD = DStream("a2cd")
        dsST = DStream("a2st")
        dsPL = DStream("a2pl")
        dsPG = DStream("a2pg")
        sPP = mksem("a2pp")
        sEvH = mksem("a2eh")
        sPEv = mksem("a2pe")
        sMP = mksem("a2mp")
        sME = mksem("a2me")
        sCDu = mksem("a2cdu")

        A2SPECS = [("c", 0), ("c", 1), ("d", 0), ("d", 1), ("b", 0), ("b", 1)]
        prep_ld_idx = [dict() for _ in range(6)]

        def a2_prep_load1(eng, k, h, ds=None, scr=None, step=1):
            kind, half = A2SPECS[k]
            wsrc = {"c": W["c_w"], "d": W["d_w"], "b": W["b_w"]}[kind]
            bP = 32 * k
            si, q = h // 4, h % 4
            if ds is None:
                ds, scr = dsPL, a2_wsl
            if h >= step:
                eng.wait_ge(sPP.h, bP + h - step + 1)
            prep_ld_idx[k][h] = (ds, ds.dma(
                eng, scr[:],
                wsrc[half * 1024 + si * 128: half * 1024 + (si + 1) * 128,
                     q * 512:(q + 1) * 512]), scr)

        def a2_prep_hi1(sc, k, h):
            bP = 32 * k
            si, q = h // 4, h % 4
            sc.wait_ge(sPP.h, bP + h + 1)
            bank = pb[6 + (h % 2)]
            col0 = (si * 16 + q * 4) * 128
            sc.copy(wThs[k % 2][:, col0:col0 + 512],
                    bank[:]).then_inc(sEvH.h, 1)
            sc.drain()

        def a2_prep_lo1(v, k, h):
            bE = 32 * k
            si, q = h // 4, h % 4
            v.wait_ge(sEvH.h, bE + h + 1)
            bank = pb[6 + (h % 2)]
            col0 = (si * 16 + q * 4) * 128
            v.tensor_sub(wTls[k % 2][:, col0:col0 + 512], bank[:],
                         wThs[k % 2][:, col0:col0 + 512]).then_inc(sPEv.h, 1)
            v.drain()

        def a2_prep_actwork(sc, k, first=False):
            if first:
                wf(sc, fvAR1)
            if k == 0:
                a2_prep_load1(sc, k, 0, dsPL, a2_wsl, step=2)
                for h in range(32):
                    a2_prep_hi1(sc, k, h)
                    if h + 2 < 32 and (h + 2) % 2 == 0:
                        a2_prep_load1(sc, k, h + 2, dsPL, a2_wsl, step=2)
            else:
                a2_prep_load1(sc, k, 0)
                a2_prep_load1(sc, k, 1)
                for h in range(32):
                    a2_prep_hi1(sc, k, h)
                    if h + 2 < 32:
                        a2_prep_load1(sc, k, h + 2)

        def a2_prep_los(v, k, hs, first=False):
            if first:
                wf(v, fvAR1)
            for h in hs:
                a2_prep_lo1(v, k, h)

        def a2_prep_trs(te, k, hs):
            bE = 32 * k
            for h in hs:
                ds, idx, scr = prep_ld_idx[k][h]
                ds.wait(te, idx)
                if bE + h >= 2:
                    te.wait_ge(sPEv.h, bE + h - 1)
                bank = pb[6 + (h % 2)]
                for c in range(4):
                    inst = te.matmul(bank[:, c * 128:(c + 1) * 128],
                                     scr[:, c * 128:(c + 1) * 128],
                                     identf[:], start=(c == 0), stop=(c == 3))
                inst.then_inc(sPP.h, 1)



        def a2_main(k):
            kind, half = A2SPECS[k]
            whbm = {"c": c_hb, "d": d_hb}.get(kind)
            bias_off = {"b": 0, "c": 16, "d": 32}[kind]
            mp0 = 64 * k
            me0 = 64 * k
            st_per = 2 if kind == "b" else 1
            st0 = len(dsST.chk)
            bUi = len(dsU.chk)
            bCDi = len(dsCD.chk)
            wTh = wThs[k % 2]
            wTl = wTls[k % 2]

            @block.sync
            def _(sync):
                if k == 0:
                    wf(sync, fvAR1)
                for t in range(NTT):
                    if t >= 2:
                        sync.wait_ge(sMP.h, mp0 + 8 * (t - 1))
                    elif mp0 - 8 * (1 - t) > 0:
                        sync.wait_ge(sMP.h, mp0 - 8 * (1 - t))
                    dsU.dma(sync,
                            uTh[t % 2][:].rearrange("p (kk n) -> p kk n", kk=16),
                            uhi_h.rearrange("(kk p) n -> p kk n", p=128)
                            [:, :, t * TT:(t + 1) * TT])
                    dsU.dma(sync,
                            uTl[t % 2][:].rearrange("p (kk n) -> p kk n", kk=16),
                            ulo_h.rearrange("(kk p) n -> p kk n", p=128)
                            [:, :, t * TT:(t + 1) * TT])
                    if kind == "b":
                        cdu0 = 128 if half == 1 else 0
                        for j in range(8):
                            idx = 8 * t + j
                            if idx >= 2:
                                sync.wait_ge(sCDu.h, cdu0 + 2 * (idx - 1))
                            elif cdu0 > 0:
                                sync.wait_ge(sCDu.h, cdu0)
                            dsCD.dma(sync, cbuf[idx % 2][:],
                                     c_hb[half * 1024 + j * 128:
                                          half * 1024 + (j + 1) * 128,
                                          t * TT:(t + 1) * TT])
                            dsCD.dma(sync, dbuf[idx % 2][:],
                                     d_hb[half * 1024 + j * 128:
                                          half * 1024 + (j + 1) * 128,
                                          t * TT:(t + 1) * TT])

            @block.tensor
            def _(te):
                te.wait_ge(sPEv.h, 32 * (k + 1))
                for t in range(NTT):
                    dsU.wait_upto(te, bUi + 2 * t + 1)
                    for j in range(8):
                        idx = 8 * t + j
                        if idx >= 3:
                            te.wait_ge(sME.h, me0 + idx - 2)
                        elif me0 > 0:
                            te.wait_ge(sME.h, me0)
                        ps = pb[idx % 3]
                        for kk in range(16):
                            col = (j * 16 + kk) * 128
                            te.matmul(ps[:], wTh[:, col:col + 128],
                                      uTh[t % 2][:, kk * TT:(kk + 1) * TT],
                                      start=(kk == 0), stop=False)
                            te.matmul(ps[:], wTh[:, col:col + 128],
                                      uTl[t % 2][:, kk * TT:(kk + 1) * TT],
                                      start=False, stop=False)
                            inst = te.matmul(ps[:], wTl[:, col:col + 128],
                                             uTh[t % 2][:, kk * TT:(kk + 1) * TT],
                                             start=False, stop=(kk == 15))
                        inst.then_inc(sMP.h, 1)
                        if k + 1 < 6 and idx % 2 == 1:
                            a2_prep_trs(te, k + 1, [idx // 2])

            @block.gpsimd
            def _(gp):
                if k == 0:
                    wf(gp, fvAR1)
                for t in range(NTT):
                    for j in range(8):
                        idx = 8 * t + j
                        gp.wait_ge(sME.h, me0 + idx + 1)
                        row = half * 1024 + j * 128
                        if kind != "b":
                            dsST.dma(gp, whbm[row:row + 128, t * TT:(t + 1) * TT],
                                     obuf[idx % 2][:])
                        else:
                            dsST.dma(gp, yph_h[row:row + 128, t * TT:(t + 1) * TT],
                                     ypbh[0][:])
                            dsST.dma(gp, ypl_h[row:row + 128, t * TT:(t + 1) * TT],
                                     ypbl[0][:])

            @block.vector
            def _(v):
                if k == 0:
                    wf(v, fvAR1)
                for t in range(NTT):
                    for j in range(8):
                        idx = 8 * t + j
                        v.wait_ge(sMP.h, mp0 + idx + 1)
                        if kind != "b" and idx >= 1:
                            dsST.wait_upto(v, st0 + st_per * idx - 1)
                        elif st0 > 0 and idx < 1:
                            dsST.wait_upto(v, st0 - 1)
                        ps = pb[idx % 3]
                        bcol = bias_off + half * 8 + j
                        inst = v.tensor_scalar(obuf[idx % 2][:], ps[:],
                                               bias_bcd[:, bcol:bcol + 1], None,
                                               ALU.add)
                        if kind != "b":
                            inst.then_inc(sME.h, 1)
                        v.drain()
                        if kind != "b" and k + 1 < 6 and idx % 2 == 1:
                            a2_prep_los(v, k + 1, [idx // 2])
                        if kind == "b":
                            dsCD.wait_upto(v, bCDi + 2 * idx + 1)
                            hsv = pb[3]
                            init = (0.0 if (t % 4 == 0)
                                    else state[:, half * 8 + j: half * 8 + j + 1])
                            v.tensor_tensor_scan(
                                hsv[:],
                                a_sb[:, half * 8 + j: half * 8 + j + 1]
                                .broadcast_to([128, TT]),
                                obuf[idx % 2][:], init, ALU.mult, ALU.add)
                            v.drain()
                            v.tensor_copy(state[:, half * 8 + j: half * 8 + j + 1],
                                          hsv[:, TT - 1: TT])
                            v.drain()
                            v.tensor_tensor(hsv[:], hsv[:], cbuf[idx % 2][:],
                                            ALU.mult).then_inc(sCDu.h, 1)
                            v.drain()
                            v.tensor_tensor(hsv[:], hsv[:], dbuf[idx % 2][:],
                                            ALU.add).then_inc(sCDu.h, 1)
                            v.drain()
                            if idx >= 1:
                                dsST.wait_upto(v, st0 + 2 * idx - 1)
                            v.tensor_copy(ypbh[0][:], hsv[:])
                            v.drain()
                            v.tensor_sub(ypbl[0][:], hsv[:],
                                         ypbh[0][:]).then_inc(sME.h, 1)
                            v.drain()
                            if k + 1 < 6 and idx % 2 == 1:
                                a2_prep_los(v, k + 1, [idx // 2])

        @block.gpsimd
        def _(gp):
            wf(gp, fvAR1)
            for h in range(1, 32, 2):
                a2_prep_load1(gp, 0, h, dsPG, cbuf[0], step=2)

        @block.scalar
        def _(sc):
            a2_prep_actwork(sc, 0, first=True)

        @block.vector
        def _(v):
            a2_prep_los(v, 0, range(32), first=True)

        @block.tensor
        def _(te):
            wf(te, fvAR1)
            a2_prep_trs(te, 0, range(32))

        for k in range(6):
            if k + 1 < 6:
                @block.scalar
                def _(sc, k=k):
                    a2_prep_actwork(sc, k + 1)
            a2_main(k)

        fv = fence_here((sMP, 64 * 6), (sME, 64 * 6), (sPEv, 32 * 6),
                        *dsST.finals())
        a2ctx.close()

        a3ctx = gctx.enter_context(ExitStack())
        obrep = mkbuf(a3ctx, "obrep", [128, D])
        idb3 = mkbuf(a3ctx, "idb3", [128, 128], BF16)
        owTh = mkbuf(a3ctx, "owTh", [128, 16 * 8 * 128], BF16)
        owTl = mkbuf(a3ctx, "owTl", [128, 16 * 8 * 128], BF16)
        a3prctx = a3ctx.enter_context(ExitStack())
        pr_ow = prep_weight(a3prctx, "pow", W["out_w"], 8, I2, [owTh, owTl], fv,
                            layout="k", ident=idb3,
                            extra_dma=[(obrep[:],
                                        W["out_b"][None, :].broadcast_to([128, D])),
                                       (idb3[:], identb_h[:])])
        fvA3 = fence_here((pr_ow.sE, pr_ow.n))
        a3prctx.close()

        ypTh = [mkbuf(a3ctx, f"a3yh{i}", [128, 16 * 128], BF16) for i in range(2)]
        ypTl = [mkbuf(a3ctx, f"a3yl{i}", [128, 16 * 128], BF16) for i in range(2)]
        xch = [mkbuf(a3ctx, f"a3x{i}", [128, D]) for i in range(2)]
        xnch = [mkbuf(a3ctx, f"a3n{i}", [128, D]) for i in range(2)]
        preb = mkbuf(a3ctx, "a3pre", [128, D])
        accb = [mkbuf(a3ctx, f"a3a{i}", [128, D]) for i in range(2)]

        dsY = DStream("a3y")
        dsXA = DStream("a3xa")
        sP3 = mksem("a3p")
        sE3 = mksem("a3e")
        dsS3 = DStream("a3s")

        @block.sync
        def _(sync):
            wf(sync, fvA3)
            for m in range(32):
                if m >= 2:
                    sync.wait_ge(sP3.h, m - 1)
                dsY.dma(sync,
                        ypTh[m % 2][:].rearrange("p (k n) -> p k n", k=16),
                        yph_h.rearrange("(k p) n -> p k n", p=128)
                        [:, :, m * 128:(m + 1) * 128])
                dsY.dma(sync,
                        ypTl[m % 2][:].rearrange("p (k n) -> p k n", k=16),
                        ypl_h.rearrange("(k p) n -> p k n", p=128)
                        [:, :, m * 128:(m + 1) * 128])

        @block.scalar
        def _(sc):
            wf(sc, fvA3)
            for m in range(32):
                if m >= 2:
                    sc.wait_ge(sE3.h, m - 1)
                dsXA.dma(sc, xch[m % 2][:], x_h[m * 128:(m + 1) * 128, :])
                dsXA.dma(sc, xnch[m % 2][:], xn_h[m * 128:(m + 1) * 128, :])

        @block.tensor
        def _(te):
            wf(te, fvA3)
            for m in range(32):
                dsY.wait_upto(te, 2 * m + 1)
                if m >= 2:
                    te.wait_ge(sE3.h, m - 1)
                psA = pb[2 * (m % 2)]
                psB = pb[2 * (m % 2) + 1]
                for k in range(16):
                    yh = ypTh[m % 2][:, k * 128:(k + 1) * 128]
                    yl = ypTl[m % 2][:, k * 128:(k + 1) * 128]
                    oh = owTh[:, k * 1024:(k + 1) * 1024]
                    ol = owTl[:, k * 1024:(k + 1) * 1024]
                    te.matmul(psA[:], yh, oh[:, 0:512], start=(k == 0), stop=False)
                    te.matmul(psB[:], yh, oh[:, 512:1024], start=(k == 0), stop=False)
                    te.matmul(psA[:], yh, ol[:, 0:512], start=False, stop=False)
                    te.matmul(psB[:], yh, ol[:, 512:1024], start=False, stop=False)
                    te.matmul(psA[:], yl, oh[:, 0:512], start=False, stop=(k == 15))
                    inst = te.matmul(psB[:], yl, oh[:, 512:1024], start=False,
                                     stop=(k == 15))
                inst.then_inc(sP3.h, 1)
            sP3.v = 32

        @block.gpsimd
        def _(gp):
            wf(gp, fvA3)
            for m in range(32):
                gp.wait_ge(sE3.h, m + 1)
                dsS3.dma(gp, acc_h[m * 128:(m + 1) * 128, :], accb[m % 2][:])

        @block.vector
        def _(v):
            wf(v, fvA3)
            for m in range(32):
                v.wait_ge(sP3.h, m + 1)
                dsXA.wait_upto(v, 2 * m + 1)
                if m >= 2:
                    dsS3.wait(v, m - 2)
                psA = pb[2 * (m % 2)]
                psB = pb[2 * (m % 2) + 1]
                v.tensor_add(preb[:], xch[m % 2][:], xnch[m % 2][:])
                v.drain()
                v.tensor_add(preb[:], preb[:], obrep[:])
                v.drain()
                v.tensor_add(accb[m % 2][:, 0:512], psA[:], preb[:, 0:512])
                v.drain()
                v.tensor_add(accb[m % 2][:, 512:1024], psB[:],
                             preb[:, 512:1024]).then_inc(sE3.h, 1)
                v.drain()
            sE3.v = 32

        fv_endA3 = fence_here(*dsS3.finals(), (s_ard, 112), (s_arc, 3))
        a3ctx.close()

        fctx = gctx.enter_context(ExitStack())
        idbf = mkbuf(fctx, "idbf", [128, 128], BF16)
        qdT = mkbuf(fctx, "qdT", [128, 22 * 8 * 128], BF16)
        f1ctx = fctx.enter_context(ExitStack())
        qwgT = mkbuf(f1ctx, "qwgT", [128, 8 * 22 * 128], BF16)
        qwuT = mkbuf(f1ctx, "qwuT", [128, 8 * 22 * 128], BF16)
        fprctx = f1ctx.enter_context(ExitStack())
        pr_qg = prep_weight(fprctx, "pqg", W["w_gate"], 22, D, [qwgT], fv_endA3,
                            mode="quant", layout="k", sw_dst=swg,
                            sw_mul_ap=sxrep[:, 0:1], ident=idbf,
                            extra_dma=[(idbf[:], identb_h[:])])
        pr_qu = prep_weight(fprctx, "pqu", W["w_up"], 22, D, [qwuT], fv_endA3,
                            mode="quant", layout="k", sw_dst=swu,
                            sw_mul_ap=sxrep[:, 0:1], ident=idbf)
        fvF1 = fence_here((pr_qg.sE, pr_qg.n), (pr_qu.sE, pr_qu.n))
        fprctx.close()

        xnf = [mkbuf(f1ctx, f"f1x{i}", [128, D]) for i in range(2)]
        q16b = mkbuf(f1ctx, "f1q16", [128, D], F16)
        qxb = [mkbuf(f1ctx, f"f1qx{i}", [128, D], BF16) for i in range(4)]
        qxT = mkbuf(f1ctx, "f1qxT", [128, 8 * TT], BF16)
        grb = [mkbuf(f1ctx, f"f1gr{i}", [128, TT]) for i in range(2)]
        slb = [mkbuf(f1ctx, f"f1sl{i}", [128, TT]) for i in range(2)]
        hb = [mkbuf(f1ctx, f"f1h{i}", [128, TT]) for i in range(2)]
        fst = mkbuf(f1ctx, "f1st", [128, 1])

        dsXF = DStream("f1x")
        sQ = mksem("f1q")
        sPF = mksem("f1p")
        sEF = mksem("f1e")
        sSL = mksem("f1sl")
        sHE = mksem("f1he")
        dsHS = DStream("f1hs")

        def f1_quant(v, t):
            if t > 0:
                v.wait_ge(sPF.h, 30 * (t - 1) + 8)
            for st in range(4):
                g = 4 * t + st
                dsXF.wait(v, g)
                v.tensor_scalar(q16b[:], xnf[g % 2][:], invsx[:, 0:1], MAGIC,
                                ALU.mult, ALU.add)
                v.drain()
                v.tensor_scalar(q16b[:], q16b[:], MAGIC - 128.0, MAGIC + 127.0,
                                ALU.max, ALU.min)
                v.drain()
                v.tensor_scalar(qxb[st][:], q16b[:], -MAGIC, None,
                                ALU.add).then_inc(sQ.h, 1)
                v.drain()

        def f1_tr(te, t):
            te.wait_ge(sQ.h, 4 * (t + 1))
            for dk in range(8):
                b = 8 * t + dk
                if b >= 2:
                    te.wait_ge(sEF.h, b - 1)
                bank = pb[6 + (b % 2)]
                for st in range(4):
                    inst = te.matmul(bank[:, st * 128:(st + 1) * 128],
                                     qxb[st][:, dk * 128:(dk + 1) * 128],
                                     idbf[:], start=(st == 0), stop=(st == 3))
                inst.then_inc(sPF.h, 1)

        def f1_trev(sc, t):
            for dk in range(8):
                b = 8 * t + dk
                sc.wait_ge(sPF.h, 30 * t + dk + 1)
                bank = pb[6 + (b % 2)]
                sc.copy(qxT[:, dk * 512:(dk + 1) * 512], bank[:]).then_inc(sEF.h, 1)
                sc.drain()

        def f1_mm(te, t):
            te.wait_ge(sEF.h, 8 * (t + 1))
            for i in range(22):
                gidx = 22 * t + i
                if gidx >= 2:
                    te.wait_ge(sHE.h, 2 * gidx - 2)
                psG = pb[2 * (i % 2)]
                psU = pb[2 * (i % 2) + 1]
                for wT, ps in ((qwgT, psG), (qwuT, psU)):
                    for k in range(8):
                        col = (k * 22 + i) * 128
                        inst = te.matmul(ps[:], wT[:, col:col + 128],
                                         qxT[:, k * 512:(k + 1) * 512],
                                         start=(k == 0), stop=(k == 7))
                inst.then_inc(sPF.h, 1)

        def f1_sil(sc, t):
            for i in range(22):
                gidx = 22 * t + i
                sc.wait_ge(sHE.h, 2 * gidx + 1)
                if gidx >= 2:
                    sc.wait_ge(sHE.h, 2 * gidx - 2)
                sc.activation(slb[i % 2][:], grb[i % 2][:], AF.Silu).then_inc(sSL.h, 1)
                sc.drain()

        def f1_hep(v, t):
            for i in range(22):
                gidx = 22 * t + i
                v.wait_ge(sPF.h, 30 * t + 8 + i + 1)
                if gidx >= 2:
                    v.wait_ge(sSL.h, gidx - 1)
                    dsHS.wait(v, gidx - 2)
                psG = pb[2 * (i % 2)]
                psU = pb[2 * (i % 2) + 1]
                v.tensor_scalar(grb[i % 2][:], psG[:], swg[:, i:i + 1], None,
                                ALU.mult).then_inc(sHE.h, 1)
                v.drain()
                v.wait_ge(sSL.h, gidx + 1)
                v.scalar_tensor_tensor(hb[i % 2][:], psU[:], swu[:, i:i + 1],
                                       slb[i % 2][:], ALU.mult, ALU.mult)
                v.drain()
                v.reduce_max(fst[:, 0:1], hb[i % 2][:], axis=AX.X,
                             apply_absolute_value=True)
                v.drain()
                v.tensor_tensor(mx_h[:], mx_h[:], fst[:, 0:1],
                                ALU.max).then_inc(sHE.h, 1)
                v.drain()

        @block.sync
        def _(sync):
            wf(sync, fvF1)
            for g in range(NST):
                if g >= 2:
                    sync.wait_ge(sQ.h, g - 1)
                dsXF.dma(sync, xnf[g % 2][:], xn_h[g * 128:(g + 1) * 128, :])

        @block.gpsimd
        def _(gp):
            wf(gp, fvF1)
            for t in range(NTT):
                for i in range(22):
                    gidx = 22 * t + i
                    gp.wait_ge(sHE.h, 2 * gidx + 2)
                    dsHS.dma(gp, hp_h[i * 128:(i + 1) * 128, t * TT:(t + 1) * TT],
                             hb[i % 2][:])

        @block.vector
        def _(v):
            wf(v, fvF1)
            f1_quant(v, 0)

        for t in range(NTT):
            @block.tensor
            def _(te, t=t):
                if t == 0:
                    wf(te, fvF1)
                f1_tr(te, t)
                f1_mm(te, t)

            @block.scalar
            def _(sc, t=t):
                if t == 0:
                    wf(sc, fvF1)
                    f1_trev(sc, 0)
                f1_sil(sc, t)
                if t + 1 < NTT:
                    f1_trev(sc, t + 1)

            @block.vector
            def _(v, t=t):
                if t + 1 < NTT:
                    f1_quant(v, t + 1)
                f1_hep(v, t)

        sQ.v = 4 * NTT
        sPF.v = 30 * NTT
        sEF.v = 8 * NTT
        sSL.v = 22 * NTT
        sHE.v = 2 * 22 * NTT

        fv_dp = fence_here(*dsHS.finals(), (sHE, sHE.v))
        f1ctx.close()

        swdrep = mkbuf(fctx, "swdrep", [128, D])
        dpctx = fctx.enter_context(ExitStack())
        pr_qd = prep_weight(dpctx, "pqd", W["w_down"], 8, IF, [qdT], fv_dp,
                            mode="quant", layout="k", sw_dst=swd8, ident=idbf)

        s_a2d = mksem("ar2d")
        s_a2c = mksem("ar2c")
        s_cc2 = mksem("cc2")

        @block.gpsimd
        def _(gp):
            gp.wait_ge(fence.h, fv_dp)
            gp.dma_start(out=scr_h[256:384][:, None], in_=mx_h[:],
                         allow_slow_non_contiguous=True).then_inc(s_a2d.h, 16)
            gp.wait_ge(s_a2d.h, 16)
            gp.dma_start(out=row128[:], in_=scr_h[256:384][None, :]).then_inc(s_a2d.h, 16)

        @block.vector
        def _(v):
            v.wait_ge(s_a2d.h, 32)
            v.reduce_max(sc4[0:1, 0:1], row128[:], axis=AX.X)
            v.drain()
            v.tensor_copy(row8[:], sc4[0:1, 0:1].broadcast_to([1, 8])).then_inc(s_a2c.h, 1)
            v.drain()

        @block.gpsimd
        def _(gp):
            gp.wait_ge(s_a2c.h, 1)
            gp.dma_start(out=cc_i[1][:], in_=row8[0:1, :]).then_inc(s_a2d.h, 16)
            gp.wait_ge(s_a2d.h, 48)
            gp.collective_compute("AllReduce", ALU.max,
                                  replica_groups=[list(range(NCORES))],
                                  ins=[cc_i[1][:]], outs=[cc_o[1][:]]).then_inc(s_cc2.h, 1)
            gp.wait_ge(s_cc2.h, 1)
            gp.dma_start(out=scal[0:1, 8:9],
                         in_=cc_o[1][0:1][None, :]).then_inc(s_a2d.h, 16)

        @block.scalar
        def _(sc):
            sc.wait_ge(s_a2d.h, 64)
            sc.activation(scal[0:1, 9:10], scal[0:1, 1:2],
                          AF.Sigmoid).then_inc(s_a2c.h, 1)
            sc.drain()

        @block.vector
        def _(v):
            v.wait_ge(s_a2c.h, 2)
            v.tensor_tensor(scal[0:1, 10:11], scal[0:1, 9:10], scal[0:1, 8:9], ALU.mult)
            v.drain()
            v.tensor_scalar(scal[0:1, 10:11], scal[0:1, 10:11], QI, None, ALU.mult)
            v.drain()
            v.reciprocal(scal[0:1, 11:12], scal[0:1, 10:11]).then_inc(s_a2c.h, 1)
            v.drain()

        @block.gpsimd
        def _(gp):
            gp.wait_ge(s_a2c.h, 3)
            gp.dma_start(out=scr_h[210:212][None, :],
                         in_=scal[0:1, 10:12]).then_inc(s_a2d.h, 16)
            gp.wait_ge(s_a2d.h, 80)
            gp.dma_start(out=sdrep[:], in_=scr_h[210:211][None, :]
                         .broadcast_to([128, 1])).then_inc(s_a2d.h, 16)
            gp.dma_start(out=invsd[:], in_=scr_h[211:212][None, :]
                         .broadcast_to([128, 1])).then_inc(s_a2d.h, 16)
            gp.wait_ge(s_a2d.h, 112)
            gp.wait_ge(pr_qd.sE.h, pr_qd.n)
            gp.dma_start(out=scr_h[1024:2048].rearrange("(c p) -> p c", p=128),
                         in_=swd8[:],
                         allow_slow_non_contiguous=True).then_inc(s_a2d.h, 16)
            gp.wait_ge(s_a2d.h, 128)
            gp.dma_start(out=swdrep[:], in_=scr_h[1024:2048][None, :]
                         .broadcast_to([128, D])).then_inc(s_a2d.h, 16)

        @block.vector
        def _(v):
            v.wait_ge(s_a2d.h, 144)
            v.tensor_scalar(swdrep[:], swdrep[:], sdrep[:, 0:1], None,
                            ALU.mult).then_inc(s_a2c.h, 1)
            v.drain()

        fvF2 = fence_here((s_a2d, 144), (s_a2c, 4))
        dpctx.close()

        hsl = [mkbuf(fctx, f"f2h{i}", [128, IF]) for i in range(2)]
        hq16 = mkbuf(fctx, "f2q16", [128, IF], F16)
        qhb = [mkbuf(fctx, f"f2qh{i}", [128, IF], BF16) for i in range(2)]
        accf = [mkbuf(fctx, f"f2a{i}", [128, D]) for i in range(2)]
        outb = [mkbuf(fctx, f"f2o{i}", [128, D]) for i in range(2)]

        dsHF = DStream("f2h")
        dsAF = DStream("f2a")
        sQF = mksem("f2q")
        sPD = mksem("f2p")
        sED = mksem("f2e")
        dsOUT = DStream("f2out")

        @block.sync
        def _(sync):
            wf(sync, fv_dp)
            sync.wait_ge(pr_qd.sE.h, pr_qd.n)
            for m in range(32):
                if m >= 2:
                    sync.wait_ge(sQF.h, m - 1)
                dsHF.dma(sync,
                         hsl[m % 2][:].rearrange("p (k n) -> p k n", k=22),
                         hp_h.rearrange("(k p) n -> p k n", p=128)
                         [:, :, m * 128:(m + 1) * 128])

        @block.scalar
        def _(sc):
            wf(sc, fv_dp)
            sc.wait_ge(pr_qd.sE.h, pr_qd.n)
            for m in range(32):
                if m >= 2:
                    sc.wait_ge(sED.h, m - 1)
                dsAF.dma(sc, accf[m % 2][:], acc_h[m * 128:(m + 1) * 128, :])

        def f2_ep(v, mm):
            v.wait_ge(sPD.h, mm + 1)
            dsAF.wait(v, mm)
            if mm >= 2:
                dsOUT.wait(v, mm - 2)
            psA = pb[2 * (mm % 2)]
            psB = pb[2 * (mm % 2) + 1]
            v.tensor_tensor(outb[mm % 2][:, 0:512], psA[:], swdrep[:, 0:512], ALU.mult)
            v.drain()
            v.tensor_tensor(outb[mm % 2][:, 512:1024], psB[:], swdrep[:, 512:1024],
                            ALU.mult)
            v.drain()
            v.tensor_add(outb[mm % 2][:], outb[mm % 2][:],
                         accf[mm % 2][:]).then_inc(sED.h, 1)
            v.drain()

        @block.gpsimd
        def _(gp):
            wf(gp, fvF2)
            for m in range(32):
                gp.wait_ge(sED.h, m + 1)
                dsOUT.dma(gp, out_h[m * 128:(m + 1) * 128, :], outb[m % 2][:])

        @block.vector
        def _(v):
            wf(v, fvF2)
            for m in range(32):
                dsHF.wait(v, m)
                if m >= 2:
                    v.wait_ge(sPD.h, m - 1)
                v.tensor_scalar(hq16[:], hsl[m % 2][:], invsd[:, 0:1], MAGIC,
                                ALU.mult, ALU.add)
                v.drain()
                v.tensor_scalar(hq16[:], hq16[:], MAGIC - 128.0, MAGIC + 127.0,
                                ALU.max, ALU.min)
                v.drain()
                v.tensor_scalar(qhb[m % 2][:], hq16[:], -MAGIC, None,
                                ALU.add).then_inc(sQF.h, 1)
                v.drain()
                if m >= 1:
                    f2_ep(v, m - 1)
            f2_ep(v, 31)
            sQF.v = 32
            sED.v = 32

        @block.tensor
        def _(te):
            wf(te, fvF2)
            for m in range(32):
                te.wait_ge(sQF.h, m + 1)
                if m >= 2:
                    te.wait_ge(sED.h, m - 1)
                psA = pb[2 * (m % 2)]
                psB = pb[2 * (m % 2) + 1]
                for k in range(22):
                    qh = qhb[m % 2][:, k * 128:(k + 1) * 128]
                    te.matmul(psA[:], qh, qdT[:, (k * 8) * 128:(k * 8) * 128 + 512],
                              start=(k == 0), stop=(k == 21))
                    inst = te.matmul(psB[:], qh,
                                     qdT[:, (k * 8 + 4) * 128:(k * 8 + 4) * 128 + 512],
                                     start=(k == 0), stop=(k == 21))
                inst.then_inc(sPD.h, 1)
            sPD.v = 32

        @block.sync
        def _(sync):
            for sm, v in dsOUT.finals():
                sync.wait_ge(sm.h, v)

        fctx.close()
    return nc


_NC_CACHE = {}


def kernel(L0_attn_x, L0_attn_norm_w, L0_attn_in_w, L0_attn_in_b, L0_attn_gate_w,
           L0_attn_gate_b, L0_attn_b_w, L0_attn_b_b, L0_attn_c_w, L0_attn_c_b,
           L0_attn_d_w, L0_attn_d_b, L0_attn_out_w, L0_attn_out_b, L0_attn_a_log,
           L0_ffn_clip_x, L0_ffn_clip_down, L0_ffn_w_gate, L0_ffn_w_up,
           L0_ffn_w_down):
    if "nc" not in _NC_CACHE:
        _NC_CACHE["nc"] = build_nc()
    nc = _NC_CACHE["nc"]
    x = np.ascontiguousarray(np.asarray(L0_attn_x, dtype=np.float32))
    common = {
        "identb": np.eye(128, dtype=ml_dtypes.bfloat16),
        "identf": np.eye(128, dtype=np.float32),
        "norm_w": np.ascontiguousarray(L0_attn_norm_w, dtype=np.float32),
        "in_w": np.ascontiguousarray(L0_attn_in_w, dtype=np.float32),
        "in_b": np.ascontiguousarray(L0_attn_in_b, dtype=np.float32),
        "gate_w": np.ascontiguousarray(L0_attn_gate_w, dtype=np.float32),
        "gate_b": np.ascontiguousarray(L0_attn_gate_b, dtype=np.float32),
        "b_w": np.ascontiguousarray(L0_attn_b_w, dtype=np.float32),
        "b_b": np.ascontiguousarray(L0_attn_b_b, dtype=np.float32),
        "c_w": np.ascontiguousarray(L0_attn_c_w, dtype=np.float32),
        "c_b": np.ascontiguousarray(L0_attn_c_b, dtype=np.float32),
        "d_w": np.ascontiguousarray(L0_attn_d_w, dtype=np.float32),
        "d_b": np.ascontiguousarray(L0_attn_d_b, dtype=np.float32),
        "out_w": np.ascontiguousarray(L0_attn_out_w, dtype=np.float32),
        "out_b": np.ascontiguousarray(L0_attn_out_b, dtype=np.float32),
        "a_log": np.ascontiguousarray(L0_attn_a_log, dtype=np.float32),
        "clip_x": np.ascontiguousarray(L0_ffn_clip_x, dtype=np.float32),
        "clip_down": np.ascontiguousarray(L0_ffn_clip_down, dtype=np.float32),
        "w_gate": np.ascontiguousarray(L0_ffn_w_gate, dtype=np.float32),
        "w_up": np.ascontiguousarray(L0_ffn_w_up, dtype=np.float32),
        "w_down": np.ascontiguousarray(L0_ffn_w_down, dtype=np.float32),
    }
    in_maps = []
    for c in range(NCORES):
        m = dict(common)
        m["x"] = np.ascontiguousarray(x[2 * c:2 * c + 2].reshape(T, D))
        in_maps.append(m)
    res = run_bass_kernel_spmd(nc, in_maps, core_ids=list(range(NCORES)))
    out = np.concatenate(
        [res.results[c]["out"].reshape(2, 2048, D) for c in range(NCORES)], axis=0)
    return np.ascontiguousarray(out.astype(np.float32))
